# version 02
# baseline (speedup 1.0000x reference)
import contextlib
import numpy as np
import ml_dtypes
import concourse.bass as bass
import concourse.mybir as mybir
from concourse.bass_utils import run_bass_kernel_spmd

F32 = mybir.dt.float32
BF16 = mybir.dt.bfloat16
U8 = mybir.dt.uint8
AF = mybir.ActivationFunctionType
ALU = mybir.AluOpType

N_CORES = 8
TOK_PER_CORE = 2048
NT = 512
N_TILES = TOK_PER_CORE // NT
H = 4096
KC = H // 128
S = 1024
SC = S // 128
F = 9984
FC = F // 128
NBAND = F // 256
OC = KC
G = 6
EPI_ROWS = S * 7 + F
EPI_KT = EPI_ROWS // 128
THD = 2.5
EPS = 1e-5


class Ctr:
    __slots__ = ("n",)
    def __init__(self):
        self.n = 0
    def add(self, k=1):
        self.n += k
        return self.n


def build_program():
    nc = bass.Bass("TRN2", target_bir_lowering=False, debug=False)

    d_x = nc.dram_tensor("x_t", [H, TOK_PER_CORE], F32, kind="ExternalInput").ap()
    d_up = nc.dram_tensor("up_t", [H, S], F32, kind="ExternalInput").ap()
    d_gate = nc.dram_tensor("gate_t", [H, S], F32, kind="ExternalInput").ap()
    d_ffngu = nc.dram_tensor("ffngu", [H, 2 * F], F32, kind="ExternalInput").ap()
    d_epiw = nc.dram_tensor("epiw", [EPI_ROWS, H], F32, kind="ExternalInput").ap()
    d_wbt = nc.dram_tensor("wbt", [H, H], F32, kind="ExternalInput").ap()
    d_qt = nc.dram_tensor("qt", [H, H], BF16, kind="ExternalInput").ap()
    d_enc = nc.dram_tensor("enc", [40, H], F32, kind="ExternalInput").ap()
    d_co0 = nc.dram_tensor("co0", [128, 3 * 32 * 32], F32, kind="ExternalInput").ap()
    d_co1 = nc.dram_tensor("co1", [128, 3 * 32 * 8], F32, kind="ExternalInput").ap()
    d_mk0 = nc.dram_tensor("mk0", [128, 3 * 32 * 32], U8, kind="ExternalInput").ap()
    d_mk1 = nc.dram_tensor("mk1", [128, 3 * 32 * 8], U8, kind="ExternalInput").ap()
    d_lp = nc.dram_tensor("lp", [128, SC], F32, kind="ExternalInput").ap()
    d_bb = nc.dram_tensor("bb", [128, OC], F32, kind="ExternalInput").ap()
    d_disc = nc.dram_tensor("disc", [1, G], F32, kind="ExternalInput").ap()
    d_out = nc.dram_tensor("out_t", [H, TOK_PER_CORE], F32, kind="ExternalOutput").ap()
    d_weff = nc.dram_tensor("weff_scr", [H, H], BF16, kind="Internal").ap()

    v_x = d_x.rearrange("(k p) t -> p k t", p=128)
    v_up = d_up.rearrange("(k p) s -> p k s", p=128)
    v_gate = d_gate.rearrange("(k p) s -> p k s", p=128)
    v_ffngu = d_ffngu.rearrange("(k p) s -> p k s", p=128)
    v_epiw = d_epiw.rearrange("(k p) h -> p k h", p=128)
    v_qt = d_qt.rearrange("(k p) t -> p k t", p=128)
    v_weff = d_weff.rearrange("(k p) t -> p k t", p=128)

    prog = {e: [] for e in ("sync", "gpsimd", "vector", "scalar", "tensor")}

    def E(eng, fn):
        prog[eng].append(fn)

    with contextlib.ExitStack() as es:
        ec = es.enter_context

        sem_pe = ec(nc.semaphore("sem_pe"))
        sem_rc = ec(nc.semaphore("sem_rc"))
        sem_a = ec(nc.semaphore("sem_a"))
        sem_v = ec(nc.semaphore("sem_v"))
        sem_bar = ec(nc.semaphore("sem_bar"))
        sem_out = ec(nc.semaphore("sem_out"))
        sem_r = [ec(nc.semaphore(f"sem_r{i}")) for i in range(3)]
        sem_xs = [ec(nc.semaphore(f"sem_xs{i}")) for i in range(2)]
        sem_wb = [ec(nc.semaphore(f"sem_wb{i}")) for i in range(2)]
        sem_ws = [ec(nc.semaphore(f"sem_ws{i}")) for i in range(2)]
        sem_xr = [ec(nc.semaphore(f"sem_xr{i}")) for i in range(2)]
        sem_web = ec(nc.semaphore("sem_web"))
        sem_ld = ec(nc.semaphore("sem_ld"))

        COUT, CPE, CRC, CA, CV = (Ctr() for _ in range(5))
        CR = [Ctr() for _ in range(3)]
        CXS = [Ctr() for _ in range(2)]
        CWB = [Ctr() for _ in range(2)]
        CWS = [Ctr() for _ in range(2)]
        CXR = [Ctr() for _ in range(2)]
        CWEB = Ctr()
        CLD = Ctr()

        def W(eng, sem, n):
            if n > 0:
                E(eng, lambda e, s=sem, n=n: e.wait_ge(s, n))

        def W2(eng, sw):
            sem, n = sw
            W(eng, sem, n)

        def OPV(fn):
            if CV.n:
                E("vector", lambda e, n=CV.n: e.wait_ge(sem_v, n))
            E("vector", lambda e, f=fn: f(e).then_inc(sem_v, 1))
            return CV.add(1)

        def OPA(fn):
            if CA.n:
                E("scalar", lambda e, n=CA.n: e.wait_ge(sem_a, n))
            E("scalar", lambda e, f=fn: f(e).then_inc(sem_a, 1))
            return CA.add(1)

        RING = ec(nc.sbuf_tensor("RING", [128, 3 * 2048], BF16))
        LP = ec(nc.sbuf_tensor("LP", [128, SC], F32))
        BB = ec(nc.sbuf_tensor("BB", [128, OC], F32))
        EMC = ec(nc.sbuf_tensor("EMC", [128, 8], F32))
        ONESB = ec(nc.sbuf_tensor("ONESB", [128, 1], BF16))
        ONES1 = ec(nc.sbuf_tensor("ONES1", [1, 128], F32))
        CST = ec(nc.sbuf_tensor("CST", [1, 8], F32))
        CSTC = ec(nc.sbuf_tensor("CSTC", [128, 2], F32))
        PB = ec(nc.psum_tensor("PB", [128, 8 * NT], F32))

        def bank(j):
            return PB[:, j * NT:(j + 1) * NT]

        ring_idx = Ctr()
        ring_cons_at = {}

        def ring_load(src_ap):
            i = ring_idx.n
            ring_idx.add()
            sl = i % 3
            slot = RING[:, sl * 2048:(sl + 1) * 2048]
            if i >= 3:
                W("gpsimd", sem_rc, ring_cons_at[i - 3])
            ncols = int(np.prod(src_ap.shape[1:]))
            dst = slot[:, 0:ncols]
            if len(src_ap.shape) == 3:
                dst = dst.rearrange("p (a b) -> p a b", a=src_ap.shape[1])
            tgt = CR[sl].add(16)
            E("gpsimd", lambda e, s=src_ap, d=dst, m=sem_r[sl]:
              e.dma_start(out=d, in_=s).then_inc(m, 16))
            return slot, i, (sem_r[sl], tgt)

        def _chain(inst, incs):
            for s_, v_ in incs:
                inst = inst.then_inc(s_, v_)
            return inst

        def TMM(fn, incs=()):
            if incs:
                E("tensor", lambda e, f=fn, iv=tuple(incs): _chain(f(e), iv))
            else:
                E("tensor", fn)

        def ring_mark(i):
            ring_cons_at[i] = CRC.add(1)
            return (sem_rc, 1)

        with contextlib.ExitStack() as ses:
            sc_ = ses.enter_context
            CO0 = sc_(nc.sbuf_tensor("CO0", [128, 3 * 32 * 32], F32))
            CO1 = sc_(nc.sbuf_tensor("CO1", [128, 3 * 32 * 8], F32))
            MK0 = sc_(nc.sbuf_tensor("MK0", [128, 3 * 32 * 32], U8))
            MK1 = sc_(nc.sbuf_tensor("MK1", [128, 3 * 32 * 8], U8))
            MKF = sc_(nc.sbuf_tensor("MKF", [128, 3 * 32 * 32], F32))
            W0S = sc_(nc.sbuf_tensor("W0S", [128, 32 * 32], F32))
            W1S = sc_(nc.sbuf_tensor("W1S", [128, 32 * 8], F32))
            W01B = sc_(nc.sbuf_tensor("W01B", [128, 32 * 40], BF16))
            ENCF = sc_(nc.sbuf_tensor("ENCF", [40, H], F32))
            ENCB = sc_(nc.sbuf_tensor("ENCB", [40, H], BF16))
            VCB = sc_(nc.sbuf_tensor("VCB", [40, H], BF16))
            WBB = sc_(nc.sbuf_tensor("WBB", [128, 2 * H], F32))
            WEO = sc_(nc.sbuf_tensor("WEO", [128, 2 * H], BF16))
            DRW = sc_(nc.sbuf_tensor("DRW", [1, 16], F32))

            for src, dst in ((d_co0, CO0[:, :]), (d_co1, CO1[:, :]),
                             (d_mk0, MK0[:, :]), (d_mk1, MK1[:, :]),
                             (d_enc, ENCF[:, :]), (d_lp, LP[:, :]),
                             (d_bb, BB[:, :]), (d_disc, DRW[:, 0:G])):
                CLD.add(16)
                E("sync", lambda e, s=src, d=dst: e.dma_start(out=d, in_=s).then_inc(sem_ld, 16))
            c_small = CLD.n

            OPV(lambda e: e.memset(ONESB[:, :], 1.0))
            OPV(lambda e: e.memset(ONES1[:, :], 1.0))
            OPV(lambda e: e.memset(CST[:, 0:1], EPS))
            OPV(lambda e: e.memset(CST[:, 1:2], -0.5))
            OPV(lambda e: e.memset(CST[:, 2:3], -1.0))
            OPV(lambda e: e.memset(CST[:, 3:4], 0.5))
            c_consts = OPV(lambda e: e.memset(CSTC[:, 0:1], 0.5))

            W("vector", sem_ld, c_small)
            OPV(lambda e: e.tensor_copy(MKF[:, :], MK0[:, :]))
            for e_i in range(3):
                sl = slice(e_i * 1024, (e_i + 1) * 1024)
                if e_i == 0:
                    OPV(lambda e, s=sl: e.tensor_mul(W0S[:, :], CO0[:, s], MKF[:, s]))
                else:
                    OPV(lambda e, s=sl: e.tensor_mul(MKF[:, s], CO0[:, s], MKF[:, s]))
                    OPV(lambda e, s=sl: e.tensor_add(W0S[:, :], W0S[:, :], MKF[:, s]))
            OPV(lambda e: e.tensor_copy(MKF[:, 0:3 * 256], MK1[:, :]))
            for e_i in range(3):
                sl = slice(e_i * 256, (e_i + 1) * 256)
                if e_i == 0:
                    OPV(lambda e, s=sl: e.tensor_mul(W1S[:, :], CO1[:, s], MKF[:, s]))
                else:
                    OPV(lambda e, s=sl: e.tensor_mul(MKF[:, s], CO1[:, s], MKF[:, s]))
                    OPV(lambda e, s=sl: e.tensor_add(W1S[:, :], W1S[:, :], MKF[:, s]))
            OPV(lambda e: e.tensor_copy(
                W01B.rearrange("p (c w) -> p c w", w=40)[:, :, 0:32],
                W0S.rearrange("p (c r) -> p c r", r=32)))
            OPV(lambda e: e.tensor_copy(
                W01B.rearrange("p (c w) -> p c w", w=40)[:, :, 32:40],
                W1S.rearrange("p (c r) -> p c r", r=8)))
            c_w01 = OPV(lambda e: e.tensor_copy(ENCB[:, :], ENCF[:, :]))

            W("scalar", sem_ld, c_small)
            W("scalar", sem_v, c_consts)
            c_emc_row = OPA(lambda e: e.activation(DRW[:, 8:8 + G], DRW[:, 0:G], AF.Exp,
                                                   bias=0.0, scale=CST[0:1, 2:3]))
            W("tensor", sem_a, c_emc_row)
            E("tensor", lambda e: e.matmul(PB[:, 7 * NT:7 * NT + G], ONES1[:, :], DRW[:, 8:8 + G],
                                           start=True, stop=True).then_inc(sem_pe, 1))
            c_emc_mm = CPE.add(1)
            W("vector", sem_pe, c_emc_mm)
            c_emc = OPV(lambda e: e.tensor_copy(EMC[:, 0:G], PB[:, 7 * NT:7 * NT + G]))

            c_vcb_prev = 0
            for hh in range(8):
                t0 = hh * NT
                for og in range(8):
                    src = v_qt[:, og * 4:(og + 1) * 4, t0:t0 + NT]
                    slot, fi, wcnt = ring_load(src)
                    W2("tensor", wcnt)
                    if hh == 0 and og == 0:
                        W("tensor", sem_v, c_w01)
                    if og == 0 and hh > 0:
                        W("tensor", sem_a, c_vcb_prev)
                    for oc in range(4):
                        c = og * 4 + oc
                        lhsT = W01B[:, c * 40:(c + 1) * 40]
                        rhs = slot[:, oc * NT:(oc + 1) * NT]
                        incs = []
                        if oc == 3:
                            incs.append(ring_mark(fi))
                        TMM(lambda e, l=lhsT, r=rhs, st=(c == 0), sp=(c == 31):
                            e.matmul(PB[0:40, 6 * NT:7 * NT], l, r, start=st, stop=sp), incs)
                c_vc_mm = (sem_rc, CRC.n)
                W2("scalar", c_vc_mm)
                c_vcb_prev = OPA(lambda e, t0=t0: e.copy(VCB[0:40, t0:t0 + NT],
                                                         PB[0:40, 6 * NT:7 * NT]))

            c_wb_load = {}
            c_wb_stt = {}
            c_stt_at = {}
            c_ws_store = {}

            def emit_wb_load(dc):
                par = dc % 2
                buf = WBB[:, par * H:par * H + H]
                if dc >= 2:
                    W("sync", sem_v, c_wb_stt[dc - 2])
                tgt = CWB[par].add(16)
                E("sync", lambda e, d=buf, s=d_wbt[dc * 128:(dc + 1) * 128, :], m=sem_wb[par]:
                  e.dma_start(out=d, in_=s).then_inc(m, 16))
                c_wb_load[dc] = (sem_wb[par], tgt)

            emit_wb_load(0)
            emit_wb_load(1)
            for dc in range(KC):
                par = dc % 2
                buf = WBB[:, par * H:par * H + H]
                obuf = WEO[:, par * H:par * H + H]
                for hh in range(8):
                    idx = dc * 8 + hh
                    pb_i = 4 + (idx % 2)
                    lhsT = ENCB[:, dc * 128:(dc + 1) * 128]
                    rhs = VCB[:, hh * NT:(hh + 1) * NT]
                    if idx >= 2:
                        W("tensor", sem_v, c_stt_at[idx - 2])
                    if idx == 0:
                        W("tensor", sem_a, c_vcb_prev)
                    E("tensor", lambda e, l=lhsT, r=rhs, b=pb_i:
                      e.matmul(bank(b), l, r, start=True, stop=True).then_inc(sem_pe, 1))
                    c_mm = CPE.add(1)
                    W("vector", sem_pe, c_mm)
                    if hh == 0:
                        W2("vector", c_wb_load[dc])
                        if dc >= 2:
                            W2("vector", c_ws_store[dc - 2])
                    c_stt_at[idx] = OPV(lambda e, o=obuf[:, hh * NT:(hh + 1) * NT], b=pb_i,
                                        w=buf[:, hh * NT:(hh + 1) * NT]:
                                        e.scalar_tensor_tensor(o, bank(b), CSTC[:, 0:1], w,
                                                               ALU.mult, ALU.add))
                c_wb_stt[dc] = c_stt_at[dc * 8 + 7]
                W("sync", sem_v, c_wb_stt[dc])
                tgt = CWS[par].add(16)
                E("sync", lambda e, o=d_weff[dc * 128:(dc + 1) * 128, :], s=obuf, m=sem_ws[par]:
                  e.dma_start(out=o, in_=s).then_inc(m, 16))
                c_ws_store[dc] = (sem_ws[par], tgt)
                if dc + 2 < KC:
                    emit_wb_load(dc + 2)

            for eng in ("sync", "gpsimd", "vector", "scalar", "tensor"):
                W(eng, sem_v, CV.n)
                W(eng, sem_a, CA.n)
                W(eng, sem_pe, CPE.n)
                W(eng, sem_rc, CRC.n)
                W(eng, sem_ld, CLD.n)
                W(eng, sem_ws[0], CWS[0].n)
                W(eng, sem_ws[1], CWS[1].n)
                W(eng, sem_wb[0], CWB[0].n)
                W(eng, sem_wb[1], CWB[1].n)
                for i_ in range(3):
                    W(eng, sem_r[i_], CR[i_].n)

        XN = ec(nc.sbuf_tensor("XN", [128, KC * NT], BF16))
        FEAT = ec(nc.sbuf_tensor("FEAT", [128, 56 * NT], BF16))
        WINT = ec(nc.sbuf_tensor("WINT", [128, FC * NT], BF16))
        WEB = ec(nc.sbuf_tensor("WEB", [128, KC * 128], BF16))
        XRES = ec(nc.sbuf_tensor("XRES", [128, 2 * NT], F32))
        OUTB = ec(nc.sbuf_tensor("OUTB", [128, NT], F32))
        SCRA = ec(nc.sbuf_tensor("SCRA", [128, NT], F32))
        SCRB = ec(nc.sbuf_tensor("SCRB", [128, NT], F32))
        SGF = ec(nc.sbuf_tensor("SGF", [128, 2 * NT], F32))
        ROWS = ec(nc.sbuf_tensor("ROWS", [1, 3 * NT], F32))

        DELTA = WINT[:, 0:8 * NT * 2].bitcast(F32)
        PW = WINT[:, 8 * NT * 2:16 * NT * 2].bitcast(F32)
        XSTAT = FEAT[:, 40 * NT:56 * NT].bitcast(F32)
        XBF = FEAT[:, 32 * NT:34 * NT]
        X2BF = FEAT[:, 34 * NT:36 * NT]

        def feat(kc):
            return FEAT[:, kc * NT:(kc + 1) * NT]

        def wint(kc):
            return WINT[:, kc * NT:(kc + 1) * NT]

        def xnc(k):
            return XN[:, k * NT:(k + 1) * NT]

        def delta_j(j):
            return DELTA[:, j * NT:(j + 1) * NT]

        def pw_j(j):
            return PW[:, j * NT:(j + 1) * NT]

        xstat_cons = {}
        c_mmst = {}
        c_epi_sg = {}
        c_epi_g = {}
        c_epi_out = {}
        c_out_dma = {}
        c_web_done = 0
        c_ffn_done_prev = 0
        c_xn_done = 0
        c_bcast = 0

        RA = ROWS[0:1, 0:NT]
        RB = ROWS[0:1, NT:2 * NT]
        RC = ROWS[0:1, 2 * NT:3 * NT]

        for t in range(N_TILES):
            tc = slice(t * NT, (t + 1) * NT)
            c_pe_prev_tile = CPE.n
            c_rc_prev_tile = CRC.n

            c_xload = {}

            def emit_ln_load(key, t=t, tc=tc):
                gk = t * 16 + key
                slot = key % 2
                if gk >= 2 and (gk - 2) in xstat_cons:
                    for sem_o, cnt_o in xstat_cons[gk - 2]:
                        W("sync", sem_o, cnt_o)
                if key == 0 and t > 0:
                    W("sync", sem_pe, c_pe_prev_tile)
                    W("sync", sem_rc, c_rc_prev_tile)
                g4 = key % 8
                src = v_x[:, g4 * 4:(g4 + 1) * 4, tc]
                dst = XSTAT[:, slot * 4 * NT:(slot + 1) * 4 * NT].rearrange(
                    "p (a b) -> p a b", a=4)
                tgt = CXS[slot].add(16)
                E("sync", lambda e, s=src, d=dst, m=sem_xs[slot]:
                  e.dma_start(out=d, in_=s).then_inc(m, 16))
                c_xload[key] = (sem_xs[slot], tgt)

            def emit_stats_group(g4, t=t):
                slot = g4 % 2
                for kk in range(4):
                    k = g4 * 4 + kk
                    xv = XSTAT[:, (slot * 4 + kk) * NT:(slot * 4 + kk + 1) * NT]
                    xb = XBF[:, (k % 2) * NT:(k % 2 + 1) * NT]
                    x2 = X2BF[:, (k % 2) * NT:(k % 2 + 1) * NT]
                    W2("vector", c_xload[g4])
                    W2("scalar", c_xload[g4])
                    if k >= 2 or t > 0:
                        kk2 = (k - 2) % KC
                        W("vector", sem_pe, c_mmst[kk2])
                        W("scalar", sem_pe, c_mmst[kk2])
                    if k == 0 and t > 0:
                        W("vector", sem_pe, c_pe_prev_tile)
                        W("scalar", sem_pe, c_pe_prev_tile)
                        W("vector", sem_rc, c_rc_prev_tile)
                        W("scalar", sem_rc, c_rc_prev_tile)
                    c_cast = OPV(lambda e, a=xb, b=xv: e.tensor_copy(a, b))
                    c_sq = OPA(lambda e, a=x2, b=xv: e.activation(a, b, AF.Square))
                    W("tensor", sem_v, c_cast)
                    W("tensor", sem_a, c_sq)
                    if k == 0 and t > 0:
                        W("tensor", sem_v, c_ffn_done_prev)
                    st = (k == 0)
                    sp = (k == KC - 1)
                    E("tensor", lambda e, a=xb, st=st, sp=sp:
                      e.matmul(PB[0:1, 7 * NT:8 * NT], ONESB[:, :], a, start=st, stop=sp))
                    E("tensor", lambda e, a=x2, st=st, sp=sp:
                      e.matmul(PB[0:1, 4 * NT:5 * NT], ONESB[:, :], a, start=st, stop=sp).then_inc(sem_pe, 1))
                    c_mmst[k] = CPE.add(1)
                xstat_cons[t * 16 + g4] = ((sem_a, CA.n), (sem_v, CV.n))

            def emit_xn_group(g4, t=t):
                slot = g4 % 2
                for kk in range(4):
                    k = g4 * 4 + kk
                    xv = XSTAT[:, (slot * 4 + kk) * NT:(slot * 4 + kk + 1) * NT]
                    W2("vector", c_xload[8 + g4])
                    if k == 0:
                        W("vector", sem_pe, c_bcast)
                    OPV(lambda e, xv=xv: e.tensor_mul(SCRB[:, :], xv, PB[:, 5 * NT:6 * NT]))
                    OPV(lambda e, k=k: e.tensor_sub(xnc(k), SCRB[:, :],
                                                    PB[:, 6 * NT:7 * NT]))
                xstat_cons[t * 16 + 8 + g4] = ((sem_v, CV.n),)

            emit_ln_load(0)
            emit_ln_load(1)
            for g4 in range(8):
                emit_stats_group(g4)
                if g4 + 2 <= 7:
                    emit_ln_load(g4 + 2)
            c_stats_mm = c_mmst[KC - 1]

            W("vector", sem_pe, c_stats_mm)
            OPV(lambda e: e.tensor_scalar_mul(RA, PB[0:1, 7 * NT:8 * NT], 1.0 / H))
            OPV(lambda e: e.tensor_scalar_mul(RB, PB[0:1, 4 * NT:5 * NT], 1.0 / H))
            OPV(lambda e: e.tensor_mul(RC, RA, RA))
            c_var = OPV(lambda e: e.tensor_sub(RB, RB, RC))
            W("scalar", sem_v, c_var)
            c_ln = OPA(lambda e: e.activation(RC, RB, AF.Ln, bias=CST[0:1, 0:1], scale=1.0))
            c_rstd = OPA(lambda e: e.activation(RB, RC, AF.Exp, bias=0.0,
                                                scale=CST[0:1, 1:2]))
            W("vector", sem_a, c_rstd)
            c_mr = OPV(lambda e: e.tensor_mul(RC, RA, RB))

            W("tensor", sem_v, c_mr)
            if t > 0:
                W("tensor", sem_v, c_ffn_done_prev)
            E("tensor", lambda e: e.matmul(PB[:, 5 * NT:6 * NT], ONES1[:, :], RB, start=True, stop=True))
            E("tensor", lambda e: e.matmul(PB[:, 6 * NT:7 * NT], ONES1[:, :], RC,
                                           start=True, stop=True).then_inc(sem_pe, 1))
            c_bcast = CPE.add(1)

            emit_ln_load(8)
            emit_ln_load(9)
            for g4 in range(8):
                emit_xn_group(g4)
                if g4 + 2 <= 7:
                    emit_ln_load(8 + g4 + 2)
            c_xn_done = CV.n

            c_p1_evac_u = {}
            c_p1_delta = {}
            for sweep in range(2):
                vsrc = v_up if sweep == 0 else v_gate
                for g2 in range(16):
                    src = vsrc[:, g2 * 2:(g2 + 1) * 2, :]
                    slot, fi, wcnt = ring_load(src)
                    W2("tensor", wcnt)
                    if g2 == 0:
                        W("tensor", sem_v, c_xn_done)
                        if sweep == 0 and t > 0:
                            W("tensor", sem_v, c_epi_sg[31])
                        if sweep == 1:
                            W("tensor", sem_a, c_p1_evac_u[SC - 1])
                    for kk in range(2):
                        k = g2 * 2 + kk
                        for j in range(SC):
                            lhsT = slot[:, kk * S + j * 128:kk * S + (j + 1) * 128]
                            incs = []
                            if kk == 1 and j == SC - 1:
                                incs.append(ring_mark(fi))
                            TMM(lambda e, l=lhsT, j=j, k=k, st=(k == 0), sp=(k == KC - 1):
                                e.matmul(bank(j), l, xnc(k), start=st, stop=sp), incs)
                c_sweep_done = (sem_rc, CRC.n)
                for j in range(SC):
                    if sweep == 0:
                        W2("scalar", c_sweep_done)
                        c_p1_evac_u[j] = OPA(lambda e, j=j: e.copy(feat(j), bank(j)))
                    else:
                        W2("vector", c_sweep_done)
                        c_p1_delta[j] = OPV(lambda e, j=j: e.tensor_scalar(
                            delta_j(j), bank(j), LP[:, j:j + 1], None, ALU.subtract))

            for j in range(SC):
                dj = delta_j(j)
                pj = pw_j(j)
                uj = feat(j)
                W("vector", sem_v, c_p1_delta[j])
                W("vector", sem_a, c_p1_evac_u[j])
                for m in range(1, 7):
                    if m == 2:
                        OPV(lambda e, d=dj, p=pj: e.tensor_mul(p, d, d))
                    elif m >= 3:
                        OPV(lambda e, d=dj, p=pj: e.tensor_mul(p, p, d))
                    if m == 5:
                        W("scalar", sem_v, CV.n)
                        c_abs = OPA(lambda e, d=dj: e.activation(SCRA[:, :], d, AF.Abs))
                        W("vector", sem_a, c_abs)
                        OPV(lambda e: e.tensor_scalar(SCRB[:, :], SCRA[:, :], THD, None, ALU.is_le))
                        OPV(lambda e, p=pj: e.tensor_mul(p, p, SCRB[:, :]))
                    src = dj if m == 1 else pj
                    dst = feat(m * 8 + j)
                    OPV(lambda e, s=src, d=dst, u=uj, m=m:
                        e.scalar_tensor_tensor(d, s, EMC[:, m - 1:m], u, ALU.mult, ALU.mult))
            c_fg_done = CV.n

            c_ffn_evac = {}
            for b in range(NBAND):
                par = b % 2
                bk = [4 * par + q for q in range(4)]
                for g4 in range(8):
                    src = v_ffngu[:, g4 * 4:(g4 + 1) * 4, b * 512:(b + 1) * 512]
                    slot, fi, wcnt = ring_load(src)
                    W2("tensor", wcnt)
                    if g4 == 0:
                        if b == 0:
                            W("tensor", sem_v, c_fg_done)
                        if b >= 2:
                            W("tensor", sem_v, c_ffn_evac[b - 2])
                    for kk in range(4):
                        k = g4 * 4 + kk
                        for q in range(4):
                            lhsT = slot[:, kk * 512 + q * 128:kk * 512 + (q + 1) * 128]
                            incs = []
                            if kk == 3 and q == 3:
                                incs.append(ring_mark(fi))
                            TMM(lambda e, l=lhsT, bq=bk[q], k=k, st=(k == 0), sp=(k == KC - 1):
                                e.matmul(bank(bq), l, xnc(k), start=st, stop=sp), incs)
                c_band_done = (sem_rc, CRC.n)
                for q in range(2):
                    fchunk = 2 * b + q
                    sg = SGF[:, q * NT:(q + 1) * NT]
                    W2("scalar", c_band_done)
                    if b >= 1:
                        W("scalar", sem_v, c_ffn_evac[b - 1])
                    c_silu = OPA(lambda e, s=sg, bq=bk[q]:
                                 e.activation(s, bank(bq), AF.Silu))
                    W("vector", sem_a, c_silu)
                    if fchunk < 32:
                        W("vector", sem_v, c_fg_done)
                    c_ffn_evac[b] = OPV(lambda e, s=sg, f=fchunk, bq=bk[2 + q]:
                                        e.tensor_mul(wint(f), s, bank(bq)))
            c_ffn_done = c_ffn_evac[NBAND - 1]
            c_ffn_done_prev = c_ffn_done

            ngr = (EPI_KT + 15) // 16
            for o in range(OC):
                par = o % 2
                bs = par
                bg = 2 + par
                gslots = []
                for gi in range(min(3, ngr)):
                    k0 = gi * 16
                    kn = min(16, EPI_KT - k0)
                    gslots.append(ring_load(v_epiw[:, k0:k0 + kn, o * 128:(o + 1) * 128]) + (kn,))
                if t > 0 or o >= 1:
                    W("sync", sem_pe, c_web_done)
                tgt_weff = CWEB.add(16)
                E("sync", lambda e, s=v_weff[:, :, o * 128:(o + 1) * 128],
                  d=WEB.rearrange("p (k c) -> p k c", c=128):
                  e.dma_start(out=d, in_=s).then_inc(sem_web, 16))
                if t > 0 or o >= 2:
                    W("sync", sem_v, c_epi_out[(o - 2) % OC])
                tgt_xres = CXR[par].add(16)
                E("sync", lambda e, s=d_x[o * 128:(o + 1) * 128, tc],
                  d=XRES[:, par * NT:(par + 1) * NT], m=sem_xr[par]:
                  e.dma_start(out=d, in_=s).then_inc(m, 16))

                if o == 0:
                    W("tensor", sem_v, c_ffn_done)
                if t > 0 or o >= 2:
                    W("tensor", sem_v, c_epi_sg[(o - 2) % OC])
                kt = 0
                for gi in range(ngr):
                    slot, fi, wcnt, kn = gslots[gi]
                    W2("tensor", wcnt)
                    for kk in range(kn):
                        lhsT = slot[:, kk * 128:(kk + 1) * 128]
                        rhs = feat(kt) if kt < 56 else wint(kt - 56)
                        incs = []
                        if kk == kn - 1:
                            incs.append(ring_mark(fi))
                        TMM(lambda e, l=lhsT, r=rhs, st=(kt == 0), sp=(kt == EPI_KT - 1), b=bs:
                            e.matmul(bank(b), l, r, start=st, stop=sp), incs)
                        kt += 1
                    if gi + 3 < ngr:
                        k0 = (gi + 3) * 16
                        kn2 = min(16, EPI_KT - k0)
                        gslots.append(ring_load(v_epiw[:, k0:k0 + kn2, o * 128:(o + 1) * 128]) + (kn2,))
                c_s_done = (sem_rc, CRC.n)

                W("tensor", sem_web, tgt_weff)
                if t > 0 or o >= 2:
                    W("tensor", sem_a, c_epi_g[(o - 2) % OC])
                for k in range(KC):
                    lhsT = WEB[:, k * 128:(k + 1) * 128]
                    incs = []
                    if k == KC - 1:
                        incs.append((sem_pe, 1))
                    TMM(lambda e, l=lhsT, k=k, st=(k == 0), sp=(k == KC - 1), b=bg:
                        e.matmul(bank(b), l, xnc(k), start=st, stop=sp), incs)
                c_g_done = CPE.add(1)
                c_web_done = c_g_done

                W("scalar", sem_pe, c_g_done)
                if t > 0 or o >= 1:
                    W("scalar", sem_v, c_epi_sg[(o - 1) % OC])
                c_epi_g[o] = OPA(lambda e, o=o, b=bg:
                                 e.activation(SCRA[:, :], bank(b), AF.Silu, bias=BB[:, o:o + 1],
                                              scale=1.0))
                sgbuf = SGF[:, par * NT:(par + 1) * NT]
                W2("vector", c_s_done)
                W("vector", sem_a, c_epi_g[o])
                c_epi_sg[o] = OPV(lambda e, s=sgbuf, b=bs:
                                  e.tensor_mul(s, bank(b), SCRA[:, :]))
                W("vector", sem_xr[par], tgt_xres)
                if t > 0 or o >= 1:
                    W("vector", sem_out, c_out_dma[(o - 1) % OC])
                c_epi_out[o] = OPV(lambda e, s=sgbuf, x=XRES[:, par * NT:(par + 1) * NT]:
                                   e.tensor_add(OUTB[:, :], s, x))
                W("sync", sem_v, c_epi_out[o])
                cnt_out = COUT.add(16)
                E("sync", lambda e, d=d_out[o * 128:(o + 1) * 128, tc]:
                  e.dma_start(out=d, in_=OUTB[:, :]).then_inc(sem_out, 16))
                c_out_dma[o] = cnt_out

        W("sync", sem_out, COUT.n)

        with nc.Block() as block:
            @block.sync
            def _(eng):
                for fn in prog["sync"]:
                    fn(eng)

            @block.gpsimd
            def _(eng):
                for fn in prog["gpsimd"]:
                    fn(eng)

            @block.vector
            def _(eng):
                for fn in prog["vector"]:
                    fn(eng)

            @block.scalar
            def _(eng):
                for fn in prog["scalar"]:
                    fn(eng)

            @block.tensor
            def _(eng):
                for fn in prog["tensor"]:
                    fn(eng)

    return nc


def _prep_inputs(inputs):
    x = np.ascontiguousarray(np.asarray(inputs["L0_attn_hidden_states"])).reshape(16384, H)
    x_t = np.ascontiguousarray(x.T)

    up_t = np.ascontiguousarray(np.asarray(inputs["L0_attn_up_w"]).T)
    gate_t = np.ascontiguousarray(np.asarray(inputs["L0_attn_gate_w"]).T)
    fg_t = np.asarray(inputs["L0_attn_ffn_gate_w"]).T
    fu_t = np.asarray(inputs["L0_attn_ffn_up_w"]).T
    gb = fg_t.reshape(H, NBAND, 256)
    ub = fu_t.reshape(H, NBAND, 256)
    ffngu = np.ascontiguousarray(np.concatenate([gb, ub], axis=2).reshape(H, 2 * F))

    lao = np.asarray(inputs["L0_attn_local_approx_output"])
    fuse = np.asarray(inputs["L0_attn_fuse_weight"])
    wcat_t = np.concatenate([lao.T[None], fuse.transpose(2, 1, 0)], axis=0).reshape(7 * S, H)
    down_t = np.asarray(inputs["L0_attn_ffn_down_w"]).T
    epiw = np.ascontiguousarray(np.concatenate([wcat_t, down_t], axis=0))

    wbt = np.ascontiguousarray(np.asarray(inputs["L0_ffn_W_base"]).T)

    n = H
    tt = np.arange(n, dtype=np.float64)
    qt = np.zeros((n, n), dtype=np.float64)
    qt[0, :] = 1.0 / np.sqrt(n)
    maxf = (n - 1) // 2
    f_idx = np.arange(1, maxf + 1, dtype=np.float64)
    ang = 2.0 * np.pi * np.outer(f_idx, tt) / n
    qt[1:1 + 2 * maxf:2, :] = np.sqrt(2.0 / n) * np.cos(ang)
    qt[2:2 + 2 * maxf:2, :] = np.sqrt(2.0 / n) * np.sin(ang)
    qt[n - 1, :] = ((-1.0) ** tt) / np.sqrt(n)
    qt = np.ascontiguousarray(qt.astype(ml_dtypes.bfloat16))

    enc = np.ascontiguousarray(np.concatenate(
        [np.asarray(inputs["L0_ffn_enc0"]), np.asarray(inputs["L0_ffn_enc1"])], axis=0))

    co0 = np.ascontiguousarray(
        np.asarray(inputs["L0_ffn_coefs0"]).reshape(3, 32, 128, 32).transpose(2, 0, 1, 3).reshape(128, -1))
    co1 = np.ascontiguousarray(
        np.asarray(inputs["L0_ffn_coefs1"]).reshape(3, 32, 128, 8).transpose(2, 0, 1, 3).reshape(128, -1))
    mk0 = np.ascontiguousarray(
        np.asarray(inputs["L0_ffn_mask0"]).astype(np.uint8).reshape(3, 32, 128, 32).transpose(2, 0, 1, 3).reshape(128, -1))
    mk1 = np.ascontiguousarray(
        np.asarray(inputs["L0_ffn_mask1"]).astype(np.uint8).reshape(3, 32, 128, 8).transpose(2, 0, 1, 3).reshape(128, -1))

    lp = np.ascontiguousarray(np.asarray(inputs["L0_attn_local_point"]).reshape(SC, 128).T)
    bb = np.ascontiguousarray(np.asarray(inputs["L0_ffn_b_base"]).reshape(OC, 128).T)
    disc = np.ascontiguousarray(np.asarray(inputs["L0_attn_discount_factor"]).reshape(1, G))

    shared = dict(up_t=up_t, gate_t=gate_t, ffngu=ffngu, epiw=epiw, wbt=wbt,
                  qt=qt, enc=enc, co0=co0, co1=co1, mk0=mk0, mk1=mk1,
                  lp=lp, bb=bb, disc=disc)
    in_maps = []
    for c in range(N_CORES):
        m = dict(shared)
        m["x_t"] = np.ascontiguousarray(x_t[:, c * TOK_PER_CORE:(c + 1) * TOK_PER_CORE])
        in_maps.append(m)
    return in_maps


def kernel(**inputs):
    in_maps = _prep_inputs(inputs)
    nc = build_program()
    res = run_bass_kernel_spmd(nc, in_maps, core_ids=list(range(N_CORES)))
    out_t = np.concatenate([r["out_t"] for r in res.results], axis=1)
    return np.ascontiguousarray(out_t.T).reshape(4, 4096, 4096)


# version 03
# speedup vs baseline: 1.0225x; 1.0225x over previous
import contextlib
import numpy as np
import ml_dtypes
import concourse.bass as bass
import concourse.mybir as mybir
from concourse.bass_utils import run_bass_kernel_spmd

F32 = mybir.dt.float32
BF16 = mybir.dt.bfloat16
U8 = mybir.dt.uint8
AF = mybir.ActivationFunctionType
ALU = mybir.AluOpType

N_CORES = 8
TOK_PER_CORE = 2048
NT = 512
N_TILES = TOK_PER_CORE // NT
H = 4096
KC = H // 128
S = 1024
SC = S // 128
F = 9984
FC = F // 128
NBAND = F // 256
OC = KC
G = 6
EPI_ROWS = S * 7 + F
EPI_KT = EPI_ROWS // 128
THD = 2.5
EPS = 1e-5


class Ctr:
    __slots__ = ("n",)
    def __init__(self):
        self.n = 0
    def add(self, k=1):
        self.n += k
        return self.n


def build_program():
    nc = bass.Bass("TRN2", target_bir_lowering=False, debug=False)

    d_x = nc.dram_tensor("x_t", [H, TOK_PER_CORE], F32, kind="ExternalInput").ap()
    d_up = nc.dram_tensor("up_t", [H, S], F32, kind="ExternalInput").ap()
    d_gate = nc.dram_tensor("gate_t", [H, S], F32, kind="ExternalInput").ap()
    d_ffngu = nc.dram_tensor("ffngu", [H, 2 * F], F32, kind="ExternalInput").ap()
    d_epiw = nc.dram_tensor("epiw", [EPI_ROWS, H], F32, kind="ExternalInput").ap()
    d_wbt = nc.dram_tensor("wbt", [H, H], F32, kind="ExternalInput").ap()
    d_qt = nc.dram_tensor("qt", [H, H], BF16, kind="ExternalInput").ap()
    d_enc = nc.dram_tensor("enc", [40, H], F32, kind="ExternalInput").ap()
    d_co0 = nc.dram_tensor("co0", [128, 3 * 32 * 32], F32, kind="ExternalInput").ap()
    d_co1 = nc.dram_tensor("co1", [128, 3 * 32 * 8], F32, kind="ExternalInput").ap()
    d_mk0 = nc.dram_tensor("mk0", [128, 3 * 32 * 32], U8, kind="ExternalInput").ap()
    d_mk1 = nc.dram_tensor("mk1", [128, 3 * 32 * 8], U8, kind="ExternalInput").ap()
    d_lp = nc.dram_tensor("lp", [128, SC], F32, kind="ExternalInput").ap()
    d_bb = nc.dram_tensor("bb", [128, OC], F32, kind="ExternalInput").ap()
    d_disc = nc.dram_tensor("disc", [1, G], F32, kind="ExternalInput").ap()
    d_out = nc.dram_tensor("out_t", [H, TOK_PER_CORE], F32, kind="ExternalOutput").ap()
    d_weff = nc.dram_tensor("weff_scr", [H, H], BF16, kind="Internal").ap()
    NGROUPS = 2 * 16 + NBAND * 8 + OC * 9
    NGH = (NGROUPS + 1) // 2
    d_blob0 = nc.dram_tensor("wblob0", [NGH * 128, 2048], BF16, kind="Internal").ap()
    d_blob1 = nc.dram_tensor("wblob1", [NGH * 128, 2048], BF16, kind="Internal").ap()

    v_x = d_x.rearrange("(k p) t -> p k t", p=128)
    v_up = d_up.rearrange("(k p) s -> p k s", p=128)
    v_gate = d_gate.rearrange("(k p) s -> p k s", p=128)
    v_ffngu = d_ffngu.rearrange("(k p) s -> p k s", p=128)
    v_epiw = d_epiw.rearrange("(k p) h -> p k h", p=128)
    v_qt = d_qt.rearrange("(k p) t -> p k t", p=128)
    v_weff = d_weff.rearrange("(k p) t -> p k t", p=128)
    v_blob0 = d_blob0.rearrange("(g p) c -> p g c", p=128)
    v_blob1 = d_blob1.rearrange("(g p) c -> p g c", p=128)

    def blob_view(g, ncols):
        if g < NGH:
            return v_blob0[:, g, 0:ncols]
        return v_blob1[:, g - NGH, 0:ncols]

    prog = {e: [] for e in ("sync", "gpsimd", "vector", "scalar", "tensor")}

    def E(eng, fn):
        prog[eng].append(fn)

    with contextlib.ExitStack() as es:
        ec = es.enter_context

        sem_pe = ec(nc.semaphore("sem_pe"))
        sem_rc = ec(nc.semaphore("sem_rc"))
        sem_a = ec(nc.semaphore("sem_a"))
        sem_v = ec(nc.semaphore("sem_v"))
        sem_bar = ec(nc.semaphore("sem_bar"))
        sem_out = ec(nc.semaphore("sem_out"))
        sem_r = [ec(nc.semaphore(f"sem_r{i}")) for i in range(3)]
        sem_xs = [ec(nc.semaphore(f"sem_xs{i}")) for i in range(2)]
        sem_wb = [ec(nc.semaphore(f"sem_wb{i}")) for i in range(2)]
        sem_ws = [ec(nc.semaphore(f"sem_ws{i}")) for i in range(2)]
        sem_xr = [ec(nc.semaphore(f"sem_xr{i}")) for i in range(2)]
        sem_web = ec(nc.semaphore("sem_web"))
        sem_bs = [ec(nc.semaphore(f"sem_bs{i}")) for i in range(3)]
        sem_ld = ec(nc.semaphore("sem_ld"))

        COUT, CPE, CRC, CA, CV = (Ctr() for _ in range(5))
        CR = [Ctr() for _ in range(3)]
        CXS = [Ctr() for _ in range(2)]
        CWB = [Ctr() for _ in range(2)]
        CWS = [Ctr() for _ in range(2)]
        CXR = [Ctr() for _ in range(2)]
        CWEB = Ctr()
        CBS = [Ctr() for _ in range(3)]
        CLD = Ctr()

        def W(eng, sem, n):
            if n > 0:
                E(eng, lambda e, s=sem, n=n: e.wait_ge(s, n))

        def W2(eng, sw):
            sem, n = sw
            W(eng, sem, n)

        def OPV(fn):
            if CV.n:
                E("vector", lambda e, n=CV.n: e.wait_ge(sem_v, n))
            E("vector", lambda e, f=fn: f(e).then_inc(sem_v, 1))
            return CV.add(1)

        def OPA(fn):
            if CA.n:
                E("scalar", lambda e, n=CA.n: e.wait_ge(sem_a, n))
            E("scalar", lambda e, f=fn: f(e).then_inc(sem_a, 1))
            return CA.add(1)

        RING = ec(nc.sbuf_tensor("RING", [128, 3 * 2048], BF16))
        LP = ec(nc.sbuf_tensor("LP", [128, SC], F32))
        BB = ec(nc.sbuf_tensor("BB", [128, OC], F32))
        EMC = ec(nc.sbuf_tensor("EMC", [128, 8], F32))
        ONESB = ec(nc.sbuf_tensor("ONESB", [128, 1], BF16))
        ONES1 = ec(nc.sbuf_tensor("ONES1", [1, 128], F32))
        CST = ec(nc.sbuf_tensor("CST", [1, 8], F32))
        CSTC = ec(nc.sbuf_tensor("CSTC", [128, 2], F32))
        PB = ec(nc.psum_tensor("PB", [128, 8 * NT], F32))

        def bank(j):
            return PB[:, j * NT:(j + 1) * NT]

        ring_idx = Ctr()
        ring_cons_at = {}
        ring_store_at = {}
        blob_cursor = Ctr()

        def ring_load(src_ap, mode="direct"):
            i = ring_idx.n
            ring_idx.add()
            sl = i % 3
            slot = RING[:, sl * 2048:(sl + 1) * 2048]
            if i >= 3:
                W("gpsimd", sem_rc, ring_cons_at[i - 3])
                if (i - 3) in ring_store_at:
                    W2("gpsimd", ring_store_at[i - 3])
            if mode == "blob":
                g = blob_cursor.n
                blob_cursor.add()
                ncols = int(np.prod(src_ap.shape[1:]))
                src_ap = blob_view(g, ncols)
            ncols = int(np.prod(src_ap.shape[1:]))
            dst = slot[:, 0:ncols]
            if len(src_ap.shape) == 3:
                dst = dst.rearrange("p (a b) -> p a b", a=src_ap.shape[1])
            tgt = CR[sl].add(16)
            E("gpsimd", lambda e, s=src_ap, d=dst, m=sem_r[sl]:
              e.dma_start(out=d, in_=s).then_inc(m, 16))
            if mode == "save":
                g = blob_cursor.n
                blob_cursor.add()
                W("sync", sem_r[sl], tgt)
                stgt = CBS[sl].add(16)
                E("sync", lambda e, s=slot[:, 0:ncols], d=blob_view(g, ncols), m=sem_bs[sl]:
                  e.dma_start(out=d, in_=s).then_inc(m, 16))
                ring_store_at[i] = (sem_bs[sl], stgt)
            return slot, i, (sem_r[sl], tgt)

        def _chain(inst, incs):
            for s_, v_ in incs:
                inst = inst.then_inc(s_, v_)
            return inst

        def TMM(fn, incs=()):
            if incs:
                E("tensor", lambda e, f=fn, iv=tuple(incs): _chain(f(e), iv))
            else:
                E("tensor", fn)

        def ring_mark(i):
            ring_cons_at[i] = CRC.add(1)
            return (sem_rc, 1)

        with contextlib.ExitStack() as ses:
            sc_ = ses.enter_context
            CO0 = sc_(nc.sbuf_tensor("CO0", [128, 3 * 32 * 32], F32))
            CO1 = sc_(nc.sbuf_tensor("CO1", [128, 3 * 32 * 8], F32))
            MK0 = sc_(nc.sbuf_tensor("MK0", [128, 3 * 32 * 32], U8))
            MK1 = sc_(nc.sbuf_tensor("MK1", [128, 3 * 32 * 8], U8))
            MKF = sc_(nc.sbuf_tensor("MKF", [128, 3 * 32 * 32], F32))
            W0S = sc_(nc.sbuf_tensor("W0S", [128, 32 * 32], F32))
            W1S = sc_(nc.sbuf_tensor("W1S", [128, 32 * 8], F32))
            W01B = sc_(nc.sbuf_tensor("W01B", [128, 32 * 40], BF16))
            ENCF = sc_(nc.sbuf_tensor("ENCF", [40, H], F32))
            ENCB = sc_(nc.sbuf_tensor("ENCB", [40, H], BF16))
            VCB = sc_(nc.sbuf_tensor("VCB", [40, H], BF16))
            WBB = sc_(nc.sbuf_tensor("WBB", [128, 2 * H], F32))
            WEO = sc_(nc.sbuf_tensor("WEO", [128, 2 * H], BF16))
            DRW = sc_(nc.sbuf_tensor("DRW", [1, 16], F32))

            for src, dst in ((d_co0, CO0[:, :]), (d_co1, CO1[:, :]),
                             (d_mk0, MK0[:, :]), (d_mk1, MK1[:, :]),
                             (d_enc, ENCF[:, :]), (d_lp, LP[:, :]),
                             (d_bb, BB[:, :]), (d_disc, DRW[:, 0:G])):
                CLD.add(16)
                E("sync", lambda e, s=src, d=dst: e.dma_start(out=d, in_=s).then_inc(sem_ld, 16))
            c_small = CLD.n

            OPV(lambda e: e.memset(ONESB[:, :], 1.0))
            OPV(lambda e: e.memset(ONES1[:, :], 1.0))
            OPV(lambda e: e.memset(CST[:, 0:1], EPS))
            OPV(lambda e: e.memset(CST[:, 1:2], -0.5))
            OPV(lambda e: e.memset(CST[:, 2:3], -1.0))
            OPV(lambda e: e.memset(CST[:, 3:4], 0.5))
            c_consts = OPV(lambda e: e.memset(CSTC[:, 0:1], 0.5))

            W("vector", sem_ld, c_small)
            OPV(lambda e: e.tensor_copy(MKF[:, :], MK0[:, :]))
            for e_i in range(3):
                sl = slice(e_i * 1024, (e_i + 1) * 1024)
                if e_i == 0:
                    OPV(lambda e, s=sl: e.tensor_mul(W0S[:, :], CO0[:, s], MKF[:, s]))
                else:
                    OPV(lambda e, s=sl: e.tensor_mul(MKF[:, s], CO0[:, s], MKF[:, s]))
                    OPV(lambda e, s=sl: e.tensor_add(W0S[:, :], W0S[:, :], MKF[:, s]))
            OPV(lambda e: e.tensor_copy(MKF[:, 0:3 * 256], MK1[:, :]))
            for e_i in range(3):
                sl = slice(e_i * 256, (e_i + 1) * 256)
                if e_i == 0:
                    OPV(lambda e, s=sl: e.tensor_mul(W1S[:, :], CO1[:, s], MKF[:, s]))
                else:
                    OPV(lambda e, s=sl: e.tensor_mul(MKF[:, s], CO1[:, s], MKF[:, s]))
                    OPV(lambda e, s=sl: e.tensor_add(W1S[:, :], W1S[:, :], MKF[:, s]))
            OPV(lambda e: e.tensor_copy(
                W01B.rearrange("p (c w) -> p c w", w=40)[:, :, 0:32],
                W0S.rearrange("p (c r) -> p c r", r=32)))
            OPV(lambda e: e.tensor_copy(
                W01B.rearrange("p (c w) -> p c w", w=40)[:, :, 32:40],
                W1S.rearrange("p (c r) -> p c r", r=8)))
            c_w01 = OPV(lambda e: e.tensor_copy(ENCB[:, :], ENCF[:, :]))

            W("scalar", sem_ld, c_small)
            W("scalar", sem_v, c_consts)
            c_emc_row = OPA(lambda e: e.activation(DRW[:, 8:8 + G], DRW[:, 0:G], AF.Exp,
                                                   bias=0.0, scale=CST[0:1, 2:3]))
            W("tensor", sem_a, c_emc_row)
            E("tensor", lambda e: e.matmul(PB[:, 7 * NT:7 * NT + G], ONES1[:, :], DRW[:, 8:8 + G],
                                           start=True, stop=True).then_inc(sem_pe, 1))
            c_emc_mm = CPE.add(1)
            W("vector", sem_pe, c_emc_mm)
            c_emc = OPV(lambda e: e.tensor_copy(EMC[:, 0:G], PB[:, 7 * NT:7 * NT + G]))

            c_vcb_prev = 0
            for hh in range(8):
                t0 = hh * NT
                for og in range(8):
                    src = v_qt[:, og * 4:(og + 1) * 4, t0:t0 + NT]
                    slot, fi, wcnt = ring_load(src)
                    W2("tensor", wcnt)
                    if hh == 0 and og == 0:
                        W("tensor", sem_v, c_w01)
                    if og == 0 and hh > 0:
                        W("tensor", sem_a, c_vcb_prev)
                    for oc in range(4):
                        c = og * 4 + oc
                        lhsT = W01B[:, c * 40:(c + 1) * 40]
                        rhs = slot[:, oc * NT:(oc + 1) * NT]
                        incs = []
                        if oc == 3:
                            incs.append(ring_mark(fi))
                        TMM(lambda e, l=lhsT, r=rhs, st=(c == 0), sp=(c == 31):
                            e.matmul(PB[0:40, 6 * NT:7 * NT], l, r, start=st, stop=sp), incs)
                c_vc_mm = (sem_rc, CRC.n)
                W2("scalar", c_vc_mm)
                c_vcb_prev = OPA(lambda e, t0=t0: e.copy(VCB[0:40, t0:t0 + NT],
                                                         PB[0:40, 6 * NT:7 * NT]))

            c_wb_load = {}
            c_wb_stt = {}
            c_stt_at = {}
            c_ws_store = {}

            def emit_wb_load(dc):
                par = dc % 2
                buf = WBB[:, par * H:par * H + H]
                if dc >= 2:
                    W("sync", sem_v, c_wb_stt[dc - 2])
                tgt = CWB[par].add(16)
                E("sync", lambda e, d=buf, s=d_wbt[dc * 128:(dc + 1) * 128, :], m=sem_wb[par]:
                  e.dma_start(out=d, in_=s).then_inc(m, 16))
                c_wb_load[dc] = (sem_wb[par], tgt)

            emit_wb_load(0)
            emit_wb_load(1)
            for dc in range(KC):
                par = dc % 2
                buf = WBB[:, par * H:par * H + H]
                obuf = WEO[:, par * H:par * H + H]
                for hh in range(8):
                    idx = dc * 8 + hh
                    pb_i = 4 + (idx % 2)
                    lhsT = ENCB[:, dc * 128:(dc + 1) * 128]
                    rhs = VCB[:, hh * NT:(hh + 1) * NT]
                    if idx >= 2:
                        W("tensor", sem_v, c_stt_at[idx - 2])
                    if idx == 0:
                        W("tensor", sem_a, c_vcb_prev)
                    E("tensor", lambda e, l=lhsT, r=rhs, b=pb_i:
                      e.matmul(bank(b), l, r, start=True, stop=True).then_inc(sem_pe, 1))
                    c_mm = CPE.add(1)
                    W("vector", sem_pe, c_mm)
                    if hh == 0:
                        W2("vector", c_wb_load[dc])
                        if dc >= 2:
                            W2("vector", c_ws_store[dc - 2])
                    c_stt_at[idx] = OPV(lambda e, o=obuf[:, hh * NT:(hh + 1) * NT], b=pb_i,
                                        w=buf[:, hh * NT:(hh + 1) * NT]:
                                        e.scalar_tensor_tensor(o, bank(b), CSTC[:, 0:1], w,
                                                               ALU.mult, ALU.add))
                c_wb_stt[dc] = c_stt_at[dc * 8 + 7]
                W("sync", sem_v, c_wb_stt[dc])
                tgt = CWS[par].add(16)
                E("sync", lambda e, o=d_weff[dc * 128:(dc + 1) * 128, :], s=obuf, m=sem_ws[par]:
                  e.dma_start(out=o, in_=s).then_inc(m, 16))
                c_ws_store[dc] = (sem_ws[par], tgt)
                if dc + 2 < KC:
                    emit_wb_load(dc + 2)

            for eng in ("sync", "gpsimd", "vector", "scalar", "tensor"):
                W(eng, sem_v, CV.n)
                W(eng, sem_a, CA.n)
                W(eng, sem_pe, CPE.n)
                W(eng, sem_rc, CRC.n)
                W(eng, sem_ld, CLD.n)
                W(eng, sem_ws[0], CWS[0].n)
                W(eng, sem_ws[1], CWS[1].n)
                W(eng, sem_wb[0], CWB[0].n)
                W(eng, sem_wb[1], CWB[1].n)
                for i_ in range(3):
                    W(eng, sem_r[i_], CR[i_].n)

        XN = ec(nc.sbuf_tensor("XN", [128, KC * NT], BF16))
        FEAT = ec(nc.sbuf_tensor("FEAT", [128, 56 * NT], BF16))
        WINT = ec(nc.sbuf_tensor("WINT", [128, FC * NT], BF16))
        WEB = ec(nc.sbuf_tensor("WEB", [128, KC * 128], BF16))
        XRES = ec(nc.sbuf_tensor("XRES", [128, 2 * NT], F32))
        OUTB = ec(nc.sbuf_tensor("OUTB", [128, NT], F32))
        SCRA = ec(nc.sbuf_tensor("SCRA", [128, NT], F32))
        SCRB = ec(nc.sbuf_tensor("SCRB", [128, NT], F32))
        SGF = ec(nc.sbuf_tensor("SGF", [128, 2 * NT], F32))
        ROWS = ec(nc.sbuf_tensor("ROWS", [1, 3 * NT], F32))

        DELTA = WINT[:, 0:8 * NT * 2].bitcast(F32)
        PW = WINT[:, 8 * NT * 2:16 * NT * 2].bitcast(F32)
        XSTAT = FEAT[:, 40 * NT:56 * NT].bitcast(F32)
        RBB = WEB[:, 0:2 * NT].bitcast(F32)
        MRB = WEB[:, 2 * NT:4 * NT].bitcast(F32)
        XBF = FEAT[:, 32 * NT:34 * NT]
        X2BF = FEAT[:, 34 * NT:36 * NT]

        def feat(kc):
            return FEAT[:, kc * NT:(kc + 1) * NT]

        def wint(kc):
            return WINT[:, kc * NT:(kc + 1) * NT]

        def xnc(k):
            return XN[:, k * NT:(k + 1) * NT]

        def delta_j(j):
            return DELTA[:, j * NT:(j + 1) * NT]

        def pw_j(j):
            return PW[:, j * NT:(j + 1) * NT]

        xstat_cons = {}
        c_mmst = {}
        c_epi_sg = {}
        c_epi_g = {}
        c_epi_out = {}
        c_out_dma = {}
        c_web_done = 0
        c_ffn_done_prev = 0
        c_xn_done = 0
        c_bcast = 0

        RA = ROWS[0:1, 0:NT]
        RB = ROWS[0:1, NT:2 * NT]
        RC = ROWS[0:1, 2 * NT:3 * NT]

        for t in range(N_TILES):
            tc = slice(t * NT, (t + 1) * NT)
            c_pe_prev_tile = CPE.n
            c_rc_prev_tile = CRC.n
            rmode = "save" if t == 0 else "blob"
            blob_cursor.n = 0
            if t == 1:
                for i_ in range(3):
                    W("gpsimd", sem_bs[i_], CBS[i_].n)

            c_xload = {}
            c_xn_at = {}

            def emit_ln_load(key, t=t, tc=tc):
                gk = t * 16 + key
                slot = key % 2
                if gk >= 2 and (gk - 2) in xstat_cons:
                    for sem_o, cnt_o in xstat_cons[gk - 2]:
                        W("sync", sem_o, cnt_o)
                if key == 0 and t > 0:
                    W("sync", sem_pe, c_pe_prev_tile)
                    W("sync", sem_rc, c_rc_prev_tile)
                g4 = key % 8
                src = v_x[:, g4 * 4:(g4 + 1) * 4, tc]
                dst = XSTAT[:, slot * 4 * NT:(slot + 1) * 4 * NT].rearrange(
                    "p (a b) -> p a b", a=4)
                tgt = CXS[slot].add(16)
                E("sync", lambda e, s=src, d=dst, m=sem_xs[slot]:
                  e.dma_start(out=d, in_=s).then_inc(m, 16))
                c_xload[key] = (sem_xs[slot], tgt)

            def emit_stats_group(g4, t=t):
                slot = g4 % 2
                for kk in range(4):
                    k = g4 * 4 + kk
                    xv = XSTAT[:, (slot * 4 + kk) * NT:(slot * 4 + kk + 1) * NT]
                    xb = XBF[:, (k % 2) * NT:(k % 2 + 1) * NT]
                    x2 = X2BF[:, (k % 2) * NT:(k % 2 + 1) * NT]
                    W2("vector", c_xload[g4])
                    W2("scalar", c_xload[g4])
                    if k >= 2 or t > 0:
                        kk2 = (k - 2) % KC
                        W("vector", sem_pe, c_mmst[kk2])
                        W("scalar", sem_pe, c_mmst[kk2])
                    if k == 0 and t > 0:
                        W("vector", sem_pe, c_pe_prev_tile)
                        W("scalar", sem_pe, c_pe_prev_tile)
                        W("vector", sem_rc, c_rc_prev_tile)
                        W("scalar", sem_rc, c_rc_prev_tile)
                    c_cast = OPV(lambda e, a=xb, b=xv: e.tensor_copy(a, b))
                    c_sq = OPA(lambda e, a=x2, b=xv: e.activation(a, b, AF.Square))
                    W("tensor", sem_v, c_cast)
                    W("tensor", sem_a, c_sq)
                    if k == 0 and t > 0:
                        W("tensor", sem_v, c_ffn_done_prev)
                    st = (k == 0)
                    sp = (k == KC - 1)
                    E("tensor", lambda e, a=xb, st=st, sp=sp:
                      e.matmul(PB[0:1, 7 * NT:8 * NT], ONESB[:, :], a, start=st, stop=sp))
                    E("tensor", lambda e, a=x2, st=st, sp=sp:
                      e.matmul(PB[0:1, 4 * NT:5 * NT], ONESB[:, :], a, start=st, stop=sp).then_inc(sem_pe, 1))
                    c_mmst[k] = CPE.add(1)
                xstat_cons[t * 16 + g4] = ((sem_a, CA.n), (sem_v, CV.n))

            def emit_xn_group(g4, t=t):
                slot = g4 % 2
                for kk in range(4):
                    k = g4 * 4 + kk
                    xv = XSTAT[:, (slot * 4 + kk) * NT:(slot * 4 + kk + 1) * NT]
                    W2("vector", c_xload[8 + g4])
                    if k == 0:
                        W("vector", sem_a, c_rbb)
                    OPV(lambda e, xv=xv: e.tensor_mul(SCRB[:, :], xv, RBB[:, :]))
                    c_xn_at[k] = OPV(lambda e, k=k: e.tensor_sub(xnc(k), SCRB[:, :], MRB[:, :]))
                xstat_cons[t * 16 + 8 + g4] = ((sem_v, CV.n),)

            emit_ln_load(0)
            emit_ln_load(1)
            for g4 in range(8):
                emit_stats_group(g4)
                if g4 + 2 <= 7:
                    emit_ln_load(g4 + 2)
            c_stats_mm = c_mmst[KC - 1]

            W("vector", sem_pe, c_stats_mm)
            OPV(lambda e: e.tensor_scalar_mul(RA, PB[0:1, 7 * NT:8 * NT], 1.0 / H))
            OPV(lambda e: e.tensor_scalar_mul(RB, PB[0:1, 4 * NT:5 * NT], 1.0 / H))
            OPV(lambda e: e.tensor_mul(RC, RA, RA))
            c_var = OPV(lambda e: e.tensor_sub(RB, RB, RC))
            W("scalar", sem_v, c_var)
            c_ln = OPA(lambda e: e.activation(RC, RB, AF.Ln, bias=CST[0:1, 0:1], scale=1.0))
            c_rstd = OPA(lambda e: e.activation(RB, RC, AF.Exp, bias=0.0,
                                                scale=CST[0:1, 1:2]))
            W("vector", sem_a, c_rstd)
            c_mr = OPV(lambda e: e.tensor_mul(RC, RA, RB))

            W("tensor", sem_v, c_mr)
            if t > 0:
                W("tensor", sem_v, c_ffn_done_prev)
            if t > 0:
                W("tensor", sem_pe, c_pe_prev_tile)
            E("tensor", lambda e: e.matmul(PB[:, 5 * NT:6 * NT], ONES1[:, :], RB, start=True, stop=True))
            E("tensor", lambda e: e.matmul(PB[:, 6 * NT:7 * NT], ONES1[:, :], RC,
                                           start=True, stop=True).then_inc(sem_pe, 1))
            c_bcast = CPE.add(1)
            W("scalar", sem_pe, c_bcast)
            if t > 0:
                W("scalar", sem_pe, c_pe_prev_tile)
            OPA(lambda e: e.copy(RBB[:, :], PB[:, 5 * NT:6 * NT]))
            c_rbb = OPA(lambda e: e.copy(MRB[:, :], PB[:, 6 * NT:7 * NT]))

            emit_ln_load(8)
            emit_ln_load(9)
            for g4 in range(8):
                emit_xn_group(g4)
                if g4 + 2 <= 7:
                    emit_ln_load(8 + g4 + 2)
            c_xn_done = CV.n

            c_p1_evac_u = {}
            c_p1_delta = {}
            for sweep in range(2):
                vsrc = v_up if sweep == 0 else v_gate
                for g2 in range(16):
                    src = vsrc[:, g2 * 2:(g2 + 1) * 2, :]
                    slot, fi, wcnt = ring_load(src, rmode)
                    W2("tensor", wcnt)
                    W("tensor", sem_v, c_xn_at[2 * g2 + 1])
                    if g2 == 0:
                        if sweep == 0:
                            W("tensor", sem_a, c_rbb)
                            if t > 0:
                                W("tensor", sem_v, c_epi_sg[31])
                        if sweep == 1:
                            W("tensor", sem_a, c_p1_evac_u[SC - 1])
                    for kk in range(2):
                        k = g2 * 2 + kk
                        for j in range(SC):
                            lhsT = slot[:, kk * S + j * 128:kk * S + (j + 1) * 128]
                            incs = []
                            if kk == 1 and j == SC - 1:
                                incs.append(ring_mark(fi))
                            TMM(lambda e, l=lhsT, j=j, k=k, st=(k == 0), sp=(k == KC - 1):
                                e.matmul(bank(j), l, xnc(k), start=st, stop=sp), incs)
                c_sweep_done = (sem_rc, CRC.n)
                for j in range(SC):
                    if sweep == 0:
                        W2("scalar", c_sweep_done)
                        c_p1_evac_u[j] = OPA(lambda e, j=j: e.copy(feat(j), bank(j)))
                    else:
                        W2("vector", c_sweep_done)
                        c_p1_delta[j] = OPV(lambda e, j=j: e.tensor_scalar(
                            delta_j(j), bank(j), LP[:, j:j + 1], None, ALU.subtract))

            def emit_fg(j):
                dj = delta_j(j)
                pj = pw_j(j)
                uj = feat(j)
                W("vector", sem_v, c_p1_delta[j])
                W("vector", sem_a, c_p1_evac_u[j])
                for m in range(1, 7):
                    if m == 2:
                        OPV(lambda e, d=dj, p=pj: e.tensor_mul(p, d, d))
                    elif m >= 3:
                        OPV(lambda e, d=dj, p=pj: e.tensor_mul(p, p, d))
                    if m == 5:
                        W("scalar", sem_v, CV.n)
                        c_abs = OPA(lambda e, d=dj: e.activation(SCRA[:, :], d, AF.Abs))
                        W("vector", sem_a, c_abs)
                        OPV(lambda e: e.tensor_scalar(SCRB[:, :], SCRA[:, :], THD, None, ALU.is_le))
                        OPV(lambda e, p=pj: e.tensor_mul(p, p, SCRB[:, :]))
                    fsrc = dj if m == 1 else pj
                    fdst = feat(m * 8 + j)
                    OPV(lambda e, s=fsrc, d=fdst, u=uj, m=m:
                        e.scalar_tensor_tensor(d, s, EMC[:, m - 1:m], u, ALU.mult, ALU.mult))

            c_ffn_evac = {}
            for b in range(NBAND):
                par = b % 2
                bk = [4 * par + q for q in range(4)]
                for g4 in range(8):
                    src = v_ffngu[:, g4 * 4:(g4 + 1) * 4, b * 512:(b + 1) * 512]
                    slot, fi, wcnt = ring_load(src, rmode)
                    W2("tensor", wcnt)
                    if g4 == 0:
                        if b == 0:
                            W("tensor", sem_v, c_p1_delta[3])
                        if b == 1:
                            W("tensor", sem_v, c_p1_delta[7])
                        if b >= 2:
                            W("tensor", sem_v, c_ffn_evac[b - 2])
                    for kk in range(4):
                        k = g4 * 4 + kk
                        for q in range(4):
                            lhsT = slot[:, kk * 512 + q * 128:kk * 512 + (q + 1) * 128]
                            incs = []
                            if kk == 3 and q == 3:
                                incs.append(ring_mark(fi))
                            TMM(lambda e, l=lhsT, bq=bk[q], k=k, st=(k == 0), sp=(k == KC - 1):
                                e.matmul(bank(bq), l, xnc(k), start=st, stop=sp), incs)
                c_band_done = (sem_rc, CRC.n)
                if b < SC:
                    emit_fg(b)
                for q in range(2):
                    fchunk = 2 * b + q
                    sg = SGF[:, q * NT:(q + 1) * NT]
                    W2("scalar", c_band_done)
                    if b >= 1:
                        W("scalar", sem_v, c_ffn_evac[b - 1])
                    c_silu = OPA(lambda e, s=sg, bq=bk[q]:
                                 e.activation(s, bank(bq), AF.Silu))
                    W("vector", sem_a, c_silu)
                    c_ffn_evac[b] = OPV(lambda e, s=sg, f=fchunk, bq=bk[2 + q]:
                                        e.tensor_mul(wint(f), s, bank(bq)))
            c_ffn_done = c_ffn_evac[NBAND - 1]
            c_ffn_done_prev = c_ffn_done

            ngr = (EPI_KT + 15) // 16
            for o in range(OC):
                par = o % 2
                bs = par
                bg = 2 + par
                gslots = []
                for gi in range(min(3, ngr)):
                    k0 = gi * 16
                    kn = min(16, EPI_KT - k0)
                    gslots.append(ring_load(v_epiw[:, k0:k0 + kn, o * 128:(o + 1) * 128], rmode) + (kn,))
                if t > 0 or o >= 1:
                    W("sync", sem_pe, c_web_done)
                if o == 0:
                    W("sync", sem_v, c_xn_done)
                tgt_weff = CWEB.add(16)
                E("sync", lambda e, s=v_weff[:, :, o * 128:(o + 1) * 128],
                  d=WEB.rearrange("p (k c) -> p k c", c=128):
                  e.dma_start(out=d, in_=s).then_inc(sem_web, 16))
                if t > 0 or o >= 2:
                    W("sync", sem_v, c_epi_out[(o - 2) % OC])
                tgt_xres = CXR[par].add(16)
                E("sync", lambda e, s=d_x[o * 128:(o + 1) * 128, tc],
                  d=XRES[:, par * NT:(par + 1) * NT], m=sem_xr[par]:
                  e.dma_start(out=d, in_=s).then_inc(m, 16))

                if o == 0:
                    W("tensor", sem_v, c_ffn_done)
                if t > 0 or o >= 2:
                    W("tensor", sem_v, c_epi_sg[(o - 2) % OC])
                kt = 0
                for gi in range(ngr):
                    slot, fi, wcnt, kn = gslots[gi]
                    W2("tensor", wcnt)
                    for kk in range(kn):
                        lhsT = slot[:, kk * 128:(kk + 1) * 128]
                        rhs = feat(kt) if kt < 56 else wint(kt - 56)
                        incs = []
                        if kk == kn - 1:
                            incs.append(ring_mark(fi))
                        TMM(lambda e, l=lhsT, r=rhs, st=(kt == 0), sp=(kt == EPI_KT - 1), b=bs:
                            e.matmul(bank(b), l, r, start=st, stop=sp), incs)
                        kt += 1
                    if gi + 3 < ngr:
                        k0 = (gi + 3) * 16
                        kn2 = min(16, EPI_KT - k0)
                        gslots.append(ring_load(v_epiw[:, k0:k0 + kn2, o * 128:(o + 1) * 128], rmode) + (kn2,))
                c_s_done = (sem_rc, CRC.n)

                W("tensor", sem_web, tgt_weff)
                if t > 0 or o >= 2:
                    W("tensor", sem_a, c_epi_g[(o - 2) % OC])
                for k in range(KC):
                    lhsT = WEB[:, k * 128:(k + 1) * 128]
                    incs = []
                    if k == KC - 1:
                        incs.append((sem_pe, 1))
                    TMM(lambda e, l=lhsT, k=k, st=(k == 0), sp=(k == KC - 1), b=bg:
                        e.matmul(bank(b), l, xnc(k), start=st, stop=sp), incs)
                c_g_done = CPE.add(1)
                c_web_done = c_g_done

                W("scalar", sem_pe, c_g_done)
                if t > 0 or o >= 1:
                    W("scalar", sem_v, c_epi_sg[(o - 1) % OC])
                c_epi_g[o] = OPA(lambda e, o=o, b=bg:
                                 e.activation(SCRA[:, :], bank(b), AF.Silu, bias=BB[:, o:o + 1],
                                              scale=1.0))
                sgbuf = SGF[:, par * NT:(par + 1) * NT]
                W2("vector", c_s_done)
                W("vector", sem_a, c_epi_g[o])
                c_epi_sg[o] = OPV(lambda e, s=sgbuf, b=bs:
                                  e.tensor_mul(s, bank(b), SCRA[:, :]))
                W("vector", sem_xr[par], tgt_xres)
                if t > 0 or o >= 1:
                    W("vector", sem_out, c_out_dma[(o - 1) % OC])
                c_epi_out[o] = OPV(lambda e, s=sgbuf, x=XRES[:, par * NT:(par + 1) * NT]:
                                   e.tensor_add(OUTB[:, :], s, x))
                W("sync", sem_v, c_epi_out[o])
                cnt_out = COUT.add(16)
                E("sync", lambda e, d=d_out[o * 128:(o + 1) * 128, tc]:
                  e.dma_start(out=d, in_=OUTB[:, :]).then_inc(sem_out, 16))
                c_out_dma[o] = cnt_out

        W("sync", sem_out, COUT.n)

        with nc.Block() as block:
            @block.sync
            def _(eng):
                for fn in prog["sync"]:
                    fn(eng)

            @block.gpsimd
            def _(eng):
                for fn in prog["gpsimd"]:
                    fn(eng)

            @block.vector
            def _(eng):
                for fn in prog["vector"]:
                    fn(eng)

            @block.scalar
            def _(eng):
                for fn in prog["scalar"]:
                    fn(eng)

            @block.tensor
            def _(eng):
                for fn in prog["tensor"]:
                    fn(eng)

    return nc


def _prep_inputs(inputs):
    x = np.ascontiguousarray(np.asarray(inputs["L0_attn_hidden_states"])).reshape(16384, H)
    x_t = np.ascontiguousarray(x.T)

    up_t = np.ascontiguousarray(np.asarray(inputs["L0_attn_up_w"]).T)
    gate_t = np.ascontiguousarray(np.asarray(inputs["L0_attn_gate_w"]).T)
    fg_t = np.asarray(inputs["L0_attn_ffn_gate_w"]).T
    fu_t = np.asarray(inputs["L0_attn_ffn_up_w"]).T
    gb = fg_t.reshape(H, NBAND, 256)
    ub = fu_t.reshape(H, NBAND, 256)
    ffngu = np.ascontiguousarray(np.concatenate([gb, ub], axis=2).reshape(H, 2 * F))

    lao = np.asarray(inputs["L0_attn_local_approx_output"])
    fuse = np.asarray(inputs["L0_attn_fuse_weight"])
    wcat_t = np.concatenate([lao.T[None], fuse.transpose(2, 1, 0)], axis=0).reshape(7 * S, H)
    down_t = np.asarray(inputs["L0_attn_ffn_down_w"]).T
    epiw = np.ascontiguousarray(np.concatenate([wcat_t, down_t], axis=0))

    wbt = np.ascontiguousarray(np.asarray(inputs["L0_ffn_W_base"]).T)

    n = H
    tt = np.arange(n, dtype=np.float64)
    qt = np.zeros((n, n), dtype=np.float64)
    qt[0, :] = 1.0 / np.sqrt(n)
    maxf = (n - 1) // 2
    f_idx = np.arange(1, maxf + 1, dtype=np.float64)
    ang = 2.0 * np.pi * np.outer(f_idx, tt) / n
    qt[1:1 + 2 * maxf:2, :] = np.sqrt(2.0 / n) * np.cos(ang)
    qt[2:2 + 2 * maxf:2, :] = np.sqrt(2.0 / n) * np.sin(ang)
    qt[n - 1, :] = ((-1.0) ** tt) / np.sqrt(n)
    qt = np.ascontiguousarray(qt.astype(ml_dtypes.bfloat16))

    enc = np.ascontiguousarray(np.concatenate(
        [np.asarray(inputs["L0_ffn_enc0"]), np.asarray(inputs["L0_ffn_enc1"])], axis=0))

    co0 = np.ascontiguousarray(
        np.asarray(inputs["L0_ffn_coefs0"]).reshape(3, 32, 128, 32).transpose(2, 0, 1, 3).reshape(128, -1))
    co1 = np.ascontiguousarray(
        np.asarray(inputs["L0_ffn_coefs1"]).reshape(3, 32, 128, 8).transpose(2, 0, 1, 3).reshape(128, -1))
    mk0 = np.ascontiguousarray(
        np.asarray(inputs["L0_ffn_mask0"]).astype(np.uint8).reshape(3, 32, 128, 32).transpose(2, 0, 1, 3).reshape(128, -1))
    mk1 = np.ascontiguousarray(
        np.asarray(inputs["L0_ffn_mask1"]).astype(np.uint8).reshape(3, 32, 128, 8).transpose(2, 0, 1, 3).reshape(128, -1))

    lp = np.ascontiguousarray(np.asarray(inputs["L0_attn_local_point"]).reshape(SC, 128).T)
    bb = np.ascontiguousarray(np.asarray(inputs["L0_ffn_b_base"]).reshape(OC, 128).T)
    disc = np.ascontiguousarray(np.asarray(inputs["L0_attn_discount_factor"]).reshape(1, G))

    shared = dict(up_t=up_t, gate_t=gate_t, ffngu=ffngu, epiw=epiw, wbt=wbt,
                  qt=qt, enc=enc, co0=co0, co1=co1, mk0=mk0, mk1=mk1,
                  lp=lp, bb=bb, disc=disc)
    in_maps = []
    for c in range(N_CORES):
        m = dict(shared)
        m["x_t"] = np.ascontiguousarray(x_t[:, c * TOK_PER_CORE:(c + 1) * TOK_PER_CORE])
        in_maps.append(m)
    return in_maps


def kernel(**inputs):
    in_maps = _prep_inputs(inputs)
    nc = build_program()
    res = run_bass_kernel_spmd(nc, in_maps, core_ids=list(range(N_CORES)))
    out_t = np.concatenate([r["out_t"] for r in res.results], axis=1)
    return np.ascontiguousarray(out_t.T).reshape(4, 4096, 4096)


# version 04
# speedup vs baseline: 1.0277x; 1.0050x over previous
import contextlib
import numpy as np
import ml_dtypes
import concourse.bass as bass
import concourse.mybir as mybir
from concourse.bass_utils import run_bass_kernel_spmd

F32 = mybir.dt.float32
BF16 = mybir.dt.bfloat16
U8 = mybir.dt.uint8
AF = mybir.ActivationFunctionType
ALU = mybir.AluOpType

N_CORES = 8
TOK_PER_CORE = 2048
NT = 512
N_TILES = TOK_PER_CORE // NT
H = 4096
KC = H // 128
S = 1024
SC = S // 128
F = 9984
FC = F // 128
NBAND = F // 256
OC = KC
G = 6
EPI_ROWS = S * 7 + F
EPI_KT = EPI_ROWS // 128
THD = 2.5
EPS = 1e-5


class Ctr:
    __slots__ = ("n",)
    def __init__(self):
        self.n = 0
    def add(self, k=1):
        self.n += k
        return self.n


def build_program():
    nc = bass.Bass("TRN2", target_bir_lowering=False, debug=False)

    d_x = nc.dram_tensor("x_t", [H, TOK_PER_CORE], F32, kind="ExternalInput").ap()
    d_up = nc.dram_tensor("up_t", [H, S], F32, kind="ExternalInput").ap()
    d_gate = nc.dram_tensor("gate_t", [H, S], F32, kind="ExternalInput").ap()
    d_ffngu = nc.dram_tensor("ffngu", [H, 2 * F], F32, kind="ExternalInput").ap()
    d_epiw = nc.dram_tensor("epiw", [EPI_ROWS, H], F32, kind="ExternalInput").ap()
    d_wbt = nc.dram_tensor("wbt", [H, H], F32, kind="ExternalInput").ap()
    d_qt = nc.dram_tensor("qt", [H, H], BF16, kind="ExternalInput").ap()
    d_enc = nc.dram_tensor("enc", [40, H], F32, kind="ExternalInput").ap()
    d_co0 = nc.dram_tensor("co0", [128, 3 * 32 * 32], F32, kind="ExternalInput").ap()
    d_co1 = nc.dram_tensor("co1", [128, 3 * 32 * 8], F32, kind="ExternalInput").ap()
    d_mk0 = nc.dram_tensor("mk0", [128, 3 * 32 * 32], U8, kind="ExternalInput").ap()
    d_mk1 = nc.dram_tensor("mk1", [128, 3 * 32 * 8], U8, kind="ExternalInput").ap()
    d_lp = nc.dram_tensor("lp", [128, SC], F32, kind="ExternalInput").ap()
    d_bb = nc.dram_tensor("bb", [128, OC], F32, kind="ExternalInput").ap()
    d_disc = nc.dram_tensor("disc", [1, G], F32, kind="ExternalInput").ap()
    d_out = nc.dram_tensor("out_t", [H, TOK_PER_CORE], F32, kind="ExternalOutput").ap()
    d_weff = nc.dram_tensor("weff_scr", [H, H], BF16, kind="Internal").ap()
    NGROUPS = 2 * 16 + NBAND * 8 + OC * 9
    NGH = (NGROUPS + 1) // 2
    d_blob0 = nc.dram_tensor("wblob0", [NGH * 128, 2048], BF16, kind="Internal").ap()
    d_blob1 = nc.dram_tensor("wblob1", [NGH * 128, 2048], BF16, kind="Internal").ap()

    v_x = d_x.rearrange("(k p) t -> p k t", p=128)
    v_up = d_up.rearrange("(k p) s -> p k s", p=128)
    v_gate = d_gate.rearrange("(k p) s -> p k s", p=128)
    v_ffngu = d_ffngu.rearrange("(k p) s -> p k s", p=128)
    v_epiw = d_epiw.rearrange("(k p) h -> p k h", p=128)
    v_qt = d_qt.rearrange("(k p) t -> p k t", p=128)
    v_weff = d_weff.rearrange("(k p) t -> p k t", p=128)
    v_blob0 = d_blob0.rearrange("(g p) c -> p g c", p=128)
    v_blob1 = d_blob1.rearrange("(g p) c -> p g c", p=128)

    def blob_view(g, ncols):
        if g < NGH:
            return v_blob0[:, g, 0:ncols]
        return v_blob1[:, g - NGH, 0:ncols]

    prog = {e: [] for e in ("sync", "gpsimd", "vector", "scalar", "tensor")}

    def E(eng, fn):
        prog[eng].append(fn)

    with contextlib.ExitStack() as es:
        ec = es.enter_context

        sem_pe = ec(nc.semaphore("sem_pe"))
        sem_rc = ec(nc.semaphore("sem_rc"))
        sem_a = ec(nc.semaphore("sem_a"))
        sem_v = ec(nc.semaphore("sem_v"))
        sem_bar = ec(nc.semaphore("sem_bar"))
        sem_out = ec(nc.semaphore("sem_out"))
        sem_r = [ec(nc.semaphore(f"sem_r{i}")) for i in range(3)]
        sem_xs = [ec(nc.semaphore(f"sem_xs{i}")) for i in range(2)]
        sem_wb = [ec(nc.semaphore(f"sem_wb{i}")) for i in range(2)]
        sem_ws = [ec(nc.semaphore(f"sem_ws{i}")) for i in range(2)]
        sem_xr = [ec(nc.semaphore(f"sem_xr{i}")) for i in range(2)]
        sem_web = ec(nc.semaphore("sem_web"))
        sem_bs = [ec(nc.semaphore(f"sem_bs{i}")) for i in range(3)]
        sem_ld = ec(nc.semaphore("sem_ld"))

        COUT, CPE, CRC, CA, CV = (Ctr() for _ in range(5))
        CR = [Ctr() for _ in range(3)]
        CXS = [Ctr() for _ in range(2)]
        CWB = [Ctr() for _ in range(2)]
        CWS = [Ctr() for _ in range(2)]
        CXR = [Ctr() for _ in range(2)]
        CWEB = Ctr()
        CBS = [Ctr() for _ in range(3)]
        CLD = Ctr()

        def W(eng, sem, n):
            if n > 0:
                E(eng, lambda e, s=sem, n=n: e.wait_ge(s, n))

        def W2(eng, sw):
            sem, n = sw
            W(eng, sem, n)

        def OPV(fn):
            if CV.n:
                E("vector", lambda e, n=CV.n: e.wait_ge(sem_v, n))
            E("vector", lambda e, f=fn: f(e).then_inc(sem_v, 1))
            return CV.add(1)

        def OPA(fn):
            if CA.n:
                E("scalar", lambda e, n=CA.n: e.wait_ge(sem_a, n))
            E("scalar", lambda e, f=fn: f(e).then_inc(sem_a, 1))
            return CA.add(1)

        RING = ec(nc.sbuf_tensor("RING", [128, 3 * 2048], BF16))
        LP = ec(nc.sbuf_tensor("LP", [128, SC], F32))
        BB = ec(nc.sbuf_tensor("BB", [128, OC], F32))
        EMC = ec(nc.sbuf_tensor("EMC", [128, 8], F32))
        ONESB = ec(nc.sbuf_tensor("ONESB", [128, 1], BF16))
        ONES1 = ec(nc.sbuf_tensor("ONES1", [1, 128], F32))
        CST = ec(nc.sbuf_tensor("CST", [1, 8], F32))
        CSTC = ec(nc.sbuf_tensor("CSTC", [128, 2], F32))
        PB = ec(nc.psum_tensor("PB", [128, 8 * NT], F32))

        def bank(j):
            return PB[:, j * NT:(j + 1) * NT]

        ring_idx = Ctr()
        ring_cons_at = {}
        ring_store_at = {}
        blob_cursor = Ctr()

        def ring_load(src_ap, policy="setup"):
            i = ring_idx.n
            ring_idx.add()
            sl = i % 3
            slot = RING[:, sl * 2048:(sl + 1) * 2048]
            if i >= 3:
                W("gpsimd", sem_rc, ring_cons_at[i - 3])
                if (i - 3) in ring_store_at:
                    W2("gpsimd", ring_store_at[i - 3])
            if policy == "setup":
                mode = "direct"
            else:
                g = blob_cursor.n
                blob_cursor.add()
                if policy == "t0":
                    mode = "save" if g < NGH else "direct"
                elif policy == "t1":
                    mode = "blob" if g < NGH else "save"
                else:
                    mode = "blob"
            ncols = int(np.prod(src_ap.shape[1:]))
            if mode == "blob":
                src_ap = blob_view(g, ncols)
            dst = slot[:, 0:ncols]
            if len(src_ap.shape) == 3:
                dst = dst.rearrange("p (a b) -> p a b", a=src_ap.shape[1])
            tgt = CR[sl].add(16)
            E("gpsimd", lambda e, s=src_ap, d=dst, m=sem_r[sl]:
              e.dma_start(out=d, in_=s).then_inc(m, 16))
            if mode == "save":
                W("sync", sem_r[sl], tgt)
                stgt = CBS[sl].add(16)
                E("sync", lambda e, s=slot[:, 0:ncols], d=blob_view(g, ncols), m=sem_bs[sl]:
                  e.dma_start(out=d, in_=s).then_inc(m, 16))
                ring_store_at[i] = (sem_bs[sl], stgt)
            return slot, i, (sem_r[sl], tgt)

        def _chain(inst, incs):
            for s_, v_ in incs:
                inst = inst.then_inc(s_, v_)
            return inst

        def TMM(fn, incs=()):
            if incs:
                E("tensor", lambda e, f=fn, iv=tuple(incs): _chain(f(e), iv))
            else:
                E("tensor", fn)

        def ring_mark(i):
            ring_cons_at[i] = CRC.add(1)
            return (sem_rc, 1)

        with contextlib.ExitStack() as ses:
            sc_ = ses.enter_context
            CO0 = sc_(nc.sbuf_tensor("CO0", [128, 3 * 32 * 32], F32))
            CO1 = sc_(nc.sbuf_tensor("CO1", [128, 3 * 32 * 8], F32))
            MK0 = sc_(nc.sbuf_tensor("MK0", [128, 3 * 32 * 32], U8))
            MK1 = sc_(nc.sbuf_tensor("MK1", [128, 3 * 32 * 8], U8))
            MKF = sc_(nc.sbuf_tensor("MKF", [128, 3 * 32 * 32], F32))
            W0S = sc_(nc.sbuf_tensor("W0S", [128, 32 * 32], F32))
            W1S = sc_(nc.sbuf_tensor("W1S", [128, 32 * 8], F32))
            W01B = sc_(nc.sbuf_tensor("W01B", [128, 32 * 40], BF16))
            ENCF = sc_(nc.sbuf_tensor("ENCF", [40, H], F32))
            ENCB = sc_(nc.sbuf_tensor("ENCB", [40, H], BF16))
            VCB = sc_(nc.sbuf_tensor("VCB", [40, H], BF16))
            WBB = sc_(nc.sbuf_tensor("WBB", [128, 2 * H], F32))
            WEO = sc_(nc.sbuf_tensor("WEO", [128, 2 * H], BF16))
            DRW = sc_(nc.sbuf_tensor("DRW", [1, 16], F32))

            for src, dst in ((d_co0, CO0[:, :]), (d_co1, CO1[:, :]),
                             (d_mk0, MK0[:, :]), (d_mk1, MK1[:, :]),
                             (d_enc, ENCF[:, :]), (d_lp, LP[:, :]),
                             (d_bb, BB[:, :]), (d_disc, DRW[:, 0:G])):
                CLD.add(16)
                E("sync", lambda e, s=src, d=dst: e.dma_start(out=d, in_=s).then_inc(sem_ld, 16))
            c_small = CLD.n

            OPV(lambda e: e.memset(ONESB[:, :], 1.0))
            OPV(lambda e: e.memset(ONES1[:, :], 1.0))
            OPV(lambda e: e.memset(CST[:, 0:1], EPS))
            OPV(lambda e: e.memset(CST[:, 1:2], -0.5))
            OPV(lambda e: e.memset(CST[:, 2:3], -1.0))
            OPV(lambda e: e.memset(CST[:, 3:4], 0.5))
            c_consts = OPV(lambda e: e.memset(CSTC[:, 0:1], 0.5))

            W("vector", sem_ld, c_small)
            OPV(lambda e: e.tensor_copy(MKF[:, :], MK0[:, :]))
            for e_i in range(3):
                sl = slice(e_i * 1024, (e_i + 1) * 1024)
                if e_i == 0:
                    OPV(lambda e, s=sl: e.tensor_mul(W0S[:, :], CO0[:, s], MKF[:, s]))
                else:
                    OPV(lambda e, s=sl: e.tensor_mul(MKF[:, s], CO0[:, s], MKF[:, s]))
                    OPV(lambda e, s=sl: e.tensor_add(W0S[:, :], W0S[:, :], MKF[:, s]))
            OPV(lambda e: e.tensor_copy(MKF[:, 0:3 * 256], MK1[:, :]))
            for e_i in range(3):
                sl = slice(e_i * 256, (e_i + 1) * 256)
                if e_i == 0:
                    OPV(lambda e, s=sl: e.tensor_mul(W1S[:, :], CO1[:, s], MKF[:, s]))
                else:
                    OPV(lambda e, s=sl: e.tensor_mul(MKF[:, s], CO1[:, s], MKF[:, s]))
                    OPV(lambda e, s=sl: e.tensor_add(W1S[:, :], W1S[:, :], MKF[:, s]))
            OPV(lambda e: e.tensor_copy(
                W01B.rearrange("p (c w) -> p c w", w=40)[:, :, 0:32],
                W0S.rearrange("p (c r) -> p c r", r=32)))
            OPV(lambda e: e.tensor_copy(
                W01B.rearrange("p (c w) -> p c w", w=40)[:, :, 32:40],
                W1S.rearrange("p (c r) -> p c r", r=8)))
            c_w01 = OPV(lambda e: e.tensor_copy(ENCB[:, :], ENCF[:, :]))

            W("scalar", sem_ld, c_small)
            W("scalar", sem_v, c_consts)
            c_emc_row = OPA(lambda e: e.activation(DRW[:, 8:8 + G], DRW[:, 0:G], AF.Exp,
                                                   bias=0.0, scale=CST[0:1, 2:3]))
            W("tensor", sem_a, c_emc_row)
            E("tensor", lambda e: e.matmul(PB[:, 7 * NT:7 * NT + G], ONES1[:, :], DRW[:, 8:8 + G],
                                           start=True, stop=True).then_inc(sem_pe, 1))
            c_emc_mm = CPE.add(1)
            W("vector", sem_pe, c_emc_mm)
            c_emc = OPV(lambda e: e.tensor_copy(EMC[:, 0:G], PB[:, 7 * NT:7 * NT + G]))

            c_vcb_prev = 0
            for hh in range(8):
                t0 = hh * NT
                for og in range(8):
                    src = v_qt[:, og * 4:(og + 1) * 4, t0:t0 + NT]
                    slot, fi, wcnt = ring_load(src)
                    W2("tensor", wcnt)
                    if hh == 0 and og == 0:
                        W("tensor", sem_v, c_w01)
                    if og == 0 and hh > 0:
                        W("tensor", sem_a, c_vcb_prev)
                    for oc in range(4):
                        c = og * 4 + oc
                        lhsT = W01B[:, c * 40:(c + 1) * 40]
                        rhs = slot[:, oc * NT:(oc + 1) * NT]
                        incs = []
                        if oc == 3:
                            incs.append(ring_mark(fi))
                        TMM(lambda e, l=lhsT, r=rhs, st=(c == 0), sp=(c == 31):
                            e.matmul(PB[0:40, 6 * NT:7 * NT], l, r, start=st, stop=sp), incs)
                c_vc_mm = (sem_rc, CRC.n)
                W2("scalar", c_vc_mm)
                c_vcb_prev = OPA(lambda e, t0=t0: e.copy(VCB[0:40, t0:t0 + NT],
                                                         PB[0:40, 6 * NT:7 * NT]))

            c_wb_load = {}
            c_wb_stt = {}
            c_stt_at = {}
            c_ws_store = {}

            def emit_wb_load(dc):
                par = dc % 2
                buf = WBB[:, par * H:par * H + H]
                if dc >= 2:
                    W("sync", sem_v, c_wb_stt[dc - 2])
                tgt = CWB[par].add(16)
                E("sync", lambda e, d=buf, s=d_wbt[dc * 128:(dc + 1) * 128, :], m=sem_wb[par]:
                  e.dma_start(out=d, in_=s).then_inc(m, 16))
                c_wb_load[dc] = (sem_wb[par], tgt)

            emit_wb_load(0)
            emit_wb_load(1)
            for dc in range(KC):
                par = dc % 2
                buf = WBB[:, par * H:par * H + H]
                obuf = WEO[:, par * H:par * H + H]
                for hh in range(8):
                    idx = dc * 8 + hh
                    pb_i = 4 + (idx % 2)
                    lhsT = ENCB[:, dc * 128:(dc + 1) * 128]
                    rhs = VCB[:, hh * NT:(hh + 1) * NT]
                    if idx >= 2:
                        W("tensor", sem_v, c_stt_at[idx - 2])
                    if idx == 0:
                        W("tensor", sem_a, c_vcb_prev)
                    E("tensor", lambda e, l=lhsT, r=rhs, b=pb_i:
                      e.matmul(bank(b), l, r, start=True, stop=True).then_inc(sem_pe, 1))
                    c_mm = CPE.add(1)
                    W("vector", sem_pe, c_mm)
                    if hh == 0:
                        W2("vector", c_wb_load[dc])
                        if dc >= 2:
                            W2("vector", c_ws_store[dc - 2])
                    c_stt_at[idx] = OPV(lambda e, o=obuf[:, hh * NT:(hh + 1) * NT], b=pb_i,
                                        w=buf[:, hh * NT:(hh + 1) * NT]:
                                        e.scalar_tensor_tensor(o, bank(b), CSTC[:, 0:1], w,
                                                               ALU.mult, ALU.add))
                c_wb_stt[dc] = c_stt_at[dc * 8 + 7]
                W("sync", sem_v, c_wb_stt[dc])
                tgt = CWS[par].add(16)
                E("sync", lambda e, o=d_weff[dc * 128:(dc + 1) * 128, :], s=obuf, m=sem_ws[par]:
                  e.dma_start(out=o, in_=s).then_inc(m, 16))
                c_ws_store[dc] = (sem_ws[par], tgt)
                if dc + 2 < KC:
                    emit_wb_load(dc + 2)

            for eng in ("sync", "gpsimd", "vector", "scalar", "tensor"):
                W(eng, sem_v, CV.n)
                W(eng, sem_a, CA.n)
                W(eng, sem_pe, CPE.n)
                W(eng, sem_rc, CRC.n)
                W(eng, sem_ld, CLD.n)
                W(eng, sem_ws[0], CWS[0].n)
                W(eng, sem_ws[1], CWS[1].n)
                W(eng, sem_wb[0], CWB[0].n)
                W(eng, sem_wb[1], CWB[1].n)
                for i_ in range(3):
                    W(eng, sem_r[i_], CR[i_].n)

        XN = ec(nc.sbuf_tensor("XN", [128, KC * NT], BF16))
        FEAT = ec(nc.sbuf_tensor("FEAT", [128, 56 * NT], BF16))
        WINT = ec(nc.sbuf_tensor("WINT", [128, FC * NT], BF16))
        WEB = ec(nc.sbuf_tensor("WEB", [128, KC * 128], BF16))
        XRES = ec(nc.sbuf_tensor("XRES", [128, 2 * NT], F32))
        OUTB = ec(nc.sbuf_tensor("OUTB", [128, NT], F32))
        SCRA = ec(nc.sbuf_tensor("SCRA", [128, NT], F32))
        SCRB = ec(nc.sbuf_tensor("SCRB", [128, NT], F32))
        SGF = ec(nc.sbuf_tensor("SGF", [128, 2 * NT], F32))
        ROWS = ec(nc.sbuf_tensor("ROWS", [1, 3 * NT], F32))

        DELTA = WINT[:, 0:8 * NT * 2].bitcast(F32)
        PW = WINT[:, 8 * NT * 2:16 * NT * 2].bitcast(F32)
        XSTAT = FEAT[:, 40 * NT:56 * NT].bitcast(F32)
        RBB = WEB[:, 0:2 * NT].bitcast(F32)
        MRB = WEB[:, 2 * NT:4 * NT].bitcast(F32)
        XBF = FEAT[:, 32 * NT:34 * NT]
        X2BF = FEAT[:, 34 * NT:36 * NT]

        def feat(kc):
            return FEAT[:, kc * NT:(kc + 1) * NT]

        def wint(kc):
            return WINT[:, kc * NT:(kc + 1) * NT]

        def xnc(k):
            return XN[:, k * NT:(k + 1) * NT]

        def delta_j(j):
            return DELTA[:, j * NT:(j + 1) * NT]

        def pw_j(j):
            return PW[:, j * NT:(j + 1) * NT]

        xstat_cons = {}
        c_mmst = {}
        c_epi_sg = {}
        c_epi_g = {}
        c_epi_out = {}
        c_out_dma = {}
        c_web_done = 0
        c_ffn_done_prev = 0
        c_xn_done = 0
        c_bcast = 0

        RA = ROWS[0:1, 0:NT]
        RB = ROWS[0:1, NT:2 * NT]
        RC = ROWS[0:1, 2 * NT:3 * NT]

        for t in range(N_TILES):
            tc = slice(t * NT, (t + 1) * NT)
            c_pe_prev_tile = CPE.n
            c_rc_prev_tile = CRC.n
            rmode = ("t0", "t1", "tn", "tn")[t]
            blob_cursor.n = 0
            if t >= 1:
                for i_ in range(3):
                    W("gpsimd", sem_bs[i_], CBS[i_].n)

            c_xload = {}
            c_xn_at = {}

            def emit_ln_load(key, t=t, tc=tc):
                gk = t * 16 + key
                slot = key % 2
                if gk >= 2 and (gk - 2) in xstat_cons:
                    for sem_o, cnt_o in xstat_cons[gk - 2]:
                        W("sync", sem_o, cnt_o)
                if key == 0 and t > 0:
                    W("sync", sem_pe, c_pe_prev_tile)
                    W("sync", sem_rc, c_rc_prev_tile)
                g4 = key % 8
                src = v_x[:, g4 * 4:(g4 + 1) * 4, tc]
                dst = XSTAT[:, slot * 4 * NT:(slot + 1) * 4 * NT].rearrange(
                    "p (a b) -> p a b", a=4)
                tgt = CXS[slot].add(16)
                E("sync", lambda e, s=src, d=dst, m=sem_xs[slot]:
                  e.dma_start(out=d, in_=s).then_inc(m, 16))
                c_xload[key] = (sem_xs[slot], tgt)

            def emit_stats_group(g4, t=t):
                slot = g4 % 2
                for kk in range(4):
                    k = g4 * 4 + kk
                    xv = XSTAT[:, (slot * 4 + kk) * NT:(slot * 4 + kk + 1) * NT]
                    xb = XBF[:, (k % 2) * NT:(k % 2 + 1) * NT]
                    x2 = X2BF[:, (k % 2) * NT:(k % 2 + 1) * NT]
                    W2("vector", c_xload[g4])
                    W2("scalar", c_xload[g4])
                    if k >= 2 or t > 0:
                        kk2 = (k - 2) % KC
                        W("vector", sem_pe, c_mmst[kk2])
                        W("scalar", sem_pe, c_mmst[kk2])
                    if k == 0 and t > 0:
                        W("vector", sem_pe, c_pe_prev_tile)
                        W("scalar", sem_pe, c_pe_prev_tile)
                        W("vector", sem_rc, c_rc_prev_tile)
                        W("scalar", sem_rc, c_rc_prev_tile)
                    c_cast = OPV(lambda e, a=xb, b=xv: e.tensor_copy(a, b))
                    c_sq = OPA(lambda e, a=x2, b=xv: e.activation(a, b, AF.Square))
                    W("tensor", sem_v, c_cast)
                    W("tensor", sem_a, c_sq)
                    if k == 0 and t > 0:
                        W("tensor", sem_v, c_ffn_done_prev)
                    st = (k == 0)
                    sp = (k == KC - 1)
                    E("tensor", lambda e, a=xb, st=st, sp=sp:
                      e.matmul(PB[0:1, 7 * NT:8 * NT], ONESB[:, :], a, start=st, stop=sp))
                    E("tensor", lambda e, a=x2, st=st, sp=sp:
                      e.matmul(PB[0:1, 4 * NT:5 * NT], ONESB[:, :], a, start=st, stop=sp).then_inc(sem_pe, 1))
                    c_mmst[k] = CPE.add(1)
                xstat_cons[t * 16 + g4] = ((sem_a, CA.n), (sem_v, CV.n))

            def emit_xn_group(g4, t=t):
                slot = g4 % 2
                for kk in range(4):
                    k = g4 * 4 + kk
                    xv = XSTAT[:, (slot * 4 + kk) * NT:(slot * 4 + kk + 1) * NT]
                    W2("vector", c_xload[8 + g4])
                    if k == 0:
                        W("vector", sem_a, c_rbb)
                    OPV(lambda e, xv=xv: e.tensor_mul(SCRB[:, :], xv, RBB[:, :]))
                    c_xn_at[k] = OPV(lambda e, k=k: e.tensor_sub(xnc(k), SCRB[:, :], MRB[:, :]))
                xstat_cons[t * 16 + 8 + g4] = ((sem_v, CV.n),)

            emit_ln_load(0)
            emit_ln_load(1)
            for g4 in range(8):
                emit_stats_group(g4)
                if g4 + 2 <= 7:
                    emit_ln_load(g4 + 2)
            c_stats_mm = c_mmst[KC - 1]

            W("vector", sem_pe, c_stats_mm)
            OPV(lambda e: e.tensor_scalar_mul(RA, PB[0:1, 7 * NT:8 * NT], 1.0 / H))
            OPV(lambda e: e.tensor_scalar_mul(RB, PB[0:1, 4 * NT:5 * NT], 1.0 / H))
            OPV(lambda e: e.tensor_mul(RC, RA, RA))
            c_var = OPV(lambda e: e.tensor_sub(RB, RB, RC))
            W("scalar", sem_v, c_var)
            c_ln = OPA(lambda e: e.activation(RC, RB, AF.Ln, bias=CST[0:1, 0:1], scale=1.0))
            c_rstd = OPA(lambda e: e.activation(RB, RC, AF.Exp, bias=0.0,
                                                scale=CST[0:1, 1:2]))
            W("vector", sem_a, c_rstd)
            c_mr = OPV(lambda e: e.tensor_mul(RC, RA, RB))

            W("tensor", sem_v, c_mr)
            if t > 0:
                W("tensor", sem_v, c_ffn_done_prev)
            if t > 0:
                W("tensor", sem_pe, c_pe_prev_tile)
            E("tensor", lambda e: e.matmul(PB[:, 5 * NT:6 * NT], ONES1[:, :], RB, start=True, stop=True))
            E("tensor", lambda e: e.matmul(PB[:, 6 * NT:7 * NT], ONES1[:, :], RC,
                                           start=True, stop=True).then_inc(sem_pe, 1))
            c_bcast = CPE.add(1)
            W("scalar", sem_pe, c_bcast)
            if t > 0:
                W("scalar", sem_pe, c_pe_prev_tile)
            OPA(lambda e: e.copy(RBB[:, :], PB[:, 5 * NT:6 * NT]))
            c_rbb = OPA(lambda e: e.copy(MRB[:, :], PB[:, 6 * NT:7 * NT]))

            emit_ln_load(8)
            emit_ln_load(9)
            for g4 in range(8):
                emit_xn_group(g4)
                if g4 + 2 <= 7:
                    emit_ln_load(8 + g4 + 2)
            c_xn_done = CV.n

            c_p1_evac_u = {}
            c_p1_delta = {}
            for sweep in range(2):
                vsrc = v_up if sweep == 0 else v_gate
                for g2 in range(16):
                    src = vsrc[:, g2 * 2:(g2 + 1) * 2, :]
                    slot, fi, wcnt = ring_load(src, rmode)
                    W2("tensor", wcnt)
                    W("tensor", sem_v, c_xn_at[2 * g2 + 1])
                    if g2 == 0:
                        if sweep == 0:
                            W("tensor", sem_a, c_rbb)
                            if t > 0:
                                W("tensor", sem_v, c_epi_sg[31])
                        if sweep == 1:
                            W("tensor", sem_a, c_p1_evac_u[SC - 1])
                    for kk in range(2):
                        k = g2 * 2 + kk
                        for j in range(SC):
                            lhsT = slot[:, kk * S + j * 128:kk * S + (j + 1) * 128]
                            incs = []
                            if kk == 1 and j == SC - 1:
                                incs.append(ring_mark(fi))
                            TMM(lambda e, l=lhsT, j=j, k=k, st=(k == 0), sp=(k == KC - 1):
                                e.matmul(bank(j), l, xnc(k), start=st, stop=sp), incs)
                c_sweep_done = (sem_rc, CRC.n)
                for j in range(SC):
                    if sweep == 0:
                        W2("scalar", c_sweep_done)
                        c_p1_evac_u[j] = OPA(lambda e, j=j: e.copy(feat(j), bank(j)))
                    else:
                        W2("vector", c_sweep_done)
                        c_p1_delta[j] = OPV(lambda e, j=j: e.tensor_scalar(
                            delta_j(j), bank(j), LP[:, j:j + 1], None, ALU.subtract))

            def emit_fg(j):
                dj = delta_j(j)
                pj = pw_j(j)
                uj = feat(j)
                W("vector", sem_v, c_p1_delta[j])
                W("vector", sem_a, c_p1_evac_u[j])
                for m in range(1, 7):
                    if m == 2:
                        OPV(lambda e, d=dj, p=pj: e.tensor_mul(p, d, d))
                    elif m >= 3:
                        OPV(lambda e, d=dj, p=pj: e.tensor_mul(p, p, d))
                    if m == 5:
                        W("scalar", sem_v, CV.n)
                        c_abs = OPA(lambda e, d=dj: e.activation(SCRA[:, :], d, AF.Abs))
                        W("vector", sem_a, c_abs)
                        OPV(lambda e: e.tensor_scalar(SCRB[:, :], SCRA[:, :], THD, None, ALU.is_le))
                        OPV(lambda e, p=pj: e.tensor_mul(p, p, SCRB[:, :]))
                    fsrc = dj if m == 1 else pj
                    fdst = feat(m * 8 + j)
                    OPV(lambda e, s=fsrc, d=fdst, u=uj, m=m:
                        e.scalar_tensor_tensor(d, s, EMC[:, m - 1:m], u, ALU.mult, ALU.mult))

            c_ffn_evac = {}
            for b in range(NBAND):
                par = b % 2
                bk = [4 * par + q for q in range(4)]
                for g4 in range(8):
                    src = v_ffngu[:, g4 * 4:(g4 + 1) * 4, b * 512:(b + 1) * 512]
                    slot, fi, wcnt = ring_load(src, rmode)
                    W2("tensor", wcnt)
                    if g4 == 0:
                        if b == 0:
                            W("tensor", sem_v, c_p1_delta[3])
                        if b == 1:
                            W("tensor", sem_v, c_p1_delta[7])
                        if b >= 2:
                            W("tensor", sem_v, c_ffn_evac[b - 2])
                    for kk in range(4):
                        k = g4 * 4 + kk
                        for q in range(4):
                            lhsT = slot[:, kk * 512 + q * 128:kk * 512 + (q + 1) * 128]
                            incs = []
                            if kk == 3 and q == 3:
                                incs.append(ring_mark(fi))
                            TMM(lambda e, l=lhsT, bq=bk[q], k=k, st=(k == 0), sp=(k == KC - 1):
                                e.matmul(bank(bq), l, xnc(k), start=st, stop=sp), incs)
                c_band_done = (sem_rc, CRC.n)
                if b < SC:
                    emit_fg(b)
                for q in range(2):
                    fchunk = 2 * b + q
                    sg = SGF[:, q * NT:(q + 1) * NT]
                    W2("scalar", c_band_done)
                    if b >= 1:
                        W("scalar", sem_v, c_ffn_evac[b - 1])
                    c_silu = OPA(lambda e, s=sg, bq=bk[q]:
                                 e.activation(s, bank(bq), AF.Silu))
                    W("vector", sem_a, c_silu)
                    c_ffn_evac[b] = OPV(lambda e, s=sg, f=fchunk, bq=bk[2 + q]:
                                        e.tensor_mul(wint(f), s, bank(bq)))
            c_ffn_done = c_ffn_evac[NBAND - 1]
            c_ffn_done_prev = c_ffn_done

            ngr = (EPI_KT + 15) // 16
            for o in range(OC):
                par = o % 2
                bs = par
                bg = 2 + par
                gslots = []
                for gi in range(min(3, ngr)):
                    k0 = gi * 16
                    kn = min(16, EPI_KT - k0)
                    gslots.append(ring_load(v_epiw[:, k0:k0 + kn, o * 128:(o + 1) * 128], rmode) + (kn,))
                if t > 0 or o >= 1:
                    W("sync", sem_pe, c_web_done)
                if o == 0:
                    W("sync", sem_v, c_xn_done)
                tgt_weff = CWEB.add(16)
                E("sync", lambda e, s=v_weff[:, :, o * 128:(o + 1) * 128],
                  d=WEB.rearrange("p (k c) -> p k c", c=128):
                  e.dma_start(out=d, in_=s).then_inc(sem_web, 16))
                if t > 0 or o >= 2:
                    W("sync", sem_v, c_epi_out[(o - 2) % OC])
                tgt_xres = CXR[par].add(16)
                E("sync", lambda e, s=d_x[o * 128:(o + 1) * 128, tc],
                  d=XRES[:, par * NT:(par + 1) * NT], m=sem_xr[par]:
                  e.dma_start(out=d, in_=s).then_inc(m, 16))

                if o == 0:
                    W("tensor", sem_v, c_ffn_done)
                if t > 0 or o >= 2:
                    W("tensor", sem_v, c_epi_sg[(o - 2) % OC])
                kt = 0
                for gi in range(ngr):
                    slot, fi, wcnt, kn = gslots[gi]
                    W2("tensor", wcnt)
                    for kk in range(kn):
                        lhsT = slot[:, kk * 128:(kk + 1) * 128]
                        rhs = feat(kt) if kt < 56 else wint(kt - 56)
                        incs = []
                        if kk == kn - 1:
                            incs.append(ring_mark(fi))
                        TMM(lambda e, l=lhsT, r=rhs, st=(kt == 0), sp=(kt == EPI_KT - 1), b=bs:
                            e.matmul(bank(b), l, r, start=st, stop=sp), incs)
                        kt += 1
                    if gi + 3 < ngr:
                        k0 = (gi + 3) * 16
                        kn2 = min(16, EPI_KT - k0)
                        gslots.append(ring_load(v_epiw[:, k0:k0 + kn2, o * 128:(o + 1) * 128], rmode) + (kn2,))
                c_s_done = (sem_rc, CRC.n)

                W("tensor", sem_web, tgt_weff)
                if t > 0 or o >= 2:
                    W("tensor", sem_a, c_epi_g[(o - 2) % OC])
                for k in range(KC):
                    lhsT = WEB[:, k * 128:(k + 1) * 128]
                    incs = []
                    if k == KC - 1:
                        incs.append((sem_pe, 1))
                    TMM(lambda e, l=lhsT, k=k, st=(k == 0), sp=(k == KC - 1), b=bg:
                        e.matmul(bank(b), l, xnc(k), start=st, stop=sp), incs)
                c_g_done = CPE.add(1)
                c_web_done = c_g_done

                W("scalar", sem_pe, c_g_done)
                if t > 0 or o >= 1:
                    W("scalar", sem_v, c_epi_sg[(o - 1) % OC])
                c_epi_g[o] = OPA(lambda e, o=o, b=bg:
                                 e.activation(SCRA[:, :], bank(b), AF.Silu, bias=BB[:, o:o + 1],
                                              scale=1.0))
                sgbuf = SGF[:, par * NT:(par + 1) * NT]
                W2("vector", c_s_done)
                W("vector", sem_a, c_epi_g[o])
                c_epi_sg[o] = OPV(lambda e, s=sgbuf, b=bs:
                                  e.tensor_mul(s, bank(b), SCRA[:, :]))
                W("vector", sem_xr[par], tgt_xres)
                if t > 0 or o >= 1:
                    W("vector", sem_out, c_out_dma[(o - 1) % OC])
                c_epi_out[o] = OPV(lambda e, s=sgbuf, x=XRES[:, par * NT:(par + 1) * NT]:
                                   e.tensor_add(OUTB[:, :], s, x))
                W("sync", sem_v, c_epi_out[o])
                cnt_out = COUT.add(16)
                E("sync", lambda e, d=d_out[o * 128:(o + 1) * 128, tc]:
                  e.dma_start(out=d, in_=OUTB[:, :]).then_inc(sem_out, 16))
                c_out_dma[o] = cnt_out

        W("sync", sem_out, COUT.n)

        with nc.Block() as block:
            @block.sync
            def _(eng):
                for fn in prog["sync"]:
                    fn(eng)

            @block.gpsimd
            def _(eng):
                for fn in prog["gpsimd"]:
                    fn(eng)

            @block.vector
            def _(eng):
                for fn in prog["vector"]:
                    fn(eng)

            @block.scalar
            def _(eng):
                for fn in prog["scalar"]:
                    fn(eng)

            @block.tensor
            def _(eng):
                for fn in prog["tensor"]:
                    fn(eng)

    return nc


def _prep_inputs(inputs):
    x = np.ascontiguousarray(np.asarray(inputs["L0_attn_hidden_states"])).reshape(16384, H)
    x_t = np.ascontiguousarray(x.T)

    up_t = np.ascontiguousarray(np.asarray(inputs["L0_attn_up_w"]).T)
    gate_t = np.ascontiguousarray(np.asarray(inputs["L0_attn_gate_w"]).T)
    fg_t = np.asarray(inputs["L0_attn_ffn_gate_w"]).T
    fu_t = np.asarray(inputs["L0_attn_ffn_up_w"]).T
    gb = fg_t.reshape(H, NBAND, 256)
    ub = fu_t.reshape(H, NBAND, 256)
    ffngu = np.ascontiguousarray(np.concatenate([gb, ub], axis=2).reshape(H, 2 * F))

    lao = np.asarray(inputs["L0_attn_local_approx_output"])
    fuse = np.asarray(inputs["L0_attn_fuse_weight"])
    wcat_t = np.concatenate([lao.T[None], fuse.transpose(2, 1, 0)], axis=0).reshape(7 * S, H)
    down_t = np.asarray(inputs["L0_attn_ffn_down_w"]).T
    epiw = np.ascontiguousarray(np.concatenate([wcat_t, down_t], axis=0))

    wbt = np.ascontiguousarray(np.asarray(inputs["L0_ffn_W_base"]).T)

    n = H
    tt = np.arange(n, dtype=np.float64)
    qt = np.zeros((n, n), dtype=np.float64)
    qt[0, :] = 1.0 / np.sqrt(n)
    maxf = (n - 1) // 2
    f_idx = np.arange(1, maxf + 1, dtype=np.float64)
    ang = 2.0 * np.pi * np.outer(f_idx, tt) / n
    qt[1:1 + 2 * maxf:2, :] = np.sqrt(2.0 / n) * np.cos(ang)
    qt[2:2 + 2 * maxf:2, :] = np.sqrt(2.0 / n) * np.sin(ang)
    qt[n - 1, :] = ((-1.0) ** tt) / np.sqrt(n)
    qt = np.ascontiguousarray(qt.astype(ml_dtypes.bfloat16))

    enc = np.ascontiguousarray(np.concatenate(
        [np.asarray(inputs["L0_ffn_enc0"]), np.asarray(inputs["L0_ffn_enc1"])], axis=0))

    co0 = np.ascontiguousarray(
        np.asarray(inputs["L0_ffn_coefs0"]).reshape(3, 32, 128, 32).transpose(2, 0, 1, 3).reshape(128, -1))
    co1 = np.ascontiguousarray(
        np.asarray(inputs["L0_ffn_coefs1"]).reshape(3, 32, 128, 8).transpose(2, 0, 1, 3).reshape(128, -1))
    mk0 = np.ascontiguousarray(
        np.asarray(inputs["L0_ffn_mask0"]).astype(np.uint8).reshape(3, 32, 128, 32).transpose(2, 0, 1, 3).reshape(128, -1))
    mk1 = np.ascontiguousarray(
        np.asarray(inputs["L0_ffn_mask1"]).astype(np.uint8).reshape(3, 32, 128, 8).transpose(2, 0, 1, 3).reshape(128, -1))

    lp = np.ascontiguousarray(np.asarray(inputs["L0_attn_local_point"]).reshape(SC, 128).T)
    bb = np.ascontiguousarray(np.asarray(inputs["L0_ffn_b_base"]).reshape(OC, 128).T)
    disc = np.ascontiguousarray(np.asarray(inputs["L0_attn_discount_factor"]).reshape(1, G))

    shared = dict(up_t=up_t, gate_t=gate_t, ffngu=ffngu, epiw=epiw, wbt=wbt,
                  qt=qt, enc=enc, co0=co0, co1=co1, mk0=mk0, mk1=mk1,
                  lp=lp, bb=bb, disc=disc)
    in_maps = []
    for c in range(N_CORES):
        m = dict(shared)
        m["x_t"] = np.ascontiguousarray(x_t[:, c * TOK_PER_CORE:(c + 1) * TOK_PER_CORE])
        in_maps.append(m)
    return in_maps


def kernel(**inputs):
    in_maps = _prep_inputs(inputs)
    nc = build_program()
    res = run_bass_kernel_spmd(nc, in_maps, core_ids=list(range(N_CORES)))
    out_t = np.concatenate([r["out_t"] for r in res.results], axis=1)
    return np.ascontiguousarray(out_t.T).reshape(4, 4096, 4096)


# version 05
# speedup vs baseline: 1.0601x; 1.0315x over previous
import contextlib
import numpy as np
import ml_dtypes
import concourse.bass as bass
import concourse.mybir as mybir
from concourse.bass_utils import run_bass_kernel_spmd

F32 = mybir.dt.float32
BF16 = mybir.dt.bfloat16
U8 = mybir.dt.uint8
AF = mybir.ActivationFunctionType
ALU = mybir.AluOpType

N_CORES = 8
TOK_PER_CORE = 2048
NT = 512
N_TILES = TOK_PER_CORE // NT
H = 4096
KC = H // 128
S = 1024
SC = S // 128
F = 9984
FC = F // 128
NBAND = F // 256
OC = KC
G = 6
EPI_ROWS = S * 7 + F
EPI_KT = EPI_ROWS // 128
THD = 2.5
EPS = 1e-5


class Ctr:
    __slots__ = ("n",)
    def __init__(self):
        self.n = 0
    def add(self, k=1):
        self.n += k
        return self.n


def build_program():
    nc = bass.Bass("TRN2", target_bir_lowering=False, debug=False)

    d_x = nc.dram_tensor("x_t", [H, TOK_PER_CORE], F32, kind="ExternalInput").ap()
    d_up = nc.dram_tensor("up_t", [H, S], F32, kind="ExternalInput").ap()
    d_gate = nc.dram_tensor("gate_t", [H, S], F32, kind="ExternalInput").ap()
    d_ffngu = nc.dram_tensor("ffngu", [H, 2 * F], F32, kind="ExternalInput").ap()
    d_epiw = nc.dram_tensor("epiw", [EPI_ROWS, H], F32, kind="ExternalInput").ap()
    d_wbt = nc.dram_tensor("wbt", [H, H], F32, kind="ExternalInput").ap()
    d_qt = nc.dram_tensor("qt", [H, H], BF16, kind="ExternalInput").ap()
    d_enc = nc.dram_tensor("enc", [40, H], F32, kind="ExternalInput").ap()
    d_co0 = nc.dram_tensor("co0", [128, 3 * 32 * 32], F32, kind="ExternalInput").ap()
    d_co1 = nc.dram_tensor("co1", [128, 3 * 32 * 8], F32, kind="ExternalInput").ap()
    d_mk0 = nc.dram_tensor("mk0", [128, 3 * 32 * 32], U8, kind="ExternalInput").ap()
    d_mk1 = nc.dram_tensor("mk1", [128, 3 * 32 * 8], U8, kind="ExternalInput").ap()
    d_lp = nc.dram_tensor("lp", [128, SC], F32, kind="ExternalInput").ap()
    d_bb = nc.dram_tensor("bb", [128, OC], F32, kind="ExternalInput").ap()
    d_disc = nc.dram_tensor("disc", [1, G], F32, kind="ExternalInput").ap()
    d_out = nc.dram_tensor("out_t", [H, TOK_PER_CORE], F32, kind="ExternalOutput").ap()
    d_weff = nc.dram_tensor("weff_scr", [H, H], BF16, kind="Internal").ap()
    NGROUPS = 2 * 16 + NBAND * 8 + OC * 9
    NGH = (NGROUPS + 1) // 2
    d_blob0 = nc.dram_tensor("wblob0", [NGH * 128, 2048], BF16, kind="Internal").ap()
    d_blob1 = nc.dram_tensor("wblob1", [NGH * 128, 2048], BF16, kind="Internal").ap()

    v_x = d_x.rearrange("(k p) t -> p k t", p=128)
    v_up = d_up.rearrange("(k p) s -> p k s", p=128)
    v_gate = d_gate.rearrange("(k p) s -> p k s", p=128)
    v_ffngu = d_ffngu.rearrange("(k p) s -> p k s", p=128)
    v_epiw = d_epiw.rearrange("(k p) h -> p k h", p=128)
    v_qt = d_qt.rearrange("(k p) t -> p k t", p=128)
    v_weff = d_weff.rearrange("(k p) t -> p k t", p=128)
    v_blob0 = d_blob0.rearrange("(g p) c -> p g c", p=128)
    v_blob1 = d_blob1.rearrange("(g p) c -> p g c", p=128)

    def blob_view(g, ncols):
        if g < NGH:
            return v_blob0[:, g, 0:ncols]
        return v_blob1[:, g - NGH, 0:ncols]

    prog = {e: [] for e in ("sync", "gpsimd", "vector", "scalar", "tensor")}

    def E(eng, fn):
        prog[eng].append(fn)

    with contextlib.ExitStack() as es:
        ec = es.enter_context

        sem_pe = ec(nc.semaphore("sem_pe"))
        sem_rc = ec(nc.semaphore("sem_rc"))
        sem_a = ec(nc.semaphore("sem_a"))
        sem_v = ec(nc.semaphore("sem_v"))
        sem_bar = ec(nc.semaphore("sem_bar"))
        sem_out = ec(nc.semaphore("sem_out"))
        sem_r = [ec(nc.semaphore(f"sem_r{i}")) for i in range(3)]
        sem_xs = [ec(nc.semaphore(f"sem_xs{i}")) for i in range(2)]
        sem_wb = [ec(nc.semaphore(f"sem_wb{i}")) for i in range(2)]
        sem_ws = [ec(nc.semaphore(f"sem_ws{i}")) for i in range(2)]
        sem_xr = [ec(nc.semaphore(f"sem_xr{i}")) for i in range(2)]
        sem_web = ec(nc.semaphore("sem_web"))
        sem_bs = [ec(nc.semaphore(f"sem_bs{i}")) for i in range(3)]
        sem_ld = ec(nc.semaphore("sem_ld"))

        COUT, CPE, CRC, CA, CV = (Ctr() for _ in range(5))
        CR = [Ctr() for _ in range(3)]
        CXS = [Ctr() for _ in range(2)]
        CWB = [Ctr() for _ in range(2)]
        CWS = [Ctr() for _ in range(2)]
        CXR = [Ctr() for _ in range(2)]
        CWEB = Ctr()
        CBS = [Ctr() for _ in range(3)]
        CLD = Ctr()

        def W(eng, sem, n):
            if n > 0:
                E(eng, lambda e, s=sem, n=n: e.wait_ge(s, n))

        def W2(eng, sw):
            sem, n = sw
            W(eng, sem, n)

        def OPV(fn):
            if CV.n:
                E("vector", lambda e, n=CV.n: e.wait_ge(sem_v, n))
            E("vector", lambda e, f=fn: f(e).then_inc(sem_v, 1))
            return CV.add(1)

        def OPA(fn):
            if CA.n:
                E("scalar", lambda e, n=CA.n: e.wait_ge(sem_a, n))
            E("scalar", lambda e, f=fn: f(e).then_inc(sem_a, 1))
            return CA.add(1)

        RING = ec(nc.sbuf_tensor("RING", [128, 3 * 2048], BF16))
        LP = ec(nc.sbuf_tensor("LP", [128, SC], F32))
        BB = ec(nc.sbuf_tensor("BB", [128, OC], F32))
        EMC = ec(nc.sbuf_tensor("EMC", [128, 8], F32))
        ONESB = ec(nc.sbuf_tensor("ONESB", [128, 1], BF16))
        ONES1 = ec(nc.sbuf_tensor("ONES1", [1, 128], F32))
        CST = ec(nc.sbuf_tensor("CST", [1, 8], F32))
        CSTC = ec(nc.sbuf_tensor("CSTC", [128, 2], F32))
        PB = ec(nc.psum_tensor("PB", [128, 8 * NT], F32))

        def bank(j):
            return PB[:, j * NT:(j + 1) * NT]

        ring_idx = Ctr()
        ring_cons_at = {}
        ring_store_at = {}
        blob_cursor = Ctr()

        def ring_load(src_ap, policy="setup"):
            i = ring_idx.n
            ring_idx.add()
            sl = i % 3
            slot = RING[:, sl * 2048:(sl + 1) * 2048]
            if i >= 3:
                W("gpsimd", sem_rc, ring_cons_at[i - 3])
                if (i - 3) in ring_store_at:
                    W2("gpsimd", ring_store_at[i - 3])
            if policy == "setup":
                mode = "direct"
            else:
                g = blob_cursor.n
                blob_cursor.add()
                if policy == "t0":
                    mode = "save"
                else:
                    mode = "blob"
            ncols = int(np.prod(src_ap.shape[1:]))
            if mode == "blob":
                src_ap = blob_view(g, ncols)
            dst = slot[:, 0:ncols]
            if len(src_ap.shape) == 3:
                dst = dst.rearrange("p (a b) -> p a b", a=src_ap.shape[1])
            tgt = CR[sl].add(16)
            E("gpsimd", lambda e, s=src_ap, d=dst, m=sem_r[sl]:
              e.dma_start(out=d, in_=s).then_inc(m, 16))
            if mode == "save":
                W("sync", sem_r[sl], tgt)
                stgt = CBS[sl].add(16)
                E("sync", lambda e, s=slot[:, 0:ncols], d=blob_view(g, ncols), m=sem_bs[sl]:
                  e.dma_start(out=d, in_=s).then_inc(m, 16))
                ring_store_at[i] = (sem_bs[sl], stgt)
            return slot, i, (sem_r[sl], tgt)

        def _chain(inst, incs):
            for s_, v_ in incs:
                inst = inst.then_inc(s_, v_)
            return inst

        def TMM(fn, incs=()):
            if incs:
                E("tensor", lambda e, f=fn, iv=tuple(incs): _chain(f(e), iv))
            else:
                E("tensor", fn)

        def ring_mark(i):
            ring_cons_at[i] = CRC.add(1)
            return (sem_rc, 1)

        with contextlib.ExitStack() as ses:
            sc_ = ses.enter_context
            CO0 = sc_(nc.sbuf_tensor("CO0", [128, 3 * 32 * 32], F32))
            CO1 = sc_(nc.sbuf_tensor("CO1", [128, 3 * 32 * 8], F32))
            MK0 = sc_(nc.sbuf_tensor("MK0", [128, 3 * 32 * 32], U8))
            MK1 = sc_(nc.sbuf_tensor("MK1", [128, 3 * 32 * 8], U8))
            MKF = sc_(nc.sbuf_tensor("MKF", [128, 3 * 32 * 32], F32))
            W0S = sc_(nc.sbuf_tensor("W0S", [128, 32 * 32], F32))
            W1S = sc_(nc.sbuf_tensor("W1S", [128, 32 * 8], F32))
            W01B = sc_(nc.sbuf_tensor("W01B", [128, 32 * 40], BF16))
            ENCF = sc_(nc.sbuf_tensor("ENCF", [40, H], F32))
            ENCB = sc_(nc.sbuf_tensor("ENCB", [40, H], BF16))
            VCB = sc_(nc.sbuf_tensor("VCB", [40, H], BF16))
            WBB = sc_(nc.sbuf_tensor("WBB", [128, 2 * H], F32))
            WEO = sc_(nc.sbuf_tensor("WEO", [128, 2 * H], BF16))
            DRW = sc_(nc.sbuf_tensor("DRW", [1, 16], F32))

            for src, dst in ((d_co0, CO0[:, :]), (d_co1, CO1[:, :]),
                             (d_mk0, MK0[:, :]), (d_mk1, MK1[:, :]),
                             (d_enc, ENCF[:, :]), (d_lp, LP[:, :]),
                             (d_bb, BB[:, :]), (d_disc, DRW[:, 0:G])):
                CLD.add(16)
                E("sync", lambda e, s=src, d=dst: e.dma_start(out=d, in_=s).then_inc(sem_ld, 16))
            c_small = CLD.n

            OPV(lambda e: e.memset(ONESB[:, :], 1.0))
            OPV(lambda e: e.memset(ONES1[:, :], 1.0))
            OPV(lambda e: e.memset(CST[:, 0:1], EPS))
            OPV(lambda e: e.memset(CST[:, 1:2], -0.5))
            OPV(lambda e: e.memset(CST[:, 2:3], -1.0))
            OPV(lambda e: e.memset(CST[:, 3:4], 0.5))
            c_consts = OPV(lambda e: e.memset(CSTC[:, 0:1], 0.5))

            W("vector", sem_ld, c_small)
            OPV(lambda e: e.tensor_copy(MKF[:, :], MK0[:, :]))
            for e_i in range(3):
                sl = slice(e_i * 1024, (e_i + 1) * 1024)
                if e_i == 0:
                    OPV(lambda e, s=sl: e.tensor_mul(W0S[:, :], CO0[:, s], MKF[:, s]))
                else:
                    OPV(lambda e, s=sl: e.tensor_mul(MKF[:, s], CO0[:, s], MKF[:, s]))
                    OPV(lambda e, s=sl: e.tensor_add(W0S[:, :], W0S[:, :], MKF[:, s]))
            OPV(lambda e: e.tensor_copy(MKF[:, 0:3 * 256], MK1[:, :]))
            for e_i in range(3):
                sl = slice(e_i * 256, (e_i + 1) * 256)
                if e_i == 0:
                    OPV(lambda e, s=sl: e.tensor_mul(W1S[:, :], CO1[:, s], MKF[:, s]))
                else:
                    OPV(lambda e, s=sl: e.tensor_mul(MKF[:, s], CO1[:, s], MKF[:, s]))
                    OPV(lambda e, s=sl: e.tensor_add(W1S[:, :], W1S[:, :], MKF[:, s]))
            OPV(lambda e: e.tensor_copy(
                W01B.rearrange("p (c w) -> p c w", w=40)[:, :, 0:32],
                W0S.rearrange("p (c r) -> p c r", r=32)))
            OPV(lambda e: e.tensor_copy(
                W01B.rearrange("p (c w) -> p c w", w=40)[:, :, 32:40],
                W1S.rearrange("p (c r) -> p c r", r=8)))
            c_w01 = OPV(lambda e: e.tensor_copy(ENCB[:, :], ENCF[:, :]))

            W("scalar", sem_ld, c_small)
            W("scalar", sem_v, c_consts)
            c_emc_row = OPA(lambda e: e.activation(DRW[:, 8:8 + G], DRW[:, 0:G], AF.Exp,
                                                   bias=0.0, scale=CST[0:1, 2:3]))
            W("tensor", sem_a, c_emc_row)
            E("tensor", lambda e: e.matmul(PB[:, 7 * NT:7 * NT + G], ONES1[:, :], DRW[:, 8:8 + G],
                                           start=True, stop=True).then_inc(sem_pe, 1))
            c_emc_mm = CPE.add(1)
            W("vector", sem_pe, c_emc_mm)
            c_emc = OPV(lambda e: e.tensor_copy(EMC[:, 0:G], PB[:, 7 * NT:7 * NT + G]))

            c_vcb_prev = 0
            for hh in range(8):
                t0 = hh * NT
                for og in range(8):
                    src = v_qt[:, og * 4:(og + 1) * 4, t0:t0 + NT]
                    slot, fi, wcnt = ring_load(src)
                    W2("tensor", wcnt)
                    if hh == 0 and og == 0:
                        W("tensor", sem_v, c_w01)
                    if og == 0 and hh > 0:
                        W("tensor", sem_a, c_vcb_prev)
                    for oc in range(4):
                        c = og * 4 + oc
                        lhsT = W01B[:, c * 40:(c + 1) * 40]
                        rhs = slot[:, oc * NT:(oc + 1) * NT]
                        incs = []
                        if oc == 3:
                            incs.append(ring_mark(fi))
                        TMM(lambda e, l=lhsT, r=rhs, st=(c == 0), sp=(c == 31):
                            e.matmul(PB[0:40, 6 * NT:7 * NT], l, r, start=st, stop=sp), incs)
                c_vc_mm = (sem_rc, CRC.n)
                W2("scalar", c_vc_mm)
                c_vcb_prev = OPA(lambda e, t0=t0: e.copy(VCB[0:40, t0:t0 + NT],
                                                         PB[0:40, 6 * NT:7 * NT]))

            c_wb_load = {}
            c_wb_stt = {}
            c_stt_at = {}
            c_ws_store = {}

            def emit_wb_load(dc):
                par = dc % 2
                buf = WBB[:, par * H:par * H + H]
                if dc >= 2:
                    W("sync", sem_v, c_wb_stt[dc - 2])
                tgt = CWB[par].add(16)
                E("sync", lambda e, d=buf, s=d_wbt[dc * 128:(dc + 1) * 128, :], m=sem_wb[par]:
                  e.dma_start(out=d, in_=s).then_inc(m, 16))
                c_wb_load[dc] = (sem_wb[par], tgt)

            emit_wb_load(0)
            emit_wb_load(1)
            for dc in range(KC):
                par = dc % 2
                buf = WBB[:, par * H:par * H + H]
                obuf = WEO[:, par * H:par * H + H]
                for hh in range(8):
                    idx = dc * 8 + hh
                    pb_i = 4 + (idx % 2)
                    lhsT = ENCB[:, dc * 128:(dc + 1) * 128]
                    rhs = VCB[:, hh * NT:(hh + 1) * NT]
                    if idx >= 2:
                        W("tensor", sem_v, c_stt_at[idx - 2])
                    if idx == 0:
                        W("tensor", sem_a, c_vcb_prev)
                    E("tensor", lambda e, l=lhsT, r=rhs, b=pb_i:
                      e.matmul(bank(b), l, r, start=True, stop=True).then_inc(sem_pe, 1))
                    c_mm = CPE.add(1)
                    W("vector", sem_pe, c_mm)
                    if hh == 0:
                        W2("vector", c_wb_load[dc])
                        if dc >= 2:
                            W2("vector", c_ws_store[dc - 2])
                    c_stt_at[idx] = OPV(lambda e, o=obuf[:, hh * NT:(hh + 1) * NT], b=pb_i,
                                        w=buf[:, hh * NT:(hh + 1) * NT]:
                                        e.scalar_tensor_tensor(o, bank(b), CSTC[:, 0:1], w,
                                                               ALU.mult, ALU.add))
                c_wb_stt[dc] = c_stt_at[dc * 8 + 7]
                W("sync", sem_v, c_wb_stt[dc])
                tgt = CWS[par].add(16)
                E("sync", lambda e, o=d_weff[dc * 128:(dc + 1) * 128, :], s=obuf, m=sem_ws[par]:
                  e.dma_start(out=o, in_=s).then_inc(m, 16))
                c_ws_store[dc] = (sem_ws[par], tgt)
                if dc + 2 < KC:
                    emit_wb_load(dc + 2)

            for eng in ("sync", "gpsimd", "vector", "scalar", "tensor"):
                W(eng, sem_v, CV.n)
                W(eng, sem_a, CA.n)
                W(eng, sem_pe, CPE.n)
                W(eng, sem_rc, CRC.n)
                W(eng, sem_ld, CLD.n)
                W(eng, sem_ws[0], CWS[0].n)
                W(eng, sem_ws[1], CWS[1].n)
                W(eng, sem_wb[0], CWB[0].n)
                W(eng, sem_wb[1], CWB[1].n)
                for i_ in range(3):
                    W(eng, sem_r[i_], CR[i_].n)

        XN = ec(nc.sbuf_tensor("XN", [128, KC * NT], BF16))
        FEAT = ec(nc.sbuf_tensor("FEAT", [128, 56 * NT], BF16))
        WINT = ec(nc.sbuf_tensor("WINT", [128, FC * NT], BF16))
        WEB = ec(nc.sbuf_tensor("WEB", [128, KC * 128], BF16))
        XRES = ec(nc.sbuf_tensor("XRES", [128, 2 * NT], F32))
        OUTB = ec(nc.sbuf_tensor("OUTB", [128, NT], F32))
        SCRA = ec(nc.sbuf_tensor("SCRA", [128, NT], F32))
        SCRB = ec(nc.sbuf_tensor("SCRB", [128, NT], F32))
        SGF = ec(nc.sbuf_tensor("SGF", [128, 2 * NT], F32))
        ROWS = ec(nc.sbuf_tensor("ROWS", [1, 3 * NT], F32))

        DELTA = WINT[:, 0:8 * NT * 2].bitcast(F32)
        PW = WINT[:, 8 * NT * 2:16 * NT * 2].bitcast(F32)
        XSTAT = FEAT[:, 40 * NT:56 * NT].bitcast(F32)
        RBB = WEB[:, 0:2 * NT].bitcast(F32)
        MRB = WEB[:, 2 * NT:4 * NT].bitcast(F32)
        XBF = FEAT[:, 32 * NT:34 * NT]
        X2BF = FEAT[:, 34 * NT:36 * NT]

        def feat(kc):
            return FEAT[:, kc * NT:(kc + 1) * NT]

        def wint(kc):
            return WINT[:, kc * NT:(kc + 1) * NT]

        def xnc(k):
            return XN[:, k * NT:(k + 1) * NT]

        def delta_j(j):
            return DELTA[:, j * NT:(j + 1) * NT]

        def pw_j(j):
            return PW[:, j * NT:(j + 1) * NT]

        xstat_cons = {}
        c_mmst = {}
        c_epi_sg = {}
        c_epi_g = {}
        c_epi_out = {}
        c_out_dma = {}
        c_web_done = 0
        c_ffn_done_prev = 0
        c_xn_done = 0
        c_bcast = 0

        RA = ROWS[0:1, 0:NT]
        RB = ROWS[0:1, NT:2 * NT]
        RC = ROWS[0:1, 2 * NT:3 * NT]

        for t in range(N_TILES):
            tc = slice(t * NT, (t + 1) * NT)
            c_pe_prev_tile = CPE.n
            c_rc_prev_tile = CRC.n
            rmode = ("t0", "tn", "tn", "tn")[t]
            blob_cursor.n = 0
            if t >= 1:
                for i_ in range(3):
                    W("gpsimd", sem_bs[i_], CBS[i_].n)

            c_xload = {}
            c_xn_at = {}

            def emit_ln_load(key, t=t, tc=tc):
                gk = t * 16 + key
                slot = key % 2
                if gk >= 2 and (gk - 2) in xstat_cons:
                    for sem_o, cnt_o in xstat_cons[gk - 2]:
                        W("sync", sem_o, cnt_o)
                if key == 0 and t > 0:
                    W("sync", sem_pe, c_pe_prev_tile)
                    W("sync", sem_rc, c_rc_prev_tile)
                g4 = key % 8
                src = v_x[:, g4 * 4:(g4 + 1) * 4, tc]
                dst = XSTAT[:, slot * 4 * NT:(slot + 1) * 4 * NT].rearrange(
                    "p (a b) -> p a b", a=4)
                tgt = CXS[slot].add(16)
                E("sync", lambda e, s=src, d=dst, m=sem_xs[slot]:
                  e.dma_start(out=d, in_=s).then_inc(m, 16))
                c_xload[key] = (sem_xs[slot], tgt)

            def emit_stats_group(g4, t=t):
                slot = g4 % 2
                for kk in range(4):
                    k = g4 * 4 + kk
                    xv = XSTAT[:, (slot * 4 + kk) * NT:(slot * 4 + kk + 1) * NT]
                    xb = XBF[:, (k % 2) * NT:(k % 2 + 1) * NT]
                    x2 = X2BF[:, (k % 2) * NT:(k % 2 + 1) * NT]
                    W2("vector", c_xload[g4])
                    W2("scalar", c_xload[g4])
                    if k >= 2 or t > 0:
                        kk2 = (k - 2) % KC
                        W("vector", sem_pe, c_mmst[kk2])
                        W("scalar", sem_pe, c_mmst[kk2])
                    if k == 0 and t > 0:
                        W("vector", sem_pe, c_pe_prev_tile)
                        W("scalar", sem_pe, c_pe_prev_tile)
                        W("vector", sem_rc, c_rc_prev_tile)
                        W("scalar", sem_rc, c_rc_prev_tile)
                    c_cast = OPV(lambda e, a=xb, b=xv: e.tensor_copy(a, b))
                    c_sq = OPA(lambda e, a=x2, b=xv: e.activation(a, b, AF.Square))
                    W("tensor", sem_v, c_cast)
                    W("tensor", sem_a, c_sq)
                    if k == 0 and t > 0:
                        W("tensor", sem_v, c_ffn_done_prev)
                    st = (k == 0)
                    sp = (k == KC - 1)
                    E("tensor", lambda e, a=xb, st=st, sp=sp:
                      e.matmul(PB[0:1, 7 * NT:8 * NT], ONESB[:, :], a, start=st, stop=sp))
                    E("tensor", lambda e, a=x2, st=st, sp=sp:
                      e.matmul(PB[0:1, 4 * NT:5 * NT], ONESB[:, :], a, start=st, stop=sp).then_inc(sem_pe, 1))
                    c_mmst[k] = CPE.add(1)
                xstat_cons[t * 16 + g4] = ((sem_a, CA.n), (sem_v, CV.n))

            def emit_xn_group(g4, t=t):
                slot = g4 % 2
                for kk in range(4):
                    k = g4 * 4 + kk
                    xv = XSTAT[:, (slot * 4 + kk) * NT:(slot * 4 + kk + 1) * NT]
                    W2("vector", c_xload[8 + g4])
                    if k == 0:
                        W("vector", sem_a, c_rbb)
                    OPV(lambda e, xv=xv: e.tensor_mul(SCRB[:, :], xv, RBB[:, :]))
                    c_xn_at[k] = OPV(lambda e, k=k: e.tensor_sub(xnc(k), SCRB[:, :], MRB[:, :]))
                xstat_cons[t * 16 + 8 + g4] = ((sem_v, CV.n),)

            emit_ln_load(0)
            emit_ln_load(1)
            for g4 in range(8):
                emit_stats_group(g4)
                if g4 + 2 <= 7:
                    emit_ln_load(g4 + 2)
            c_stats_mm = c_mmst[KC - 1]

            W("vector", sem_pe, c_stats_mm)
            OPV(lambda e: e.tensor_scalar_mul(RA, PB[0:1, 7 * NT:8 * NT], 1.0 / H))
            OPV(lambda e: e.tensor_scalar_mul(RB, PB[0:1, 4 * NT:5 * NT], 1.0 / H))
            OPV(lambda e: e.tensor_mul(RC, RA, RA))
            c_var = OPV(lambda e: e.tensor_sub(RB, RB, RC))
            W("scalar", sem_v, c_var)
            c_ln = OPA(lambda e: e.activation(RC, RB, AF.Ln, bias=CST[0:1, 0:1], scale=1.0))
            c_rstd = OPA(lambda e: e.activation(RB, RC, AF.Exp, bias=0.0,
                                                scale=CST[0:1, 1:2]))
            W("vector", sem_a, c_rstd)
            c_mr = OPV(lambda e: e.tensor_mul(RC, RA, RB))

            W("tensor", sem_v, c_mr)
            if t > 0:
                W("tensor", sem_v, c_ffn_done_prev)
            if t > 0:
                W("tensor", sem_pe, c_pe_prev_tile)
            E("tensor", lambda e: e.matmul(PB[:, 5 * NT:6 * NT], ONES1[:, :], RB, start=True, stop=True))
            E("tensor", lambda e: e.matmul(PB[:, 6 * NT:7 * NT], ONES1[:, :], RC,
                                           start=True, stop=True).then_inc(sem_pe, 1))
            c_bcast = CPE.add(1)
            W("scalar", sem_pe, c_bcast)
            if t > 0:
                W("scalar", sem_pe, c_pe_prev_tile)
            OPA(lambda e: e.copy(RBB[:, :], PB[:, 5 * NT:6 * NT]))
            c_rbb = OPA(lambda e: e.copy(MRB[:, :], PB[:, 6 * NT:7 * NT]))

            emit_ln_load(8)
            emit_ln_load(9)
            for g4 in range(8):
                emit_xn_group(g4)
                if g4 + 2 <= 7:
                    emit_ln_load(8 + g4 + 2)
            c_xn_done = CV.n

            c_p1_evac_u = {}
            c_p1_delta = {}
            for sweep in range(2):
                vsrc = v_up if sweep == 0 else v_gate
                for g2 in range(16):
                    src = vsrc[:, g2 * 2:(g2 + 1) * 2, :]
                    slot, fi, wcnt = ring_load(src, rmode)
                    W2("tensor", wcnt)
                    W("tensor", sem_v, c_xn_at[2 * g2 + 1])
                    if g2 == 0:
                        if sweep == 0:
                            W("tensor", sem_a, c_rbb)
                            if t > 0:
                                W("tensor", sem_v, c_epi_sg[31])
                        if sweep == 1:
                            W("tensor", sem_a, c_p1_evac_u[SC - 1])
                    for kk in range(2):
                        k = g2 * 2 + kk
                        for j in range(SC):
                            lhsT = slot[:, kk * S + j * 128:kk * S + (j + 1) * 128]
                            incs = []
                            if kk == 1 and j == SC - 1:
                                incs.append(ring_mark(fi))
                            TMM(lambda e, l=lhsT, j=j, k=k, st=(k == 0), sp=(k == KC - 1):
                                e.matmul(bank(j), l, xnc(k), start=st, stop=sp), incs)
                c_sweep_done = (sem_rc, CRC.n)
                for j in range(SC):
                    if sweep == 0:
                        W2("scalar", c_sweep_done)
                        c_p1_evac_u[j] = OPA(lambda e, j=j: e.copy(feat(j), bank(j)))
                    else:
                        W2("vector", c_sweep_done)
                        c_p1_delta[j] = OPV(lambda e, j=j: e.tensor_scalar(
                            delta_j(j), bank(j), LP[:, j:j + 1], None, ALU.subtract))

            def emit_fg(j):
                dj = delta_j(j)
                pj = pw_j(j)
                uj = feat(j)
                W("vector", sem_v, c_p1_delta[j])
                W("vector", sem_a, c_p1_evac_u[j])
                for m in range(1, 7):
                    if m == 2:
                        OPV(lambda e, d=dj, p=pj: e.tensor_mul(p, d, d))
                    elif m >= 3:
                        OPV(lambda e, d=dj, p=pj: e.tensor_mul(p, p, d))
                    if m == 5:
                        W("scalar", sem_v, CV.n)
                        c_abs = OPA(lambda e, d=dj: e.activation(SCRA[:, :], d, AF.Abs))
                        W("vector", sem_a, c_abs)
                        OPV(lambda e: e.tensor_scalar(SCRB[:, :], SCRA[:, :], THD, None, ALU.is_le))
                        OPV(lambda e, p=pj: e.tensor_mul(p, p, SCRB[:, :]))
                    fsrc = dj if m == 1 else pj
                    fdst = feat(m * 8 + j)
                    OPV(lambda e, s=fsrc, d=fdst, u=uj, m=m:
                        e.scalar_tensor_tensor(d, s, EMC[:, m - 1:m], u, ALU.mult, ALU.mult))

            c_ffn_evac = {}
            for b in range(NBAND):
                par = b % 2
                bk = [4 * par + q for q in range(4)]
                for g4 in range(8):
                    src = v_ffngu[:, g4 * 4:(g4 + 1) * 4, b * 512:(b + 1) * 512]
                    slot, fi, wcnt = ring_load(src, rmode)
                    W2("tensor", wcnt)
                    if g4 == 0:
                        if b == 0:
                            W("tensor", sem_v, c_p1_delta[3])
                        if b == 1:
                            W("tensor", sem_v, c_p1_delta[7])
                        if b >= 2:
                            W("tensor", sem_v, c_ffn_evac[b - 2])
                    for kk in range(4):
                        k = g4 * 4 + kk
                        for q in range(4):
                            lhsT = slot[:, kk * 512 + q * 128:kk * 512 + (q + 1) * 128]
                            incs = []
                            if kk == 3 and q == 3:
                                incs.append(ring_mark(fi))
                            TMM(lambda e, l=lhsT, bq=bk[q], k=k, st=(k == 0), sp=(k == KC - 1):
                                e.matmul(bank(bq), l, xnc(k), start=st, stop=sp), incs)
                c_band_done = (sem_rc, CRC.n)
                if b < SC:
                    emit_fg(b)
                for q in range(2):
                    fchunk = 2 * b + q
                    sg = SGF[:, q * NT:(q + 1) * NT]
                    W2("scalar", c_band_done)
                    if b >= 1:
                        W("scalar", sem_v, c_ffn_evac[b - 1])
                    c_silu = OPA(lambda e, s=sg, bq=bk[q]:
                                 e.activation(s, bank(bq), AF.Silu))
                    W("vector", sem_a, c_silu)
                    c_ffn_evac[b] = OPV(lambda e, s=sg, f=fchunk, bq=bk[2 + q]:
                                        e.tensor_mul(wint(f), s, bank(bq)))
            c_ffn_done = c_ffn_evac[NBAND - 1]
            c_ffn_done_prev = c_ffn_done

            ngr = (EPI_KT + 15) // 16
            for o in range(OC):
                par = o % 2
                bs = par
                bg = 2 + par
                gslots = []
                for gi in range(min(3, ngr)):
                    k0 = gi * 16
                    kn = min(16, EPI_KT - k0)
                    gslots.append(ring_load(v_epiw[:, k0:k0 + kn, o * 128:(o + 1) * 128], rmode) + (kn,))
                if t > 0 or o >= 1:
                    W("sync", sem_pe, c_web_done)
                if o == 0:
                    W("sync", sem_v, c_xn_done)
                tgt_weff = CWEB.add(16)
                E("sync", lambda e, s=v_weff[:, :, o * 128:(o + 1) * 128],
                  d=WEB.rearrange("p (k c) -> p k c", c=128):
                  e.dma_start(out=d, in_=s).then_inc(sem_web, 16))
                if t > 0 or o >= 2:
                    W("sync", sem_v, c_epi_out[(o - 2) % OC])
                tgt_xres = CXR[par].add(16)
                E("sync", lambda e, s=d_x[o * 128:(o + 1) * 128, tc],
                  d=XRES[:, par * NT:(par + 1) * NT], m=sem_xr[par]:
                  e.dma_start(out=d, in_=s).then_inc(m, 16))

                if o == 0:
                    W("tensor", sem_v, c_ffn_done)
                if t > 0 or o >= 2:
                    W("tensor", sem_v, c_epi_sg[(o - 2) % OC])
                kt = 0
                for gi in range(ngr):
                    slot, fi, wcnt, kn = gslots[gi]
                    W2("tensor", wcnt)
                    for kk in range(kn):
                        lhsT = slot[:, kk * 128:(kk + 1) * 128]
                        rhs = feat(kt) if kt < 56 else wint(kt - 56)
                        incs = []
                        if kk == kn - 1:
                            incs.append(ring_mark(fi))
                        TMM(lambda e, l=lhsT, r=rhs, st=(kt == 0), sp=(kt == EPI_KT - 1), b=bs:
                            e.matmul(bank(b), l, r, start=st, stop=sp), incs)
                        kt += 1
                    if gi + 3 < ngr:
                        k0 = (gi + 3) * 16
                        kn2 = min(16, EPI_KT - k0)
                        gslots.append(ring_load(v_epiw[:, k0:k0 + kn2, o * 128:(o + 1) * 128], rmode) + (kn2,))
                c_s_done = (sem_rc, CRC.n)

                W("tensor", sem_web, tgt_weff)
                if t > 0 or o >= 2:
                    W("tensor", sem_a, c_epi_g[(o - 2) % OC])
                for k in range(KC):
                    lhsT = WEB[:, k * 128:(k + 1) * 128]
                    incs = []
                    if k == KC - 1:
                        incs.append((sem_pe, 1))
                    TMM(lambda e, l=lhsT, k=k, st=(k == 0), sp=(k == KC - 1), b=bg:
                        e.matmul(bank(b), l, xnc(k), start=st, stop=sp), incs)
                c_g_done = CPE.add(1)
                c_web_done = c_g_done

                W("scalar", sem_pe, c_g_done)
                if t > 0 or o >= 1:
                    W("scalar", sem_v, c_epi_sg[(o - 1) % OC])
                c_epi_g[o] = OPA(lambda e, o=o, b=bg:
                                 e.activation(SCRA[:, :], bank(b), AF.Silu, bias=BB[:, o:o + 1],
                                              scale=1.0))
                sgbuf = SGF[:, par * NT:(par + 1) * NT]
                W2("vector", c_s_done)
                W("vector", sem_a, c_epi_g[o])
                c_epi_sg[o] = OPV(lambda e, s=sgbuf, b=bs:
                                  e.tensor_mul(s, bank(b), SCRA[:, :]))
                W("vector", sem_xr[par], tgt_xres)
                if t > 0 or o >= 1:
                    W("vector", sem_out, c_out_dma[(o - 1) % OC])
                c_epi_out[o] = OPV(lambda e, s=sgbuf, x=XRES[:, par * NT:(par + 1) * NT]:
                                   e.tensor_add(OUTB[:, :], s, x))
                W("sync", sem_v, c_epi_out[o])
                cnt_out = COUT.add(16)
                E("sync", lambda e, d=d_out[o * 128:(o + 1) * 128, tc]:
                  e.dma_start(out=d, in_=OUTB[:, :]).then_inc(sem_out, 16))
                c_out_dma[o] = cnt_out

        W("sync", sem_out, COUT.n)

        with nc.Block() as block:
            @block.sync
            def _(eng):
                for fn in prog["sync"]:
                    fn(eng)

            @block.gpsimd
            def _(eng):
                for fn in prog["gpsimd"]:
                    fn(eng)

            @block.vector
            def _(eng):
                for fn in prog["vector"]:
                    fn(eng)

            @block.scalar
            def _(eng):
                for fn in prog["scalar"]:
                    fn(eng)

            @block.tensor
            def _(eng):
                for fn in prog["tensor"]:
                    fn(eng)

    return nc


def _prep_inputs(inputs):
    x = np.ascontiguousarray(np.asarray(inputs["L0_attn_hidden_states"])).reshape(16384, H)
    x_t = np.ascontiguousarray(x.T)

    up_t = np.ascontiguousarray(np.asarray(inputs["L0_attn_up_w"]).T)
    gate_t = np.ascontiguousarray(np.asarray(inputs["L0_attn_gate_w"]).T)
    fg_t = np.asarray(inputs["L0_attn_ffn_gate_w"]).T
    fu_t = np.asarray(inputs["L0_attn_ffn_up_w"]).T
    gb = fg_t.reshape(H, NBAND, 256)
    ub = fu_t.reshape(H, NBAND, 256)
    ffngu = np.ascontiguousarray(np.concatenate([gb, ub], axis=2).reshape(H, 2 * F))

    lao = np.asarray(inputs["L0_attn_local_approx_output"])
    fuse = np.asarray(inputs["L0_attn_fuse_weight"])
    wcat_t = np.concatenate([lao.T[None], fuse.transpose(2, 1, 0)], axis=0).reshape(7 * S, H)
    down_t = np.asarray(inputs["L0_attn_ffn_down_w"]).T
    epiw = np.ascontiguousarray(np.concatenate([wcat_t, down_t], axis=0))

    wbt = np.ascontiguousarray(np.asarray(inputs["L0_ffn_W_base"]).T)

    n = H
    tt = np.arange(n, dtype=np.float64)
    qt = np.zeros((n, n), dtype=np.float64)
    qt[0, :] = 1.0 / np.sqrt(n)
    maxf = (n - 1) // 2
    f_idx = np.arange(1, maxf + 1, dtype=np.float64)
    ang = 2.0 * np.pi * np.outer(f_idx, tt) / n
    qt[1:1 + 2 * maxf:2, :] = np.sqrt(2.0 / n) * np.cos(ang)
    qt[2:2 + 2 * maxf:2, :] = np.sqrt(2.0 / n) * np.sin(ang)
    qt[n - 1, :] = ((-1.0) ** tt) / np.sqrt(n)
    qt = np.ascontiguousarray(qt.astype(ml_dtypes.bfloat16))

    enc = np.ascontiguousarray(np.concatenate(
        [np.asarray(inputs["L0_ffn_enc0"]), np.asarray(inputs["L0_ffn_enc1"])], axis=0))

    co0 = np.ascontiguousarray(
        np.asarray(inputs["L0_ffn_coefs0"]).reshape(3, 32, 128, 32).transpose(2, 0, 1, 3).reshape(128, -1))
    co1 = np.ascontiguousarray(
        np.asarray(inputs["L0_ffn_coefs1"]).reshape(3, 32, 128, 8).transpose(2, 0, 1, 3).reshape(128, -1))
    mk0 = np.ascontiguousarray(
        np.asarray(inputs["L0_ffn_mask0"]).astype(np.uint8).reshape(3, 32, 128, 32).transpose(2, 0, 1, 3).reshape(128, -1))
    mk1 = np.ascontiguousarray(
        np.asarray(inputs["L0_ffn_mask1"]).astype(np.uint8).reshape(3, 32, 128, 8).transpose(2, 0, 1, 3).reshape(128, -1))

    lp = np.ascontiguousarray(np.asarray(inputs["L0_attn_local_point"]).reshape(SC, 128).T)
    bb = np.ascontiguousarray(np.asarray(inputs["L0_ffn_b_base"]).reshape(OC, 128).T)
    disc = np.ascontiguousarray(np.asarray(inputs["L0_attn_discount_factor"]).reshape(1, G))

    shared = dict(up_t=up_t, gate_t=gate_t, ffngu=ffngu, epiw=epiw, wbt=wbt,
                  qt=qt, enc=enc, co0=co0, co1=co1, mk0=mk0, mk1=mk1,
                  lp=lp, bb=bb, disc=disc)
    in_maps = []
    for c in range(N_CORES):
        m = dict(shared)
        m["x_t"] = np.ascontiguousarray(x_t[:, c * TOK_PER_CORE:(c + 1) * TOK_PER_CORE])
        in_maps.append(m)
    return in_maps


def kernel(**inputs):
    in_maps = _prep_inputs(inputs)
    nc = build_program()
    res = run_bass_kernel_spmd(nc, in_maps, core_ids=list(range(N_CORES)))
    out_t = np.concatenate([r["out_t"] for r in res.results], axis=1)
    return np.ascontiguousarray(out_t.T).reshape(4, 4096, 4096)


# version 06 (final)
# speedup vs baseline: 1.1609x; 1.0951x over previous
import contextlib
import numpy as np
import ml_dtypes
import concourse.bass as bass
import concourse.mybir as mybir
from concourse.bass_utils import run_bass_kernel_spmd

F32 = mybir.dt.float32
BF16 = mybir.dt.bfloat16
U8 = mybir.dt.uint8
AF = mybir.ActivationFunctionType
ALU = mybir.AluOpType

N_CORES = 8
TOK_PER_CORE = 2048
NT = 512
N_TILES = TOK_PER_CORE // NT
H = 4096
KC = H // 128
S = 1024
SC = S // 128
F = 9984
FC = F // 128
NBAND = F // 256
OC = KC
G = 6
EPI_ROWS = S * 7 + F
EPI_KT = EPI_ROWS // 128
THD = 2.5
EPS = 1e-5


class Ctr:
    __slots__ = ("n",)
    def __init__(self):
        self.n = 0
    def add(self, k=1):
        self.n += k
        return self.n


def build_program():
    nc = bass.Bass("TRN2", target_bir_lowering=False, debug=False)

    d_x = nc.dram_tensor("x_t", [H, TOK_PER_CORE], F32, kind="ExternalInput").ap()
    d_up = nc.dram_tensor("up_t", [H, S], F32, kind="ExternalInput").ap()
    d_gate = nc.dram_tensor("gate_t", [H, S], F32, kind="ExternalInput").ap()
    d_ffngu = nc.dram_tensor("ffngu", [H, 2 * F], F32, kind="ExternalInput").ap()
    d_epiw = nc.dram_tensor("epiw", [EPI_ROWS, H], F32, kind="ExternalInput").ap()
    d_wbt = nc.dram_tensor("wbt", [H, H], F32, kind="ExternalInput").ap()
    d_qt = nc.dram_tensor("qt", [H, H], BF16, kind="ExternalInput").ap()
    d_enc = nc.dram_tensor("enc", [40, H], F32, kind="ExternalInput").ap()
    d_co0 = nc.dram_tensor("co0", [128, 3 * 32 * 32], F32, kind="ExternalInput").ap()
    d_co1 = nc.dram_tensor("co1", [128, 3 * 32 * 8], F32, kind="ExternalInput").ap()
    d_mk0 = nc.dram_tensor("mk0", [128, 3 * 32 * 32], U8, kind="ExternalInput").ap()
    d_mk1 = nc.dram_tensor("mk1", [128, 3 * 32 * 8], U8, kind="ExternalInput").ap()
    d_lp = nc.dram_tensor("lp", [128, SC], F32, kind="ExternalInput").ap()
    d_bb = nc.dram_tensor("bb", [128, OC], F32, kind="ExternalInput").ap()
    d_disc = nc.dram_tensor("disc", [1, G], F32, kind="ExternalInput").ap()
    d_out = nc.dram_tensor("out_t", [H, TOK_PER_CORE], F32, kind="ExternalOutput").ap()
    d_weff = nc.dram_tensor("weff_scr", [H, H], BF16, kind="Internal").ap()
    NGROUPS = 2 * 16 + NBAND * 8 + OC * 9
    NGH = (NGROUPS + 1) // 2
    d_blob0 = nc.dram_tensor("wblob0", [NGH * 128, 2048], BF16, kind="Internal").ap()
    d_blob1 = nc.dram_tensor("wblob1", [NGH * 128, 2048], BF16, kind="Internal").ap()

    v_x = d_x.rearrange("(k p) t -> p k t", p=128)
    v_up = d_up.rearrange("(k p) s -> p k s", p=128)
    v_gate = d_gate.rearrange("(k p) s -> p k s", p=128)
    v_ffngu = d_ffngu.rearrange("(k p) s -> p k s", p=128)
    v_epiw = d_epiw.rearrange("(k p) h -> p k h", p=128)
    v_qt = d_qt.rearrange("(k p) t -> p k t", p=128)
    v_weff = d_weff.rearrange("(k p) t -> p k t", p=128)
    v_blob0 = d_blob0.rearrange("(g p) c -> p g c", p=128)
    v_blob1 = d_blob1.rearrange("(g p) c -> p g c", p=128)

    def blob_view(g, ncols):
        if g < NGH:
            return v_blob0[:, g, 0:ncols]
        return v_blob1[:, g - NGH, 0:ncols]

    prog = {e: [] for e in ("sync", "gpsimd", "vector", "scalar", "tensor")}

    def E(eng, fn):
        prog[eng].append(fn)

    with contextlib.ExitStack() as es:
        ec = es.enter_context

        sem_pe = ec(nc.semaphore("sem_pe"))
        sem_rc = ec(nc.semaphore("sem_rc"))
        sem_a = ec(nc.semaphore("sem_a"))
        sem_v = ec(nc.semaphore("sem_v"))
        sem_bar = ec(nc.semaphore("sem_bar"))
        sem_out = ec(nc.semaphore("sem_out"))
        sem_r = [ec(nc.semaphore(f"sem_r{i}")) for i in range(4)]
        sem_xs = [ec(nc.semaphore(f"sem_xs{i}")) for i in range(2)]
        sem_wb = [ec(nc.semaphore(f"sem_wb{i}")) for i in range(2)]
        sem_ws = [ec(nc.semaphore(f"sem_ws{i}")) for i in range(2)]
        sem_xr = [ec(nc.semaphore(f"sem_xr{i}")) for i in range(2)]
        sem_web = ec(nc.semaphore("sem_web"))
        sem_bs = [ec(nc.semaphore(f"sem_bs{i}")) for i in range(4)]
        sem_ld = ec(nc.semaphore("sem_ld"))

        COUT, CPE, CRC, CA, CV = (Ctr() for _ in range(5))
        CR = [Ctr() for _ in range(4)]
        CXS = [Ctr() for _ in range(2)]
        CWB = [Ctr() for _ in range(2)]
        CWS = [Ctr() for _ in range(2)]
        CXR = [Ctr() for _ in range(2)]
        CWEB = Ctr()
        CBS = [Ctr() for _ in range(4)]
        CLD = Ctr()

        def W(eng, sem, n):
            if n > 0:
                E(eng, lambda e, s=sem, n=n: e.wait_ge(s, n))

        def W2(eng, sw):
            sem, n = sw
            W(eng, sem, n)

        def OPV(fn):
            if CV.n:
                E("vector", lambda e, n=CV.n: e.wait_ge(sem_v, n))
            E("vector", lambda e, f=fn: f(e).then_inc(sem_v, 1))
            return CV.add(1)

        def OPA(fn):
            if CA.n:
                E("scalar", lambda e, n=CA.n: e.wait_ge(sem_a, n))
            E("scalar", lambda e, f=fn: f(e).then_inc(sem_a, 1))
            return CA.add(1)

        RING = ec(nc.sbuf_tensor("RING", [128, 4 * 2048], BF16))
        LP = ec(nc.sbuf_tensor("LP", [128, SC], F32))
        BB = ec(nc.sbuf_tensor("BB", [128, OC], F32))
        EMC = ec(nc.sbuf_tensor("EMC", [128, 8], F32))
        ONESB = ec(nc.sbuf_tensor("ONESB", [128, 1], BF16))
        ONES1 = ec(nc.sbuf_tensor("ONES1", [1, 128], F32))
        CST = ec(nc.sbuf_tensor("CST", [1, 8], F32))
        CSTC = ec(nc.sbuf_tensor("CSTC", [128, 2], F32))
        PB = ec(nc.psum_tensor("PB", [128, 8 * NT], F32))

        def bank(j):
            return PB[:, j * NT:(j + 1) * NT]

        ring_idx = Ctr()
        ring_cons_at = {}
        ring_store_at = {}
        blob_cursor = Ctr()

        def ring_load(src_ap, policy="setup"):
            i = ring_idx.n
            ring_idx.add()
            sl = i % 4
            slot = RING[:, sl * 2048:(sl + 1) * 2048]
            if i >= 4:
                W("gpsimd", sem_rc, ring_cons_at[i - 4])
                if (i - 4) in ring_store_at:
                    W2("gpsimd", ring_store_at[i - 4])
            if policy == "setup":
                mode = "direct"
            else:
                g = blob_cursor.n
                blob_cursor.add()
                if policy == "t0":
                    mode = "save"
                else:
                    mode = "blob"
            ncols = int(np.prod(src_ap.shape[1:]))
            if mode == "blob":
                src_ap = blob_view(g, ncols)
            dst = slot[:, 0:ncols]
            if len(src_ap.shape) == 3:
                dst = dst.rearrange("p (a b) -> p a b", a=src_ap.shape[1])
            tgt = CR[sl].add(16)
            E("gpsimd", lambda e, s=src_ap, d=dst, m=sem_r[sl]:
              e.dma_start(out=d, in_=s).then_inc(m, 16))
            if mode == "save":
                W("sync", sem_r[sl], tgt)
                stgt = CBS[sl].add(16)
                E("sync", lambda e, s=slot[:, 0:ncols], d=blob_view(g, ncols), m=sem_bs[sl]:
                  e.dma_start(out=d, in_=s).then_inc(m, 16))
                ring_store_at[i] = (sem_bs[sl], stgt)
            return slot, i, (sem_r[sl], tgt)

        def _chain(inst, incs):
            for s_, v_ in incs:
                inst = inst.then_inc(s_, v_)
            return inst

        def TMM(fn, incs=()):
            if incs:
                E("tensor", lambda e, f=fn, iv=tuple(incs): _chain(f(e), iv))
            else:
                E("tensor", fn)

        def ring_mark(i):
            ring_cons_at[i] = CRC.add(1)
            return (sem_rc, 1)

        with contextlib.ExitStack() as ses:
            sc_ = ses.enter_context
            CO0 = sc_(nc.sbuf_tensor("CO0", [128, 3 * 32 * 32], F32))
            CO1 = sc_(nc.sbuf_tensor("CO1", [128, 3 * 32 * 8], F32))
            MK0 = sc_(nc.sbuf_tensor("MK0", [128, 3 * 32 * 32], U8))
            MK1 = sc_(nc.sbuf_tensor("MK1", [128, 3 * 32 * 8], U8))
            MKF = sc_(nc.sbuf_tensor("MKF", [128, 3 * 32 * 32], F32))
            W0S = sc_(nc.sbuf_tensor("W0S", [128, 32 * 32], F32))
            W1S = sc_(nc.sbuf_tensor("W1S", [128, 32 * 8], F32))
            W01B = sc_(nc.sbuf_tensor("W01B", [128, 32 * 40], BF16))
            ENCF = sc_(nc.sbuf_tensor("ENCF", [40, H], F32))
            ENCB = sc_(nc.sbuf_tensor("ENCB", [40, H], BF16))
            VCB = sc_(nc.sbuf_tensor("VCB", [40, H], BF16))
            WBB = sc_(nc.sbuf_tensor("WBB", [128, 2 * H], F32))
            WEO = sc_(nc.sbuf_tensor("WEO", [128, 2 * H], BF16))
            DRW = sc_(nc.sbuf_tensor("DRW", [1, 16], F32))

            for src, dst in ((d_co0, CO0[:, :]), (d_co1, CO1[:, :]),
                             (d_mk0, MK0[:, :]), (d_mk1, MK1[:, :]),
                             (d_enc, ENCF[:, :]), (d_lp, LP[:, :]),
                             (d_bb, BB[:, :]), (d_disc, DRW[:, 0:G])):
                CLD.add(16)
                E("sync", lambda e, s=src, d=dst: e.dma_start(out=d, in_=s).then_inc(sem_ld, 16))
            c_small = CLD.n

            OPV(lambda e: e.memset(ONESB[:, :], 1.0))
            OPV(lambda e: e.memset(ONES1[:, :], 1.0))
            OPV(lambda e: e.memset(CST[:, 0:1], EPS))
            OPV(lambda e: e.memset(CST[:, 1:2], -0.5))
            OPV(lambda e: e.memset(CST[:, 2:3], -1.0))
            OPV(lambda e: e.memset(CST[:, 3:4], 0.5))
            c_consts = OPV(lambda e: e.memset(CSTC[:, 0:1], 0.5))

            W("vector", sem_ld, c_small)
            OPV(lambda e: e.tensor_copy(MKF[:, :], MK0[:, :]))
            for e_i in range(3):
                sl = slice(e_i * 1024, (e_i + 1) * 1024)
                if e_i == 0:
                    OPV(lambda e, s=sl: e.tensor_mul(W0S[:, :], CO0[:, s], MKF[:, s]))
                else:
                    OPV(lambda e, s=sl: e.tensor_mul(MKF[:, s], CO0[:, s], MKF[:, s]))
                    OPV(lambda e, s=sl: e.tensor_add(W0S[:, :], W0S[:, :], MKF[:, s]))
            OPV(lambda e: e.tensor_copy(MKF[:, 0:3 * 256], MK1[:, :]))
            for e_i in range(3):
                sl = slice(e_i * 256, (e_i + 1) * 256)
                if e_i == 0:
                    OPV(lambda e, s=sl: e.tensor_mul(W1S[:, :], CO1[:, s], MKF[:, s]))
                else:
                    OPV(lambda e, s=sl: e.tensor_mul(MKF[:, s], CO1[:, s], MKF[:, s]))
                    OPV(lambda e, s=sl: e.tensor_add(W1S[:, :], W1S[:, :], MKF[:, s]))
            OPV(lambda e: e.tensor_copy(
                W01B.rearrange("p (c w) -> p c w", w=40)[:, :, 0:32],
                W0S.rearrange("p (c r) -> p c r", r=32)))
            OPV(lambda e: e.tensor_copy(
                W01B.rearrange("p (c w) -> p c w", w=40)[:, :, 32:40],
                W1S.rearrange("p (c r) -> p c r", r=8)))
            c_w01 = OPV(lambda e: e.tensor_copy(ENCB[:, :], ENCF[:, :]))

            W("scalar", sem_ld, c_small)
            W("scalar", sem_v, c_consts)
            c_emc_row = OPA(lambda e: e.activation(DRW[:, 8:8 + G], DRW[:, 0:G], AF.Exp,
                                                   bias=0.0, scale=CST[0:1, 2:3]))
            W("tensor", sem_a, c_emc_row)
            E("tensor", lambda e: e.matmul(PB[:, 7 * NT:7 * NT + G], ONES1[:, :], DRW[:, 8:8 + G],
                                           start=True, stop=True).then_inc(sem_pe, 1))
            c_emc_mm = CPE.add(1)
            W("vector", sem_pe, c_emc_mm)
            c_emc = OPV(lambda e: e.tensor_copy(EMC[:, 0:G], PB[:, 7 * NT:7 * NT + G]))

            c_vcb_prev = 0
            for hh in range(8):
                t0 = hh * NT
                for og in range(8):
                    src = v_qt[:, og * 4:(og + 1) * 4, t0:t0 + NT]
                    slot, fi, wcnt = ring_load(src)
                    W2("tensor", wcnt)
                    if hh == 0 and og == 0:
                        W("tensor", sem_v, c_w01)
                    if og == 0 and hh > 0:
                        W("tensor", sem_a, c_vcb_prev)
                    for oc in range(4):
                        c = og * 4 + oc
                        lhsT = W01B[:, c * 40:(c + 1) * 40]
                        rhs = slot[:, oc * NT:(oc + 1) * NT]
                        incs = []
                        if oc == 3:
                            incs.append(ring_mark(fi))
                        TMM(lambda e, l=lhsT, r=rhs, st=(c == 0), sp=(c == 31):
                            e.matmul(PB[0:40, 6 * NT:7 * NT], l, r, start=st, stop=sp), incs)
                c_vc_mm = (sem_rc, CRC.n)
                W2("scalar", c_vc_mm)
                c_vcb_prev = OPA(lambda e, t0=t0: e.copy(VCB[0:40, t0:t0 + NT],
                                                         PB[0:40, 6 * NT:7 * NT]))

            c_wb_load = {}
            c_wb_stt = {}
            c_stt_at = {}
            c_ws_store = {}

            def emit_wb_load(dc):
                par = dc % 2
                buf = WBB[:, par * H:par * H + H]
                if dc >= 2:
                    W("sync", sem_v, c_wb_stt[dc - 2])
                tgt = CWB[par].add(16)
                E("sync", lambda e, d=buf, s=d_wbt[dc * 128:(dc + 1) * 128, :], m=sem_wb[par]:
                  e.dma_start(out=d, in_=s).then_inc(m, 16))
                c_wb_load[dc] = (sem_wb[par], tgt)

            emit_wb_load(0)
            emit_wb_load(1)
            for dc in range(KC):
                par = dc % 2
                buf = WBB[:, par * H:par * H + H]
                obuf = WEO[:, par * H:par * H + H]
                for hh in range(8):
                    idx = dc * 8 + hh
                    pb_i = 4 + (idx % 2)
                    lhsT = ENCB[:, dc * 128:(dc + 1) * 128]
                    rhs = VCB[:, hh * NT:(hh + 1) * NT]
                    if idx >= 2:
                        W("tensor", sem_v, c_stt_at[idx - 2])
                    if idx == 0:
                        W("tensor", sem_a, c_vcb_prev)
                    E("tensor", lambda e, l=lhsT, r=rhs, b=pb_i:
                      e.matmul(bank(b), l, r, start=True, stop=True).then_inc(sem_pe, 1))
                    c_mm = CPE.add(1)
                    W("vector", sem_pe, c_mm)
                    if hh == 0:
                        W2("vector", c_wb_load[dc])
                        if dc >= 2:
                            W2("vector", c_ws_store[dc - 2])
                    c_stt_at[idx] = OPV(lambda e, o=obuf[:, hh * NT:(hh + 1) * NT], b=pb_i,
                                        w=buf[:, hh * NT:(hh + 1) * NT]:
                                        e.scalar_tensor_tensor(o, bank(b), CSTC[:, 0:1], w,
                                                               ALU.mult, ALU.add))
                c_wb_stt[dc] = c_stt_at[dc * 8 + 7]
                W("sync", sem_v, c_wb_stt[dc])
                tgt = CWS[par].add(16)
                E("sync", lambda e, o=d_weff[dc * 128:(dc + 1) * 128, :], s=obuf, m=sem_ws[par]:
                  e.dma_start(out=o, in_=s).then_inc(m, 16))
                c_ws_store[dc] = (sem_ws[par], tgt)
                if dc + 2 < KC:
                    emit_wb_load(dc + 2)

            for eng in ("sync", "gpsimd", "vector", "scalar", "tensor"):
                W(eng, sem_v, CV.n)
                W(eng, sem_a, CA.n)
                W(eng, sem_pe, CPE.n)
                W(eng, sem_rc, CRC.n)
                W(eng, sem_ld, CLD.n)
                W(eng, sem_ws[0], CWS[0].n)
                W(eng, sem_ws[1], CWS[1].n)
                W(eng, sem_wb[0], CWB[0].n)
                W(eng, sem_wb[1], CWB[1].n)
                for i_ in range(4):
                    W(eng, sem_r[i_], CR[i_].n)

        XN = ec(nc.sbuf_tensor("XN", [128, KC * NT], BF16))
        FEAT = ec(nc.sbuf_tensor("FEAT", [128, 56 * NT], BF16))
        WINT = ec(nc.sbuf_tensor("WINT", [128, FC * NT], BF16))
        WEB = ec(nc.sbuf_tensor("WEB", [128, KC * 128], BF16))
        XRES = ec(nc.sbuf_tensor("XRES", [128, NT], F32))
        OUTB = ec(nc.sbuf_tensor("OUTB", [128, NT], F32))
        SCRA = ec(nc.sbuf_tensor("SCRA", [128, NT], F32))
        SCRB = ec(nc.sbuf_tensor("SCRB", [128, NT], F32))
        SGF = ec(nc.sbuf_tensor("SGF", [128, 2 * NT], F32))
        ROWS = ec(nc.sbuf_tensor("ROWS", [1, 2 * NT], F32))

        DELTA = WINT[:, 0:8 * NT * 2].bitcast(F32)
        PW = WINT[:, 8 * NT * 2:16 * NT * 2].bitcast(F32)
        XSTAT = FEAT[:, 40 * NT:56 * NT].bitcast(F32)
        RBB = WEB[:, 0:2 * NT].bitcast(F32)
        MRB = WEB[:, 2 * NT:4 * NT].bitcast(F32)
        XBF = FEAT[:, 32 * NT:34 * NT]
        X2BF = FEAT[:, 34 * NT:36 * NT]

        def feat(kc):
            return FEAT[:, kc * NT:(kc + 1) * NT]

        def wint(kc):
            return WINT[:, kc * NT:(kc + 1) * NT]

        def xnc(k):
            return XN[:, k * NT:(k + 1) * NT]

        def delta_j(j):
            return DELTA[:, j * NT:(j + 1) * NT]

        def pw_j(j):
            return PW[:, j * NT:(j + 1) * NT]

        xstat_cons = {}
        c_mmst = {}
        c_epi_sg = {}
        c_epi_g = {}
        c_epi_out = {}
        c_out_dma = {}
        c_web_done = 0
        c_ffn_done_prev = 0
        c_xn_done = 0
        c_bcast = 0

        RA = ROWS[0:1, 0:NT]
        RB = ROWS[0:1, NT:2 * NT]
        RC = SCRB[0:1, 0:NT]

        for t in range(N_TILES):
            tc = slice(t * NT, (t + 1) * NT)
            c_pe_prev_tile = CPE.n
            c_rc_prev_tile = CRC.n
            rmode = ("t0", "tn", "tn", "tn")[t]
            blob_cursor.n = 0
            if t >= 1:
                for i_ in range(4):
                    W("gpsimd", sem_bs[i_], CBS[i_].n)

            c_xload = {}
            c_xn_at = {}

            def emit_ln_load(key, t=t, tc=tc):
                gk = t * 16 + key
                slot = key % 2
                if gk >= 2 and (gk - 2) in xstat_cons:
                    for sem_o, cnt_o in xstat_cons[gk - 2]:
                        W("sync", sem_o, cnt_o)
                if key == 0 and t > 0:
                    W("sync", sem_pe, c_pe_prev_tile)
                    W("sync", sem_rc, c_rc_prev_tile)
                g4 = key % 8
                src = v_x[:, g4 * 4:(g4 + 1) * 4, tc]
                dst = XSTAT[:, slot * 4 * NT:(slot + 1) * 4 * NT].rearrange(
                    "p (a b) -> p a b", a=4)
                tgt = CXS[slot].add(16)
                E("sync", lambda e, s=src, d=dst, m=sem_xs[slot]:
                  e.dma_start(out=d, in_=s).then_inc(m, 16))
                c_xload[key] = (sem_xs[slot], tgt)

            def emit_stats_group(g4, t=t):
                slot = g4 % 2
                for kk in range(4):
                    k = g4 * 4 + kk
                    xv = XSTAT[:, (slot * 4 + kk) * NT:(slot * 4 + kk + 1) * NT]
                    xb = XBF[:, (k % 2) * NT:(k % 2 + 1) * NT]
                    x2 = X2BF[:, (k % 2) * NT:(k % 2 + 1) * NT]
                    W2("vector", c_xload[g4])
                    W2("scalar", c_xload[g4])
                    if k >= 2 or t > 0:
                        kk2 = (k - 2) % KC
                        W("vector", sem_pe, c_mmst[kk2])
                        W("scalar", sem_pe, c_mmst[kk2])
                    if k == 0 and t > 0:
                        W("vector", sem_pe, c_pe_prev_tile)
                        W("scalar", sem_pe, c_pe_prev_tile)
                        W("vector", sem_rc, c_rc_prev_tile)
                        W("scalar", sem_rc, c_rc_prev_tile)
                    c_cast = OPV(lambda e, a=xb, b=xv: e.tensor_copy(a, b))
                    c_sq = OPA(lambda e, a=x2, b=xv: e.activation(a, b, AF.Square))
                    W("tensor", sem_v, c_cast)
                    W("tensor", sem_a, c_sq)
                    if k == 0 and t > 0:
                        W("tensor", sem_v, c_ffn_done_prev)
                    st = (k == 0)
                    sp = (k == KC - 1)
                    E("tensor", lambda e, a=xb, st=st, sp=sp:
                      e.matmul(PB[0:1, 7 * NT:8 * NT], ONESB[:, :], a, start=st, stop=sp))
                    E("tensor", lambda e, a=x2, st=st, sp=sp:
                      e.matmul(PB[0:1, 4 * NT:5 * NT], ONESB[:, :], a, start=st, stop=sp).then_inc(sem_pe, 1))
                    c_mmst[k] = CPE.add(1)
                xstat_cons[t * 16 + g4] = ((sem_a, CA.n), (sem_v, CV.n))

            def emit_xn_group(g4, t=t):
                slot = g4 % 2
                for kk in range(4):
                    k = g4 * 4 + kk
                    xv = XSTAT[:, (slot * 4 + kk) * NT:(slot * 4 + kk + 1) * NT]
                    W2("vector", c_xload[8 + g4])
                    if k == 0:
                        W("vector", sem_a, c_rbb)
                    OPV(lambda e, xv=xv: e.tensor_mul(SCRB[:, :], xv, RBB[:, :]))
                    c_xn_at[k] = OPV(lambda e, k=k: e.tensor_sub(xnc(k), SCRB[:, :], MRB[:, :]))
                xstat_cons[t * 16 + 8 + g4] = ((sem_v, CV.n),)

            emit_ln_load(0)
            emit_ln_load(1)
            for g4 in range(8):
                emit_stats_group(g4)
                if g4 + 2 <= 7:
                    emit_ln_load(g4 + 2)
            c_stats_mm = c_mmst[KC - 1]

            W("vector", sem_pe, c_stats_mm)
            OPV(lambda e: e.tensor_scalar_mul(RA, PB[0:1, 7 * NT:8 * NT], 1.0 / H))
            OPV(lambda e: e.tensor_scalar_mul(RB, PB[0:1, 4 * NT:5 * NT], 1.0 / H))
            OPV(lambda e: e.tensor_mul(RC, RA, RA))
            c_var = OPV(lambda e: e.tensor_sub(RB, RB, RC))
            W("scalar", sem_v, c_var)
            c_ln = OPA(lambda e: e.activation(RC, RB, AF.Ln, bias=CST[0:1, 0:1], scale=1.0))
            c_rstd = OPA(lambda e: e.activation(RB, RC, AF.Exp, bias=0.0,
                                                scale=CST[0:1, 1:2]))
            W("vector", sem_a, c_rstd)
            c_mr = OPV(lambda e: e.tensor_mul(RC, RA, RB))

            W("tensor", sem_v, c_mr)
            if t > 0:
                W("tensor", sem_v, c_ffn_done_prev)
            if t > 0:
                W("tensor", sem_pe, c_pe_prev_tile)
            E("tensor", lambda e: e.matmul(PB[:, 5 * NT:6 * NT], ONES1[:, :], RB, start=True, stop=True))
            E("tensor", lambda e: e.matmul(PB[:, 6 * NT:7 * NT], ONES1[:, :], RC,
                                           start=True, stop=True).then_inc(sem_pe, 1))
            c_bcast = CPE.add(1)
            W("scalar", sem_pe, c_bcast)
            if t > 0:
                W("scalar", sem_pe, c_pe_prev_tile)
            OPA(lambda e: e.copy(RBB[:, :], PB[:, 5 * NT:6 * NT]))
            c_rbb = OPA(lambda e: e.copy(MRB[:, :], PB[:, 6 * NT:7 * NT]))

            emit_ln_load(8)
            emit_ln_load(9)
            for g4 in range(8):
                emit_xn_group(g4)
                if g4 + 2 <= 7:
                    emit_ln_load(8 + g4 + 2)
            c_xn_done = CV.n

            c_p1_evac_u = {}
            c_p1_delta = {}
            for sweep in range(2):
                vsrc = v_up if sweep == 0 else v_gate
                for g2 in range(16):
                    src = vsrc[:, g2 * 2:(g2 + 1) * 2, :]
                    slot, fi, wcnt = ring_load(src, rmode)
                    W2("tensor", wcnt)
                    W("tensor", sem_v, c_xn_at[2 * g2 + 1])
                    if g2 == 0:
                        if sweep == 0:
                            W("tensor", sem_a, c_rbb)
                            if t > 0:
                                W("tensor", sem_v, c_epi_sg[31])
                        if sweep == 1:
                            W("tensor", sem_a, c_p1_evac_u[SC - 1])
                    for kk in range(2):
                        k = g2 * 2 + kk
                        for j in range(SC):
                            lhsT = slot[:, kk * S + j * 128:kk * S + (j + 1) * 128]
                            incs = []
                            if kk == 1 and j == SC - 1:
                                incs.append(ring_mark(fi))
                            TMM(lambda e, l=lhsT, j=j, k=k, st=(k == 0), sp=(k == KC - 1):
                                e.matmul(bank(j), l, xnc(k), start=st, stop=sp), incs)
                c_sweep_done = (sem_rc, CRC.n)
                for j in range(SC):
                    if sweep == 0:
                        W2("scalar", c_sweep_done)
                        c_p1_evac_u[j] = OPA(lambda e, j=j: e.copy(feat(j), bank(j)))
                    else:
                        W2("vector", c_sweep_done)
                        c_p1_delta[j] = OPV(lambda e, j=j: e.tensor_scalar(
                            delta_j(j), bank(j), LP[:, j:j + 1], None, ALU.subtract))

            def emit_fg(j):
                dj = delta_j(j)
                pj = pw_j(j)
                uj = feat(j)
                W("vector", sem_v, c_p1_delta[j])
                W("vector", sem_a, c_p1_evac_u[j])
                for m in range(1, 7):
                    if m == 2:
                        OPV(lambda e, d=dj, p=pj: e.tensor_mul(p, d, d))
                    elif m >= 3:
                        OPV(lambda e, d=dj, p=pj: e.tensor_mul(p, p, d))
                    if m == 5:
                        W("scalar", sem_v, CV.n)
                        c_abs = OPA(lambda e, d=dj: e.activation(SCRA[:, :], d, AF.Abs))
                        W("vector", sem_a, c_abs)
                        OPV(lambda e: e.tensor_scalar(SCRB[:, :], SCRA[:, :], THD, None, ALU.is_le))
                        OPV(lambda e, p=pj: e.tensor_mul(p, p, SCRB[:, :]))
                    fsrc = dj if m == 1 else pj
                    fdst = feat(m * 8 + j)
                    OPV(lambda e, s=fsrc, d=fdst, u=uj, m=m:
                        e.scalar_tensor_tensor(d, s, EMC[:, m - 1:m], u, ALU.mult, ALU.mult))

            c_ffn_evac = {}
            for b in range(NBAND):
                par = b % 2
                bk = [4 * par + q for q in range(4)]
                for g4 in range(8):
                    src = v_ffngu[:, g4 * 4:(g4 + 1) * 4, b * 512:(b + 1) * 512]
                    slot, fi, wcnt = ring_load(src, rmode)
                    W2("tensor", wcnt)
                    if g4 == 0:
                        if b == 0:
                            W("tensor", sem_v, c_p1_delta[3])
                        if b == 1:
                            W("tensor", sem_v, c_p1_delta[7])
                        if b >= 2:
                            W("tensor", sem_v, c_ffn_evac[b - 2])
                    for kk in range(4):
                        k = g4 * 4 + kk
                        for q in range(4):
                            lhsT = slot[:, kk * 512 + q * 128:kk * 512 + (q + 1) * 128]
                            incs = []
                            if kk == 3 and q == 3:
                                incs.append(ring_mark(fi))
                            TMM(lambda e, l=lhsT, bq=bk[q], k=k, st=(k == 0), sp=(k == KC - 1):
                                e.matmul(bank(bq), l, xnc(k), start=st, stop=sp), incs)
                c_band_done = (sem_rc, CRC.n)
                if b < SC:
                    emit_fg(b)
                for q in range(2):
                    fchunk = 2 * b + q
                    sg = SGF[:, q * NT:(q + 1) * NT]
                    W2("scalar", c_band_done)
                    if b >= 1:
                        W("scalar", sem_v, c_ffn_evac[b - 1])
                    c_silu = OPA(lambda e, s=sg, bq=bk[q]:
                                 e.activation(s, bank(bq), AF.Silu))
                    W("vector", sem_a, c_silu)
                    c_ffn_evac[b] = OPV(lambda e, s=sg, f=fchunk, bq=bk[2 + q]:
                                        e.tensor_mul(wint(f), s, bank(bq)))
            c_ffn_done = c_ffn_evac[NBAND - 1]
            c_ffn_done_prev = c_ffn_done

            ngr = (EPI_KT + 15) // 16
            for o in range(OC):
                par = o % 2
                bs = par
                bg = 2 + par
                gslots = []
                for gi in range(min(4, ngr)):
                    k0 = gi * 16
                    kn = min(16, EPI_KT - k0)
                    gslots.append(ring_load(v_epiw[:, k0:k0 + kn, o * 128:(o + 1) * 128], rmode) + (kn,))
                if t > 0 or o >= 1:
                    W("sync", sem_pe, c_web_done)
                if o == 0:
                    W("sync", sem_v, c_xn_done)
                tgt_weff = CWEB.add(16)
                E("sync", lambda e, s=v_weff[:, :, o * 128:(o + 1) * 128],
                  d=WEB.rearrange("p (k c) -> p k c", c=128):
                  e.dma_start(out=d, in_=s).then_inc(sem_web, 16))
                if t > 0 or o >= 1:
                    W("sync", sem_v, c_epi_out[(o - 1) % OC])
                tgt_xres = CXR[0].add(16)
                E("sync", lambda e, s=d_x[o * 128:(o + 1) * 128, tc],
                  d=XRES[:, :], m=sem_xr[0]:
                  e.dma_start(out=d, in_=s).then_inc(m, 16))

                if o == 0:
                    W("tensor", sem_v, c_ffn_done)
                if t > 0 or o >= 2:
                    W("tensor", sem_v, c_epi_sg[(o - 2) % OC])
                kt = 0
                for gi in range(ngr):
                    slot, fi, wcnt, kn = gslots[gi]
                    W2("tensor", wcnt)
                    for kk in range(kn):
                        lhsT = slot[:, kk * 128:(kk + 1) * 128]
                        rhs = feat(kt) if kt < 56 else wint(kt - 56)
                        incs = []
                        if kk == kn - 1:
                            incs.append(ring_mark(fi))
                        TMM(lambda e, l=lhsT, r=rhs, st=(kt == 0), sp=(kt == EPI_KT - 1), b=bs:
                            e.matmul(bank(b), l, r, start=st, stop=sp), incs)
                        kt += 1
                    if gi + 4 < ngr:
                        k0 = (gi + 4) * 16
                        kn2 = min(16, EPI_KT - k0)
                        gslots.append(ring_load(v_epiw[:, k0:k0 + kn2, o * 128:(o + 1) * 128], rmode) + (kn2,))
                c_s_done = (sem_rc, CRC.n)

                W("tensor", sem_web, tgt_weff)
                if t > 0 or o >= 2:
                    W("tensor", sem_a, c_epi_g[(o - 2) % OC])
                for k in range(KC):
                    lhsT = WEB[:, k * 128:(k + 1) * 128]
                    incs = []
                    if k == KC - 1:
                        incs.append((sem_pe, 1))
                    TMM(lambda e, l=lhsT, k=k, st=(k == 0), sp=(k == KC - 1), b=bg:
                        e.matmul(bank(b), l, xnc(k), start=st, stop=sp), incs)
                c_g_done = CPE.add(1)
                c_web_done = c_g_done

                W("scalar", sem_pe, c_g_done)
                if t > 0 or o >= 1:
                    W("scalar", sem_v, c_epi_sg[(o - 1) % OC])
                c_epi_g[o] = OPA(lambda e, o=o, b=bg:
                                 e.activation(SCRA[:, :], bank(b), AF.Silu, bias=BB[:, o:o + 1],
                                              scale=1.0))
                sgbuf = SGF[:, par * NT:(par + 1) * NT]
                W2("vector", c_s_done)
                W("vector", sem_a, c_epi_g[o])
                c_epi_sg[o] = OPV(lambda e, s=sgbuf, b=bs:
                                  e.tensor_mul(s, bank(b), SCRA[:, :]))
                W("vector", sem_xr[0], tgt_xres)
                if t > 0 or o >= 1:
                    W("vector", sem_out, c_out_dma[(o - 1) % OC])
                c_epi_out[o] = OPV(lambda e, s=sgbuf, x=XRES[:, :]:
                                   e.tensor_add(OUTB[:, :], s, x))
                W("sync", sem_v, c_epi_out[o])
                cnt_out = COUT.add(16)
                E("sync", lambda e, d=d_out[o * 128:(o + 1) * 128, tc]:
                  e.dma_start(out=d, in_=OUTB[:, :]).then_inc(sem_out, 16))
                c_out_dma[o] = cnt_out

        W("sync", sem_out, COUT.n)

        with nc.Block() as block:
            @block.sync
            def _(eng):
                for fn in prog["sync"]:
                    fn(eng)

            @block.gpsimd
            def _(eng):
                for fn in prog["gpsimd"]:
                    fn(eng)

            @block.vector
            def _(eng):
                for fn in prog["vector"]:
                    fn(eng)

            @block.scalar
            def _(eng):
                for fn in prog["scalar"]:
                    fn(eng)

            @block.tensor
            def _(eng):
                for fn in prog["tensor"]:
                    fn(eng)

    return nc


def _prep_inputs(inputs):
    x = np.ascontiguousarray(np.asarray(inputs["L0_attn_hidden_states"])).reshape(16384, H)
    x_t = np.ascontiguousarray(x.T)

    up_t = np.ascontiguousarray(np.asarray(inputs["L0_attn_up_w"]).T)
    gate_t = np.ascontiguousarray(np.asarray(inputs["L0_attn_gate_w"]).T)
    fg_t = np.asarray(inputs["L0_attn_ffn_gate_w"]).T
    fu_t = np.asarray(inputs["L0_attn_ffn_up_w"]).T
    gb = fg_t.reshape(H, NBAND, 256)
    ub = fu_t.reshape(H, NBAND, 256)
    ffngu = np.ascontiguousarray(np.concatenate([gb, ub], axis=2).reshape(H, 2 * F))

    lao = np.asarray(inputs["L0_attn_local_approx_output"])
    fuse = np.asarray(inputs["L0_attn_fuse_weight"])
    wcat_t = np.concatenate([lao.T[None], fuse.transpose(2, 1, 0)], axis=0).reshape(7 * S, H)
    down_t = np.asarray(inputs["L0_attn_ffn_down_w"]).T
    epiw = np.ascontiguousarray(np.concatenate([wcat_t, down_t], axis=0))

    wbt = np.ascontiguousarray(np.asarray(inputs["L0_ffn_W_base"]).T)

    n = H
    tt = np.arange(n, dtype=np.float64)
    qt = np.zeros((n, n), dtype=np.float64)
    qt[0, :] = 1.0 / np.sqrt(n)
    maxf = (n - 1) // 2
    f_idx = np.arange(1, maxf + 1, dtype=np.float64)
    ang = 2.0 * np.pi * np.outer(f_idx, tt) / n
    qt[1:1 + 2 * maxf:2, :] = np.sqrt(2.0 / n) * np.cos(ang)
    qt[2:2 + 2 * maxf:2, :] = np.sqrt(2.0 / n) * np.sin(ang)
    qt[n - 1, :] = ((-1.0) ** tt) / np.sqrt(n)
    qt = np.ascontiguousarray(qt.astype(ml_dtypes.bfloat16))

    enc = np.ascontiguousarray(np.concatenate(
        [np.asarray(inputs["L0_ffn_enc0"]), np.asarray(inputs["L0_ffn_enc1"])], axis=0))

    co0 = np.ascontiguousarray(
        np.asarray(inputs["L0_ffn_coefs0"]).reshape(3, 32, 128, 32).transpose(2, 0, 1, 3).reshape(128, -1))
    co1 = np.ascontiguousarray(
        np.asarray(inputs["L0_ffn_coefs1"]).reshape(3, 32, 128, 8).transpose(2, 0, 1, 3).reshape(128, -1))
    mk0 = np.ascontiguousarray(
        np.asarray(inputs["L0_ffn_mask0"]).astype(np.uint8).reshape(3, 32, 128, 32).transpose(2, 0, 1, 3).reshape(128, -1))
    mk1 = np.ascontiguousarray(
        np.asarray(inputs["L0_ffn_mask1"]).astype(np.uint8).reshape(3, 32, 128, 8).transpose(2, 0, 1, 3).reshape(128, -1))

    lp = np.ascontiguousarray(np.asarray(inputs["L0_attn_local_point"]).reshape(SC, 128).T)
    bb = np.ascontiguousarray(np.asarray(inputs["L0_ffn_b_base"]).reshape(OC, 128).T)
    disc = np.ascontiguousarray(np.asarray(inputs["L0_attn_discount_factor"]).reshape(1, G))

    shared = dict(up_t=up_t, gate_t=gate_t, ffngu=ffngu, epiw=epiw, wbt=wbt,
                  qt=qt, enc=enc, co0=co0, co1=co1, mk0=mk0, mk1=mk1,
                  lp=lp, bb=bb, disc=disc)
    in_maps = []
    for c in range(N_CORES):
        m = dict(shared)
        m["x_t"] = np.ascontiguousarray(x_t[:, c * TOK_PER_CORE:(c + 1) * TOK_PER_CORE])
        in_maps.append(m)
    return in_maps


def kernel(**inputs):
    in_maps = _prep_inputs(inputs)
    nc = build_program()
    res = run_bass_kernel_spmd(nc, in_maps, core_ids=list(range(N_CORES)))
    out_t = np.concatenate([r["out_t"] for r in res.results], axis=1)
    return np.ascontiguousarray(out_t.T).reshape(4, 4096, 4096)
